# version 07
# baseline (speedup 1.0000x reference)
import numpy as np
import ml_dtypes
import concourse.bass as bass
import concourse.mybir as mybir
from concourse.bass_utils import run_bass_kernel_spmd

F32 = mybir.dt.float32
BF16 = mybir.dt.bfloat16
AF = mybir.ActivationFunctionType
ALU = mybir.AluOpType
AX = mybir.AxisListType

NCORES = 8
S = 2048
BPC = 2
T = S * BPC
D = 1024
NH = 16
HD = 64
DROPE = 16
FC = D // 128
LAT = 256

BF = ml_dtypes.bfloat16



class Sem:
    def __init__(self, handle):
        self.h = handle
        self.n = 0


class Prog:
    ENGINES = ("sync", "gpsimd", "scalar", "vector", "tensor")

    CHAINED = ("vector", "scalar", "gpsimd")

    def __init__(self, nc, ctx):
        self.nc = nc
        self.ctx = ctx
        self.ops = {e: [] for e in self.ENGINES}
        self._semcount = 0
        self.chain = {e: self.sem(f"chain_{e}") for e in self.CHAINED}
        self.last_evt = {e: None for e in self.CHAINED}

    def sem(self, name=None):
        self._semcount += 1
        h = self.ctx.enter_context(self.nc.semaphore(name or f"s{self._semcount}"))
        return Sem(h)

    def op(self, engine, make, waits=(), inc=None, inc_amt=1, _chain=True):
        waits = list(waits)
        if _chain and engine in self.CHAINED:
            if self.last_evt[engine] is not None:
                waits.append(self.last_evt[engine])
            if inc is None:
                inc = self.chain[engine]
                inc_amt = 1
        wl = [(s.h, v) for (s, v) in waits if v > 0]
        il = None
        if inc is not None:
            inc.n += inc_amt
            il = (inc.h, inc_amt)
            if _chain and engine in self.CHAINED:
                self.last_evt[engine] = (inc, inc.n)
        def fn(eng):
            for h, v in wl:
                eng.wait_ge(h, v)
            inst = make(eng)
            if il is not None:
                if inst is None:
                    inst = eng.nop()
                inst.then_inc(*il)
        self.ops[engine].append(fn)

    def dma(self, queue, out, in_, waits=(), inc=None, **kw):
        waits = list(waits)
        if inc is not None and inc.n > 0:
            waits.append((inc, inc.n))
        self.op(queue, lambda e: e.dma_start(out=out, in_=in_, **kw),
                waits=waits, inc=inc, inc_amt=16, _chain=False)

    def dmaT(self, queue, out, in_, waits=(), inc=None):
        waits = list(waits)
        if inc is not None and inc.n > 0:
            waits.append((inc, inc.n))
        self.op(queue, lambda e: e.dma_start_transpose(out=out, in_=in_),
                waits=waits, inc=inc, inc_amt=16, _chain=False)

    def mm(self, out, lhsT, rhs, start, stop, waits=(), inc=None):
        self.op("tensor",
                lambda e: e.matmul(out, lhsT, rhs, start=start, stop=stop),
                waits=waits, inc=inc)

    def finalize(self):
        block = self.ctx.enter_context(self.nc.Block())
        for ename in self.ENGINES:
            ops = self.ops[ename]
            if not ops:
                continue
            section = getattr(block, ename)
            def make_body(ops_):
                def body(eng):
                    for fn in ops_:
                        fn(eng)
                return body
            section(make_body(list(ops)))


def after(*sems):
    return [(s, s.n) for s in sems]


ARENA_WORDS = 53200


class Arena:

    def __init__(self, big):
        self.big = big
        self.off = 0

    def reset(self):
        self.off = 0

    def __call__(self, rows, cols, dtype):
        if dtype == F32:
            w = cols
        elif dtype == BF16:
            w = (cols + 1) // 2
        else:
            raise ValueError(dtype)
        off = self.off
        self.off = off + w
        assert self.off <= ARENA_WORDS, f"arena overflow: {self.off} words"
        ap = self.big[0:rows, off:off + w]
        if dtype == BF16:
            ap = ap.bitcast(BF16)[:, 0:cols]
        return ap


class SemPool:

    def __init__(self, p):
        self.E = [p.sem(f"E{i}") for i in range(9)]
        self.H = [p.sem(f"H{i}") for i in range(6)]
        self.G = [p.sem(f"G{i}") for i in range(18)]


def barrier(p, pool, skip=()):
    waits = [(s, s.n) for s in (pool.E + pool.H + pool.G +
                                list(p.chain.values()))
             if s.n > 0 and s not in skip]
    for e in Prog.ENGINES:
        p.op(e, lambda _: None, waits=waits, _chain=False)



def host_consts_a():
    inv = (1.0 / 10000.0 ** (np.arange(0, HD, 2, dtype=np.float64) / HD))
    CT = np.ones((128, FC), np.float32)
    ST = np.zeros((128, FC), np.float32)
    for oc in range(FC):
        for blk in range(2):
            h = 2 * oc + blk
            for r in range(16):
                ang = h * inv[r // 2]
                CT[blk * 64 + r, oc] = np.cos(ang)
                ST[blk * 64 + r, oc] = np.sin(ang) * (-1.0 if r % 2 == 0 else 1.0)
    return {
        "ct": CT, "st": ST,
        "ident_bf": np.eye(128, dtype=BF),
    }



def phase_a1(p, A, psA, psB, sm, io):
    if True:
        x0 = io["x0"]; wq = io["wq"]; wk_a = io["wk_a"]; wk_b = io["wk_b"]
        wv_a = io["wv_a"]; wv_b = io["wv_b"]
        ct_in = io["ct"]; st_in = io["st"]; ident = io["ident_bf"]
        qTr = io["qTr"]; kTr = io["kTr"]; vext = io["vext"]; x0bf = io["x0bf"]
        A.reset()
        wqT = A(128, FC * D, BF16)
        wkaT = A(128, FC * LAT, BF16)
        wkbT = A(128, 2 * D, BF16)
        wvaT = A(128, FC * LAT, BF16)
        wvbT = A(128, 2 * D, BF16)
        wnat = A(128, FC * D, BF16)
        id_sb = A(128, 128, BF16)
        ct_sb = A(128, FC, F32)
        st_sb = A(128, FC, F32)
        x0T = A(128, FC * S, BF16)
        qT = A(128, FC * S, BF16)
        kT = A(128, FC * S, BF16)
        klatT = A(128, 2 * S, BF16)
        vlatT = A(128, 2 * S, BF16)
        rs1 = A(128, S, F32)
        rs2 = A(128, S, F32)
        vstg = A(128, 2 * NH * 65, BF16)
        PS = [psA, psB]
        ld = sm.G[0]; ldC = sm.H[0]; wtr = sm.E[7]; wev = sm.E[8]
        xb = sm.G[1]; xt = sm.H[1]; mm = sm.E[0]; ve = sm.E[1]
        vd = sm.H[2]; qs = sm.H[3]

        p.dma("sync", id_sb[:], ident, inc=ldC)
        p.dma("sync", ct_sb[:], ct_in, inc=ldC)
        p.dma("sync", st_sb[:], st_in, inc=ldC)
        for i in range(4):
            sl = slice(i * (T // 4), (i + 1) * (T // 4))
            p.dma("gpsimd", x0bf[sl, :], x0[sl, :], inc=xb)

        def prep_weight(wdram, R, C, dest):
            RC, CC = R // 128, C // 128
            nat_view = wnat[:, 0:RC * C].rearrange("p (r c) -> p r c", r=RC)
            src = wdram.rearrange("(r p) c -> p r c", p=128)
            p.dma("gpsimd", nat_view, src, waits=after(wev), inc=ld)
            ldv = (ld, ld.n)
            ldcv = (ldC, ldC.n)
            for rc in range(RC):
                pst = PS[rc % 2][:, 0:512].bitcast(BF16)
                for cc in range(CC):
                    p.op("tensor",
                         (lambda rc_, cc_, pst_: lambda e: e.transpose(
                             pst_[:, cc_ * 128:(cc_ + 1) * 128],
                             wnat[:, rc_ * C + cc_ * 128: rc_ * C + (cc_ + 1) * 128],
                             id_sb[:]))(rc, cc, pst),
                         waits=[ldv, ldcv, (wev, wev.n)],
                         inc=(wtr if cc == CC - 1 else None))
                dv = dest[:, 0:CC * R].rearrange(
                    "p (cc r) -> p cc r", cc=CC)[:, :, rc * 128:(rc + 1) * 128]
                p.op("vector",
                     (lambda dv_, pst_, CC_: lambda e: e.tensor_copy(
                         out=dv_, in_=pst_[:, 0:CC_ * 128]))(dv, pst, CC),
                     waits=after(wtr), inc=wev)

        prep_weight(wq, D, D, wqT)
        prep_weight(wk_a, LAT, D, wkaT)
        prep_weight(wk_b, D, LAT, wkbT)
        prep_weight(wv_a, LAT, D, wvaT)
        prep_weight(wv_b, D, LAT, wvbT)
        wev_all = after(wev)
        ld_all = after(ld)
        ldc_all = after(ldC)

        shuf_mask = []
        for j in range(16):
            shuf_mask += [2 * j + 1, 2 * j]

        x0t_free = []
        store_guard = []

        for b in range(BPC):
            tb = b * S
            for fc in range(FC):
                p.dmaT("scalar", x0T[:, fc * S:(fc + 1) * S],
                       x0bf[tb:tb + S, fc * 128:(fc + 1) * 128],
                       waits=after(xb) + x0t_free, inc=xt)
            xt_all = after(xt)

            for lc in range(2):
                for fc in range(FC):
                    for n in range(4):
                        p.mm(psA[:, n * 512:(n + 1) * 512],
                             wkaT[:, fc * LAT + lc * 128: fc * LAT + (lc + 1) * 128],
                             x0T[:, fc * S + n * 512: fc * S + (n + 1) * 512],
                             start=(fc == 0), stop=(fc == FC - 1),
                             waits=(wev_all + ld_all + ldc_all + xt_all + after(ve)
                                    if fc == 0 and n == 0 else []),
                             inc=(mm if fc == FC - 1 and n == 3 else None))
                p.op("vector",
                     (lambda lc_: lambda e: e.tensor_copy(
                         out=klatT[:, lc_ * S:(lc_ + 1) * S], in_=psA[:, 0:2048]))(lc),
                     waits=after(mm), inc=ve)
                for fc in range(FC):
                    for n in range(4):
                        p.mm(psB[:, n * 512:(n + 1) * 512],
                             wvaT[:, fc * LAT + lc * 128: fc * LAT + (lc + 1) * 128],
                             x0T[:, fc * S + n * 512: fc * S + (n + 1) * 512],
                             start=(fc == 0), stop=(fc == FC - 1),
                             waits=(wev_all + xt_all + after(ve) if fc == 0 and n == 0 else []),
                             inc=(mm if fc == FC - 1 and n == 3 else None))
                p.op("vector",
                     (lambda lc_: lambda e: e.tensor_copy(
                         out=vlatT[:, lc_ * S:(lc_ + 1) * S], in_=psB[:, 0:2048]))(lc),
                     waits=after(mm), inc=ve)

            def rope_chunk(dest, oc, ps):
                p.op("vector",
                     (lambda ps_, oc_: lambda e: e.tensor_scalar(
                         out=rs1[:], in0=ps_[:, 0:2048],
                         scalar1=ct_sb[:, oc_:oc_ + 1], scalar2=None,
                         op0=ALU.mult))(ps, oc),
                     waits=after(mm))
                p.op("vector",
                     (lambda ps_: lambda e: e.stream_shuffle(
                         out=rs2[:], in_=ps_[:, 0:2048], mask=shuf_mask))(ps))
                p.op("vector",
                     (lambda oc_: lambda e: e.tensor_scalar(
                         out=rs2[:], in0=rs2[:],
                         scalar1=st_sb[:, oc_:oc_ + 1], scalar2=None,
                         op0=ALU.mult))(oc))
                p.op("vector",
                     (lambda dest_, oc_: lambda e: e.tensor_tensor(
                         out=dest_[:, oc_ * S:(oc_ + 1) * S], in0=rs1[:], in1=rs2[:],
                         op=ALU.add))(dest, oc),
                     waits=store_guard, inc=ve)

            ve0 = ve.n
            for oc in range(FC):
                ps = PS[oc % 2]
                for fc in range(FC):
                    for n in range(4):
                        p.mm(ps[:, n * 512:(n + 1) * 512],
                             wqT[:, fc * D + oc * 128: fc * D + oc * 128 + 128],
                             x0T[:, fc * S + n * 512: fc * S + (n + 1) * 512],
                             start=(fc == 0), stop=(fc == FC - 1),
                             waits=([(ve, ve0 + max(0, oc - 1))] if fc == 0 and n == 0 else []),
                             inc=(mm if fc == FC - 1 and n == 3 else None))
                rope_chunk(qT, oc, ps)

            ve0 = ve.n
            for oc in range(FC):
                ps = PS[oc % 2]
                for lc in range(2):
                    for n in range(4):
                        p.mm(ps[:, n * 512:(n + 1) * 512],
                             wkbT[:, lc * D + oc * 128: lc * D + oc * 128 + 128],
                             klatT[:, lc * S + n * 512: lc * S + (n + 1) * 512],
                             start=(lc == 0), stop=(lc == 1),
                             waits=([(ve, ve0 + max(0, oc - 1))] if lc == 0 and n == 0 else []),
                             inc=(mm if lc == 1 and n == 3 else None))
                rope_chunk(kT, oc, ps)

            x0t_free = after(mm)

            for fc in range(FC):
                p.dma("sync", qTr[fc * 128:(fc + 1) * 128, tb:tb + S],
                      qT[:, fc * S:(fc + 1) * S], waits=after(ve), inc=qs)
                p.dma("sync", kTr[fc * 128:(fc + 1) * 128, tb:tb + S],
                      kT[:, fc * S:(fc + 1) * S], waits=after(ve), inc=qs)
            store_guard = after(qs)

            ve0 = ve.n
            for tc in range(16):
                ps = PS[tc % 2]
                for lc in range(2):
                    for n in range(2):
                        p.mm(ps[:, n * 512:(n + 1) * 512],
                             vlatT[:, lc * S + tc * 128: lc * S + tc * 128 + 128],
                             wvbT[:, lc * D + n * 512: lc * D + (n + 1) * 512],
                             start=(lc == 0), stop=(lc == 1),
                             waits=([(ve, ve0 + max(0, tc - 1))] if lc == 0 and n == 0 else []),
                             inc=(mm if lc == 1 and n == 1 else None))
                buf = tc % 2
                vv = vstg[:, buf * NH * 65:(buf + 1) * NH * 65]
                p.op("vector",
                     (lambda vv_: lambda e: e.memset(
                         vv_.rearrange("p (h c) -> p h c", c=65)[:, :, 64:65], 1.0))(vv),
                     waits=after(mm) + [(vd, max(0, vd.n - 16))])
                p.op("vector",
                     (lambda vv_, ps_: lambda e: e.tensor_copy(
                         out=vv_.rearrange("p (h c) -> p h c", c=65)[:, :, 0:64],
                         in_=ps_[:, 0:1024]))(vv, ps),
                     inc=ve)
                p.dma("scalar", vext[tb + tc * 128: tb + (tc + 1) * 128, :], vv,
                      waits=after(ve), inc=vd)

        return after(qs, vd, xb)


def a1_io(nc):
    io = {}
    io["x0"] = nc.dram_tensor("x0", [T, D], F32, kind="ExternalInput").ap()
    io["wq"] = nc.dram_tensor("wq", [D, D], F32, kind="ExternalInput").ap()
    io["wk_a"] = nc.dram_tensor("wk_a", [LAT, D], F32, kind="ExternalInput").ap()
    io["wk_b"] = nc.dram_tensor("wk_b", [D, LAT], F32, kind="ExternalInput").ap()
    io["wv_a"] = nc.dram_tensor("wv_a", [LAT, D], F32, kind="ExternalInput").ap()
    io["wv_b"] = nc.dram_tensor("wv_b", [D, LAT], F32, kind="ExternalInput").ap()
    io["ct"] = nc.dram_tensor("ct", [128, FC], F32, kind="ExternalInput").ap()
    io["st"] = nc.dram_tensor("st", [128, FC], F32, kind="ExternalInput").ap()
    io["x0bf"] = nc.dram_tensor("x0bf", [T, D], BF16, kind="Internal").ap()
    return io


def build_a1():
    nc = bass.Bass("TRN2", target_bir_lowering=False)
    io = a1_io(nc)
    io["ident_bf"] = nc.dram_tensor("ident_bf", [128, 128], BF16,
                                    kind="ExternalInput").ap()
    io["qTr"] = nc.dram_tensor("qTr", [D, T], BF16, kind="ExternalOutput").ap()
    io["kTr"] = nc.dram_tensor("kTr", [D, T], BF16, kind="ExternalOutput").ap()
    io["vext"] = nc.dram_tensor("vext", [T, NH * 65], BF16, kind="ExternalOutput").ap()
    from contextlib import ExitStack
    ctx = ExitStack()
    with ctx:
        big = ctx.enter_context(nc.sbuf_tensor("big", [128, ARENA_WORDS], F32))
        psA = ctx.enter_context(nc.psum_tensor("psA", [128, 2048], F32))
        psB = ctx.enter_context(nc.psum_tensor("psB", [128, 2048], F32))
        p = Prog(nc, ctx)
        sm = SemPool(p)
        ex = phase_a1(p, Arena(big), psA, psB, sm, io)
        p.op("sync", lambda e: None, waits=ex)
        p.finalize()
    return nc


_NC_CACHE = {}


def _get_nc(name):
    if name not in _NC_CACHE:
        _NC_CACHE[name] = {"a1": build_a1, "a2": build_a2, "b": build_b,
                           "c": build_c, "d": build_d,
                           "full": build_full}[name]()
    return _NC_CACHE[name]


def build_full():
    nc = bass.Bass("TRN2", target_bir_lowering=False)
    io = a1_io(nc)
    io.update(b_io(nc))
    io.update(c_io(nc))
    io["ident_bf"] = nc.dram_tensor("ident_bf", [128, 128], BF16, kind="ExternalInput").ap()
    io["ident_f32"] = nc.dram_tensor("ident_f32", [128, 128], F32, kind="ExternalInput").ap()
    io["wo"] = nc.dram_tensor("wo", [D, D], F32, kind="ExternalInput").ap()
    io["w_gate"] = nc.dram_tensor("w_gate", [FFD, D], F32, kind="ExternalInput").ap()
    io["w_up"] = nc.dram_tensor("w_up", [FFD, D], F32, kind="ExternalInput").ap()
    io["w_down"] = nc.dram_tensor("w_down", [D, FFD], F32, kind="ExternalInput").ap()
    io["qTr"] = nc.dram_tensor("qTr", [D, T], BF16, kind="Internal").ap()
    io["kTr"] = nc.dram_tensor("kTr", [D, T], BF16, kind="Internal").ap()
    io["vext"] = nc.dram_tensor("vext", [T, NH * 65], BF16, kind="Internal").ap()
    io["x1T"] = nc.dram_tensor("x1T", [D, T], F32, kind="Internal").ap()
    io["x2T"] = nc.dram_tensor("x2T", [D, T], F32, kind="Internal").ap()
    io["x3T"] = nc.dram_tensor("x3T", [D, T], F32, kind="Internal").ap()
    io["out4"] = nc.dram_tensor("out4", [T, D], F32, kind="ExternalOutput").ap()
    from contextlib import ExitStack
    ctx = ExitStack()
    with ctx:
        big = ctx.enter_context(nc.sbuf_tensor("big", [128, ARENA_WORDS], F32))
        psA = ctx.enter_context(nc.psum_tensor("psA", [128, 2048], F32))
        psB = ctx.enter_context(nc.psum_tensor("psB", [128, 2048], F32))
        p = Prog(nc, ctx)
        sm = SemPool(p)
        A = Arena(big)
        phase_a1(p, A, psA, psB, sm, io)
        casts = phase_b_casts(p, sm, io)
        cast_sems = set(casts[0])
        barrier(p, sm, skip=cast_sems)
        phase_a2(p, A, psA, psB, sm, io)
        barrier(p, sm, skip=cast_sems)
        phase_b(p, A, psA, psB, sm, io, casts=casts)
        barrier(p, sm)
        phase_c(p, A, psA, psB, sm, io)
        barrier(p, sm)
        phase_d(p, A, psA, psB, sm, io)
        endw = [(s, s.n) for s in (sm.E + sm.H + sm.G) if s.n > 0]
        p.op("sync", lambda e: None, waits=endw)
        p.finalize()
    return nc


def kernel(**inputs):
    g = lambda k: np.ascontiguousarray(np.asarray(inputs[k], dtype=np.float32))
    x0 = g("L0_attn_x")
    ca = host_consts_a()
    cb = host_consts_b()
    cc = host_consts_c()
    idb = np.eye(128, dtype=BF)
    idf = np.eye(128, dtype=np.float32)
    cores = list(range(NCORES))
    shards = [np.ascontiguousarray(x0[2 * c:2 * c + 2].reshape(T, D)) for c in cores]

    common = {
        "wq": g("L0_attn_wq"), "wk_a": g("L0_attn_wk_a"), "wk_b": g("L0_attn_wk_b"),
        "wv_a": g("L0_attn_wv_a"), "wv_b": g("L0_attn_wv_b"), "wo": g("L0_attn_wo"),
        "Wg": g("L0_ffn_Wg"), "Ws1": g("L0_ffn_Ws1"), "Ws3": g("L0_ffn_Ws3"),
        "Ws2": g("L0_ffn_Ws2"), "We1": g("L0_ffn_We1"), "We3": g("L0_ffn_We3"),
        "We2": g("L0_ffn_We2"),
        "qkv_w": g("L1_attn_qkv_w"), "qkv_b": g("L1_attn_qkv_b"),
        "rq_w": g("L1_attn_rq_w"), "rq_b": g("L1_attn_rq_b"),
        "rkv_w": g("L1_attn_rkv_w"), "rkv_b": g("L1_attn_rkv_b"),
        "rout_w": g("L1_attn_rout_w"), "rout_b": g("L1_attn_rout_b"),
        "out_w": g("L1_attn_out_w"), "out_b": g("L1_attn_out_b"),
        "gl_w": g("L1_attn_gl_w"), "gr_w": g("L1_attn_gr_w"),
        "w_gate": g("L1_ffn_w_gate"), "w_up": g("L1_ffn_w_up"),
        "w_down": g("L1_ffn_w_down"),
        **ca, **cb, **cc, "ident_bf": idb, "ident_f32": idf,
    }
    maps = [{"x0": shards[c], **common} for c in cores]
    res = run_bass_kernel_spmd(_get_nc("full"), maps, core_ids=cores).results

    out = np.empty((16, S, D), np.float32)
    for c in cores:
        out[2 * c:2 * c + 2] = res[c]["out4"].reshape(2, S, D)
    return out



def phase_a2(p, A, psA, psB, sm, io, dbg=False):
    if True:
        x0 = io["x0"]; wo = io["wo"]
        qTr = io["qTr"]; kTr = io["kTr"]; vext = io["vext"]
        identb = io["ident_bf"]; identf = io["ident_f32"]; x1T = io["x1T"]
        if dbg:
            dattn = io["dattn"]; dacc = io["dacc"]; drows = io["drows"]
        A.reset()
        woT = A(128, FC * D, BF16)
        wnat = A(128, FC * D, BF16)
        idb = A(128, 128, BF16)
        idf = A(128, 128, F32)
        qT = A(128, FC * S, BF16)
        AR = A(128, 33024, BF16)
        PT = A(128, 16384, BF16)
        recip = A(64, 1024, F32)
        r1 = A(1, S, F32)
        r2 = A(1, S, F32)
        r3 = A(1, S, F32)
        epst = A(1, 1, F32)
        sumbf = A(1, 1024, BF16)
        ones_c = A(128, 1, BF16)
        ones_rb = A(1, 128, BF16)
        ones_rf = A(1, 128, F32)

        kS = AR[:, 0:16384]
        vS = AR[:, 16384:33024]
        accf = AR.bitcast(F32)[:, 0:16384]
        PTf = PT.bitcast(F32)
        x0stg = wnat.bitcast(F32)[:, 0:2048]

        ld = sm.H[0]; wtr = sm.E[7]; wev = sm.E[8]; kvS = sm.H[1]
        kvA = sm.H[2]; mm = sm.E[0]; ex = sm.E[2]; ve = sm.E[1]
        nrm = sm.E[3]; rq = sm.E[4]; so = sm.H[3]; st2 = sm.E[5]
        ldg = sm.G[0]

        p.dma("sync", idb[:], identb, inc=ld)
        p.dma("sync", idf[:], identf, inc=ld)
        p.op("vector", lambda e: e.memset(ones_c[:], 1.0))
        p.op("vector", lambda e: e.memset(ones_rb[:], 1.0))
        p.op("vector", lambda e: e.memset(ones_rf[:], 1.0))
        p.op("vector", lambda e: e.memset(epst[:], 1e-5), inc=ve)

        def prep_weight(wdram, R, C, dest):
            RC, CC = R // 128, C // 128
            nat_view = wnat[:, 0:RC * C].rearrange("p (r c) -> p r c", r=RC)
            src = wdram.rearrange("(r p) c -> p r c", p=128)
            p.dma("gpsimd", nat_view, src, waits=after(wev), inc=ldg)
            ldv = (ldg, ldg.n)
            idv = (ld, ld.n)
            for rc in range(RC):
                pst = [psA, psB][rc % 2][:, 0:512].bitcast(BF16)
                for cc in range(CC):
                    p.op("tensor",
                         (lambda rc_, cc_, pst_: lambda e: e.transpose(
                             pst_[:, cc_ * 128:(cc_ + 1) * 128],
                             wnat[:, rc_ * C + cc_ * 128: rc_ * C + (cc_ + 1) * 128],
                             idb[:]))(rc, cc, pst),
                         waits=[ldv, idv, (wev, wev.n)],
                         inc=(wtr if cc == CC - 1 else None))
                dv = dest[:, 0:CC * R].rearrange(
                    "p (cc r) -> p cc r", cc=CC)[:, :, rc * 128:(rc + 1) * 128]
                p.op("vector",
                     (lambda dv_, pst_, CC_: lambda e: e.tensor_copy(
                         out=dv_, in_=pst_[:, 0:CC_ * 128]))(dv, pst, CC),
                     waits=after(wtr), inc=wev)

        prep_weight(wo, D, D, woT)
        wev_all = after(wev)

        acc_read_done = []

        for b in range(BPC):
            tb = b * S
            for oc in range(FC):
                p.dma("sync", qT[:, oc * S:(oc + 1) * S],
                      qTr[oc * 128:(oc + 1) * 128, tb:tb + S],
                      waits=acc_read_done, inc=kvS)
                p.dma("sync", kS[:, oc * S:(oc + 1) * S],
                      kTr[oc * 128:(oc + 1) * 128, tb:tb + S],
                      waits=acc_read_done, inc=kvS)
            p.dma("scalar",
                  vS.rearrange("p (tc c) -> p tc c", c=1040),
                  vext[tb:tb + S, :].rearrange("(tc p) c -> p tc c", p=128),
                  waits=acc_read_done, inc=kvA)
            kv_all = after(kvS, kvA)

            it = 0
            for h in range(NH):
                oc, base = h // 2, (h % 2) * 64
                for hq in range(2):
                    qoff = oc * S + hq * 1024
                    for sc in range(16):
                        pse = psA[:, (sc % 2) * 1024:(sc % 2) * 1024 + 1024]
                        for n in range(2):
                            w = []
                            if n == 0:
                                w += [(ex, max(0, ex.n - 1))]
                                if sc == 0:
                                    w += kv_all + wev_all + [(nrm, nrm.n)]
                            p.mm(pse[:, n * 512:(n + 1) * 512],
                                 kS[base:base + 64, oc * S + sc * 128: oc * S + sc * 128 + 128],
                                 qT[base:base + 64, qoff + n * 512: qoff + n * 512 + 512],
                                 start=True, stop=True,
                                 waits=w, inc=(mm if n == 1 else None))
                        p.op("scalar",
                             (lambda sc_, pse_: lambda e: e.activation(
                                 out=PT[:, sc_ * 1024:(sc_ + 1) * 1024], in_=pse_,
                                 func=AF.Exp, scale=0.125))(sc, pse),
                             waits=after(mm), inc=ex)
                        for n in range(2):
                            p.mm(psB[0:65, n * 512:(n + 1) * 512],
                                 vS[:, sc * 1040 + 65 * h: sc * 1040 + 65 * h + 65],
                                 PT[:, sc * 1024 + n * 512: sc * 1024 + n * 512 + 512],
                                 start=(sc == 0), stop=(sc == 15),
                                 waits=(after(ex) if n == 0 else []),
                                 inc=(mm if sc == 15 and n == 1 else None))
                    p.op("vector", lambda e: e.tensor_copy(out=sumbf[:], in_=psB[64:65, 0:1024]),
                         waits=after(mm), inc=ve)
                    for n in range(2):
                        p.mm(psA[0:64, n * 512:(n + 1) * 512],
                             ones_rb[0:1, 0:64], sumbf[0:1, n * 512:(n + 1) * 512],
                             start=True, stop=True,
                             waits=(after(ve, ex) if n == 0 else []),
                             inc=(mm if n == 1 else None))
                    p.op("vector", lambda e: e.reciprocal(out=recip[:], in_=psA[0:64, 0:1024]),
                         waits=after(mm))
                    p.op("vector",
                         (lambda base_, qoff_: lambda e: e.tensor_tensor(
                             out=qT[base_:base_ + 64, qoff_: qoff_ + 1024],
                             in0=psB[0:64, 0:1024], in1=recip[:], op=ALU.mult))(base, qoff),
                         inc=nrm)
                    it += 1

            att_done = after(nrm)
            dump_g = []
            if dbg and b == 0:
                for ocq in range(FC):
                    p.dma("gpsimd", dattn[ocq * 128:(ocq + 1) * 128, :],
                          qT[:, ocq * S:(ocq + 1) * S], waits=att_done, inc=so)
                dump_g = [(so, so.n)]
            for oc in range(FC):
                for dc in range(FC):
                    for n in range(4):
                        w = []
                        if dc == 0 and n == 0:
                            w = att_done + dump_g + ([(rq, rq.n)] if oc > 0 else [])
                        p.mm(psA[:, n * 512:(n + 1) * 512],
                             woT[:, dc * D + oc * 128: dc * D + oc * 128 + 128],
                             qT[:, dc * S + n * 512: dc * S + (n + 1) * 512],
                             start=(dc == 0), stop=(dc == FC - 1),
                             waits=w, inc=(mm if dc == FC - 1 and n == 3 else None))
                p.dma("scalar", x0stg[:].rearrange("p (tc f) -> p tc f", f=128),
                      x0[tb:tb + S, oc * 128:(oc + 1) * 128].rearrange("(tc p) f -> p tc f", p=128),
                      waits=[(wtr, wtr.n), (rq, rq.n)], inc=kvA)
                kvv = (kvA, kvA.n)
                for tc in range(16):
                    p.op("tensor",
                         (lambda tc_: lambda e: e.transpose(
                             psB[:, tc_ * 128:(tc_ + 1) * 128],
                             x0stg[:, tc_ * 128:(tc_ + 1) * 128], idf[:]))(tc),
                         waits=([kvv, (rq, rq.n)] + (att_done if oc == 0 else [])
                                if tc == 0 else []),
                         inc=(mm if tc == 15 else None))
                p.op("vector",
                     (lambda oc_: lambda e: e.tensor_copy(
                         out=accf[:, oc_ * S:(oc_ + 1) * S], in_=psB[:, 0:2048]))(oc),
                     waits=after(mm))
                p.op("vector",
                     (lambda oc_: lambda e: e.tensor_tensor(
                         out=accf[:, oc_ * S:(oc_ + 1) * S], in0=psA[:, 0:2048],
                         in1=accf[:, oc_ * S:(oc_ + 1) * S], op=ALU.add))(oc),
                     inc=rq)

            if dbg and b == 0:
                for ocq in range(FC):
                    p.dma("gpsimd", dacc[ocq * 128:(ocq + 1) * 128, :],
                          accf[:, ocq * S:(ocq + 1) * S], waits=after(rq), inc=so)
                p.op("vector", lambda e: None, waits=[(so, so.n)], inc=ve)
            r1_done = after(rq) + ([(so, so.n)] if dbg and b == 0 else [])
            st2_0 = st2.n
            for oc in range(FC):
                par = (oc % 2) * 2048
                p.op("vector",
                     (lambda oc_, par_: lambda e: e.tensor_copy(
                         out=PT[:, par_:par_ + 2048], in_=accf[:, oc_ * S:(oc_ + 1) * S]))(oc, par),
                     waits=r1_done + [(st2, st2_0 + 2 * max(0, oc - 1))], inc=ve)
                p.op("scalar",
                     (lambda par_: lambda e: e.activation(
                         out=PT[:, 4096 + par_: 4096 + par_ + 2048], in_=PT[:, par_:par_ + 2048],
                         func=AF.Square))(par),
                     waits=after(ve) + [(st2, st2_0 + 2 * max(0, oc - 1))], inc=ex)
                for n in range(4):
                    p.mm(psB[0:1, n * 512:(n + 1) * 512], ones_c[:],
                         PT[:, 4096 + par + n * 512: 4096 + par + (n + 1) * 512],
                         start=(oc == 0), stop=(oc == FC - 1),
                         waits=(after(ex) if n == 0 else []),
                         inc=(st2 if n == 3 else None))
                for n in range(4):
                    p.mm(psA[0:1, n * 512:(n + 1) * 512], ones_c[:],
                         PT[:, par + n * 512: par + (n + 1) * 512],
                         start=(oc == 0), stop=(oc == FC - 1),
                         waits=(after(ve) if n == 0 else []),
                         inc=(st2 if n == 3 else None))
            p.op("vector", lambda e: e.tensor_copy(out=r1[:], in_=psA[0:1, 0:2048]),
                 waits=[(st2, st2_0 + 2 * FC)])
            p.op("vector", lambda e: e.tensor_copy(out=r2[:], in_=psB[0:1, 0:2048]))
            p.op("vector", lambda e: e.tensor_scalar_mul(out=r3[:], in0=r1[:], scalar1=1.0 / D))
            p.op("vector", lambda e: e.tensor_tensor(out=r1[:], in0=r3[:], in1=r3[:], op=ALU.mult))
            p.op("vector", lambda e: e.tensor_scalar_mul(out=r2[:], in0=r2[:], scalar1=1.0 / D))
            p.op("vector", lambda e: e.tensor_tensor(out=r2[:], in0=r2[:], in1=r1[:], op=ALU.subtract))
            p.op("vector", lambda e: e.tensor_copy(out=PTf[0:1, 0:2048], in_=r2[:]), inc=ve)
            p.op("scalar", lambda e: e.activation(out=r1[:], in_=r2[:], func=AF.Sqrt,
                                                  bias=epst[0:1, 0:1]),
                 waits=after(ve), inc=ex)
            p.op("vector", lambda e: e.reciprocal(out=r2[:], in_=r1[:]),
                 waits=after(ex))
            p.op("vector", lambda e: e.tensor_copy(out=PTf[0:1, 0:2048], in_=r2[:]), inc=ve)
            if dbg and b == 0:
                p.dma("gpsimd", drows[0:1, :], r3[:], waits=after(ve), inc=so)
                p.dma("gpsimd", drows[1:2, :], r2[:], waits=after(ve), inc=so)
            rows_done = after(ve) + ([(so, so.n)] if dbg and b == 0 else [])
            for n in range(4):
                p.mm(psA[:, n * 512:(n + 1) * 512], ones_rf[:],
                     r3[0:1, n * 512:(n + 1) * 512], start=True, stop=True,
                     waits=(rows_done if n == 0 else []))
            for n in range(4):
                p.mm(psB[:, n * 512:(n + 1) * 512], ones_rf[:],
                     r2[0:1, n * 512:(n + 1) * 512], start=True, stop=True,
                     inc=(mm if n == 3 else None))
            for oc in range(FC):
                par = (oc % 2) * 2048
                sub = PTf[:, 4096 + par: 4096 + par + 2048]
                outv = PTf[:, par: par + 2048]
                p.op("vector",
                     (lambda oc_, sub_: lambda e: e.tensor_tensor(
                         out=sub_, in0=accf[:, oc_ * S:(oc_ + 1) * S], in1=psA[:, 0:2048],
                         op=ALU.subtract))(oc, sub),
                     waits=after(mm) + [(so, max(0, so.n - 16))])
                p.op("vector",
                     (lambda sub_, outv_: lambda e: e.tensor_tensor(
                         out=outv_, in0=sub_, in1=psB[:, 0:2048], op=ALU.mult))(sub, outv),
                     inc=rq)
                p.dma("sync", x1T[oc * 128:(oc + 1) * 128, tb:tb + S],
                      outv, waits=after(rq), inc=so)
            acc_read_done = after(rq) + after(so)

        return after(so)


def build_a2(dbg=False):
    nc = bass.Bass("TRN2", target_bir_lowering=False)
    io = {}
    io["x0"] = nc.dram_tensor("x0", [T, D], F32, kind="ExternalInput").ap()
    io["wo"] = nc.dram_tensor("wo", [D, D], F32, kind="ExternalInput").ap()
    io["qTr"] = nc.dram_tensor("qTr", [D, T], BF16, kind="ExternalInput").ap()
    io["kTr"] = nc.dram_tensor("kTr", [D, T], BF16, kind="ExternalInput").ap()
    io["vext"] = nc.dram_tensor("vext", [T, NH * 65], BF16, kind="ExternalInput").ap()
    io["ident_bf"] = nc.dram_tensor("ident_bf", [128, 128], BF16, kind="ExternalInput").ap()
    io["ident_f32"] = nc.dram_tensor("ident_f32", [128, 128], F32, kind="ExternalInput").ap()
    io["x1T"] = nc.dram_tensor("x1T", [D, T], F32, kind="ExternalOutput").ap()
    if dbg:
        io["dattn"] = nc.dram_tensor("dattn", [D, S], BF16, kind="ExternalOutput").ap()
        io["dacc"] = nc.dram_tensor("dacc", [D, S], F32, kind="ExternalOutput").ap()
        io["drows"] = nc.dram_tensor("drows", [2, S], F32, kind="ExternalOutput").ap()
    from contextlib import ExitStack
    ctx = ExitStack()
    with ctx:
        big = ctx.enter_context(nc.sbuf_tensor("big", [128, ARENA_WORDS], F32))
        psA = ctx.enter_context(nc.psum_tensor("psA", [128, 2048], F32))
        psB = ctx.enter_context(nc.psum_tensor("psB", [128, 2048], F32))
        p = Prog(nc, ctx)
        sm = SemPool(p)
        ex = phase_a2(p, Arena(big), psA, psB, sm, io, dbg=dbg)
        p.op("sync", lambda e: None, waits=ex)
        p.finalize()
    return nc



NEXP = 8
FF = 2048
FFC = FF // 128


def host_consts_b():
    sel = np.zeros((8, 8 * 128), np.float32)
    for e in range(8):
        sel[e, e * 128:(e + 1) * 128] = 1.0
    return {"sel": sel.astype(BF), "ident_f32": np.eye(128, dtype=np.float32)}


def phase_b_casts(p, sm, io):
    w1bf = io["w1bf"]; w3bf = io["w3bf"]; w2bf = io["w2bf"]
    Ws1 = io["Ws1"]; Ws3 = io["Ws3"]; Ws2 = io["Ws2"]
    We1 = io["We1"]; We3 = io["We3"]; We2 = io["We2"]
    cbs = [sm.G[9 + e] for e in range(9)]
    for e in range(9):
        if e == 0:
            s1, s3, s2 = Ws1, Ws3, Ws2
        else:
            s1, s3, s2 = We1[e - 1], We3[e - 1], We2[e - 1]
        p.dma("gpsimd", w1bf[e], s1, inc=cbs[e])
        p.dma("gpsimd", w3bf[e], s3, inc=cbs[e])
        p.dma("gpsimd", w2bf[e], s2, inc=cbs[e])
    return cbs, [s.n for s in cbs]


def phase_b(p, A, psA, psB, sm, io, dbg=False, dump_half=0, casts=None):
    if True:
        x1T = io["x1T"]; Wg = io["Wg"]; Ws1 = io["Ws1"]; Ws3 = io["Ws3"]
        Ws2 = io["Ws2"]; We1 = io["We1"]; We3 = io["We3"]; We2 = io["We2"]
        sel_in = io["sel"]; identf = io["ident_f32"]; x2T = io["x2T"]
        w1bf = io["w1bf"]; w3bf = io["w3bf"]; w2bf = io["w2bf"]
        if dbg:
            dgates = io["dgates"]; dwrows = io["dwrows"]; dwb = io["dwb"]
            dracc0 = io["dracc0"]; dracc = io["dracc"]; dh = io["dh"]
            dxq = io["dxq"]; dhb = io["dhb"]; dw1 = io["dw1"]
            dgtm = io["dgtm"]; dwtm = io["dwtm"]
        A.reset()
        xqT = A(128, FC * S, BF16)
        h = A(128, FFC * S, BF16)
        racc = A(128, FC * S, F32)
        w1c = A(128, 4 * FC * 128, BF16)
        w3c = A(128, 4 * FC * 128, BF16)
        w2c = A(128, 2 * FFC * 128, BF16)
        wb_sb = A(128, S, F32)
        wrows = A(8, S, BF16)
        sel_sb = A(8, 8 * 128, BF16)
        wg_sb = A(128, FC * NEXP, F32)
        xsc = A(128, S, F32)
        idf = A(128, 128, F32)
        ones_c = A(128, 1, BF16)
        ones_rf = A(1, 128, F32)
        epst = A(1, 1, F32)

        hf = h.bitcast(F32)
        gatesT = hf[0:8, 0:2048]
        gates_tm = hf[:, 2048:2304].rearrange("p (tc e) -> p tc e", e=8)
        w_tm = hf[:, 2304:2560].rearrange("p (tc e) -> p tc e", e=8)
        r1 = hf[0:1, 4096:6144]
        r2 = hf[0:1, 6144:8192]
        r3 = hf[0:1, 2048:4096]

        ld = sm.H[0]; wlS = sm.H[1]; wlA = sm.H[2]; mm = sm.E[0]
        sx = sm.E[2]; ve = sm.E[1]; hv = sm.E[3]; dn = sm.E[4]
        so = sm.H[3]; g2 = sm.E[5]; vp = sm.E[6]; wlV = sm.H[4]

        p.dma("sync", idf[:], identf, inc=ld)
        p.dma("sync", sel_sb[:], sel_in, inc=ld)
        p.dma("sync", wg_sb[:].rearrange("p (r c) -> p r c", r=FC),
              Wg.rearrange("(r p) c -> p r c", p=128), inc=ld)
        p.op("vector", lambda e: e.memset(ones_c[:], 1.0))
        p.op("vector", lambda e: e.memset(ones_rf[:], 1.0))
        p.op("vector", lambda e: e.memset(epst[:], 1e-5), inc=ve)

        if casts is None:
            casts = phase_b_casts(p, sm, io)
        cbs, cb_done = casts

        half_done = []
        dm_prev = None
        for th in range(2):
            tb = th * S
            for fc in range(FC):
                p.dma("scalar", xsc[:],
                      x1T[fc * 128:(fc + 1) * 128, tb:tb + S],
                      waits=half_done + ([(ve, ve.n), (mm, mm.n)] if fc > 0
                                         else [(mm, mm.n)]), inc=wlA)
                p.op("vector",
                     (lambda fc_: lambda e: e.tensor_copy(
                         out=xqT[:, fc_ * S:(fc_ + 1) * S], in_=xsc[:]))(fc),
                     waits=after(wlA), inc=ve)
                for n in range(4):
                    p.mm(psA[0:8, n * 512:(n + 1) * 512],
                         wg_sb[:, fc * NEXP:(fc + 1) * NEXP],
                         xsc[:, n * 512:(n + 1) * 512],
                         start=(fc == 0), stop=(fc == FC - 1),
                         waits=(after(wlA) + after(ld) + half_done if n == 0 and fc == 0
                                else (after(wlA) if n == 0 else [])),
                         inc=(mm if n == 3 else None))
            xq_ready = after(ve)
            p.op("scalar", lambda e: e.activation(out=gatesT, in_=psA[0:8, 0:2048],
                                                  func=AF.Sigmoid),
                 waits=after(mm) + half_done, inc=sx)
            g20 = g2.n
            for tc in range(16):
                p.op("tensor",
                     (lambda tc_: lambda e: e.transpose(
                         psB[:, (tc_ % 4) * 8:(tc_ % 4) * 8 + 8],
                         gatesT[0:8, tc_ * 128:(tc_ + 1) * 128], idf[0:8, 0:8]))(tc),
                     waits=(after(sx) if tc == 0 else []) + [(g2, g20 + max(0, 2 * tc - 6))],
                     inc=mm)
                p.op("vector",
                     (lambda tc_: lambda e: e.tensor_copy(
                         out=gates_tm[:, tc_, :], in_=psB[:, (tc_ % 4) * 8:(tc_ % 4) * 8 + 8]))(tc),
                     waits=after(mm), inc=g2)
                gv = gates_tm[:, tc, :]
                wv = w_tm[:, tc, :]
                def vlink(make, extra=()):
                    p.op("vector", make, waits=after(vp) + list(extra), inc=vp)
                vlink((lambda gv_: lambda e: e.tensor_reduce(
                    out=xsc[:, 0:1], in_=gv_, axis=AX.X, op=ALU.max))(gv),
                      extra=after(g2))
                vlink((lambda gv_: lambda e: e.tensor_scalar(
                    out=xsc[:, 8:16], in0=gv_, scalar1=xsc[:, 0:1], scalar2=-1e30,
                    op0=ALU.is_equal, op1=ALU.mult))(gv))
                vlink((lambda gv_: lambda e: e.tensor_tensor(
                    out=xsc[:, 8:16], in0=xsc[:, 8:16], in1=gv_, op=ALU.add))(gv))
                vlink(lambda e: e.tensor_reduce(
                    out=xsc[:, 1:2], in_=xsc[:, 8:16], axis=AX.X, op=ALU.max))
                vlink(lambda e: e.tensor_tensor(
                    out=xsc[:, 2:3], in0=xsc[:, 0:1], in1=xsc[:, 1:2], op=ALU.add))
                vlink(lambda e: e.tensor_scalar_mul(
                    out=xsc[:, 2:3], in0=xsc[:, 2:3], scalar1=3.0))
                vlink(lambda e: e.reciprocal(out=xsc[:, 3:4], in_=xsc[:, 2:3]))
                vlink((lambda gv_: lambda e: e.tensor_scalar(
                    out=xsc[:, 8:16], in0=gv_, scalar1=xsc[:, 1:2], scalar2=None,
                    op0=ALU.is_ge))(gv))
                p.op("vector", (lambda gv_, wv_: lambda e: e.scalar_tensor_tensor(
                    out=wv_, in0=gv_, scalar=xsc[:, 3:4], in1=xsc[:, 8:16],
                    op0=ALU.mult, op1=ALU.mult))(gv, wv),
                     waits=after(vp), inc=ve)
                p.op("tensor",
                     (lambda tc_, wv_: lambda e: e.transpose(
                         psB[:, 512 + (tc_ % 4) * 128: 512 + (tc_ % 4) * 128 + 128][0:8, :],
                         wv_, idf[:]))(tc, wv),
                     waits=after(ve), inc=mm)
                p.op("vector",
                     (lambda tc_: lambda e: e.tensor_copy(
                         out=wrows[0:8, tc_ * 128:(tc_ + 1) * 128],
                         in_=psB[:, 512 + (tc_ % 4) * 128: 512 + (tc_ % 4) * 128 + 128][0:8, :]))(tc),
                     waits=after(mm), inc=g2)
            p.op("vector", lambda e_: e_.tensor_copy(out=wb_sb[0:8, 0:2048], in_=wrows[0:8, :]),
                 inc=g2)
            gating_done = after(g2, ve)
            if dbg in (True, "dumps_only", "g4") and th == dump_half:
                p.dma("sync", dgates, gatesT, waits=gating_done, inc=so)
                p.dma("sync", dwrows, wrows[0:8, :], waits=gating_done, inc=so)
                p.dma("sync", dgtm, hf[:, 2048:2304], waits=gating_done, inc=so)
                p.dma("sync", dwtm, hf[:, 2304:2560], waits=gating_done, inc=so)

            for e in range(1 if dbg == "shared" else 9):
                routed = e > 0
                if routed:
                    for n in range(4):
                        p.mm(psB[:, n * 512:(n + 1) * 512],
                             sel_sb[:, (e - 1) * 128:e * 128],
                             wrows[0:8, n * 512:(n + 1) * 512],
                             start=True, stop=True,
                             waits=(gating_done + after(ve, dn, hv) if n == 0 else []),
                             inc=(mm if n == 3 else None))
                    p.op("vector", lambda e_: e_.tensor_copy(out=wb_sb[:], in_=psB[:, 0:2048]),
                         waits=after(mm), inc=ve)
                    if dbg in (True, "dumps_only", "wb1") and th == dump_half and e == 1:
                        p.dma("sync", dwb, wb_sb[:], waits=after(ve), inc=so)
                hv0 = hv.n
                for ffc in range(FFC):
                    if ffc % 2 == 0:
                        sb = (ffc // 2) % 2
                        p.dma("sync", w1c[:, sb * 2048:(sb + 1) * 2048].rearrange(
                                  "p (r c) -> p r c", r=FC),
                              w1bf[e].rearrange("(r p) c -> p r c", p=128)[:, :, ffc * 128:(ffc + 2) * 128],
                              waits=[(cbs[e], cb_done[e]), (hv, hv0 + max(0, ffc - 2))], inc=wlS)
                        p.dma("scalar", w3c[:, sb * 2048:(sb + 1) * 2048].rearrange(
                                  "p (r c) -> p r c", r=FC),
                              w3bf[e].rearrange("(r p) c -> p r c", p=128)[:, :, ffc * 128:(ffc + 2) * 128],
                              waits=[(cbs[e], cb_done[e]), (hv, hv0 + max(0, ffc - 2))], inc=wlA)
                    sb_ = (ffc // 2) % 2
                    wcol = lambda fc_: sb_ * 2048 + fc_ * 256 + (ffc % 2) * 128
                    wlv = [(wlS, wlS.n), (wlA, wlA.n)]
                    for fc in range(FC):
                        for n in range(4):
                            p.mm(psA[:, n * 512:(n + 1) * 512],
                                 w1c[:, wcol(fc): wcol(fc) + 128],
                                 xqT[:, fc * S + n * 512: fc * S + (n + 1) * 512],
                                 start=(fc == 0), stop=(fc == FC - 1),
                                 waits=((wlv + [(sx, sx.n)]
                                         + ([(dn, dn.n), (ve, ve.n)] + (gating_done if e == 0 else [])
                                            if ffc == 0 else []))
                                        if fc == 0 and n == 0 else []),
                                 inc=(mm if fc == FC - 1 and n == 3 else None))
                    for fc in range(FC):
                        for n in range(4):
                            p.mm(psB[:, n * 512:(n + 1) * 512],
                                 w3c[:, wcol(fc): wcol(fc) + 128],
                                 xqT[:, fc * S + n * 512: fc * S + (n + 1) * 512],
                                 start=(fc == 0), stop=(fc == FC - 1),
                                 waits=((wlv + [(hv, hv.n)]
                                         + ([(dn, dn.n), (ve, ve.n)] + (gating_done if e == 0 else [])
                                            if ffc == 0 else []))
                                        if fc == 0 and n == 0 else []),
                                 inc=(mm if fc == FC - 1 and n == 3 else None))
                    mmv = (mm, mm.n)
                    p.op("scalar",
                         (lambda ffc_: lambda e_: e_.activation(
                             out=h[:, ffc_ * S:(ffc_ + 1) * S], in_=psA[:, 0:2048],
                             func=AF.Silu))(ffc),
                         waits=[mmv, (dn, dn.n)], inc=sx)
                    p.op("vector",
                         (lambda ffc_: lambda e_: e_.tensor_tensor(
                             out=h[:, ffc_ * S:(ffc_ + 1) * S],
                             in0=h[:, ffc_ * S:(ffc_ + 1) * S],
                             in1=psB[:, 0:2048], op=ALU.mult))(ffc),
                         waits=after(sx), inc=hv)
                hv_all = after(hv)
                dm_cur = []
                for dc in range(FC):
                    buf = dc % 2
                    if dc >= 2:
                        w2guard = (mm, dm_cur[dc - 2])
                    elif dm_prev is not None:
                        w2guard = (mm, dm_prev[dc + 6])
                    else:
                        w2guard = (mm, 0)
                    p.dma("scalar", w2c[:, buf * 2048:(buf + 1) * 2048].rearrange(
                              "p (r c) -> p r c", r=FFC),
                          w2bf[e].rearrange("(r p) c -> p r c", p=128)[:, :, dc * 128:(dc + 1) * 128],
                          waits=[(cbs[e], cb_done[e]), w2guard],
                          inc=wlA)
                    wlv = (wlA, wlA.n)
                    for ffc in range(FFC):
                        for n in range(4):
                            p.mm(psA[:, n * 512:(n + 1) * 512],
                                 w2c[:, buf * 2048 + ffc * 128: buf * 2048 + (ffc + 1) * 128],
                                 h[:, ffc * S + n * 512: ffc * S + (n + 1) * 512],
                                 start=(ffc == 0), stop=(ffc == FFC - 1),
                                 waits=(hv_all + [wlv, (dn, max(0, dn.n - 1))]
                                        if ffc == 0 and n == 0 else []),
                                 inc=(mm if ffc == FFC - 1 and n == 3 else None))
                    dm_cur.append(mm.n)
                    if routed:
                        p.op("vector",
                             (lambda dc_: lambda e_: e_.tensor_tensor(
                                 out=xsc[:], in0=psA[:, 0:2048], in1=wb_sb[:],
                                 op=ALU.mult))(dc),
                             waits=after(mm), inc=dn)
                    else:
                        p.op("vector",
                             (lambda dc_: lambda e_: e_.tensor_scalar_mul(
                                 out=xsc[:], in0=psA[:, 0:2048], scalar1=1.0 / 3.0))(dc),
                             waits=after(mm), inc=dn)
                    if e == 0:
                        p.op("vector",
                             (lambda dc_: lambda e_: e_.tensor_copy(
                                 out=racc[:, dc_ * S:(dc_ + 1) * S], in_=xsc[:]))(dc),
                             waits=half_done, inc=dn)
                        if dbg in (True, "dumps_only", "r0") and th == dump_half and dc == FC - 1:
                            _dn_now = after(dn)
                            for dcq in range(FC):
                                p.dma("gpsimd", dracc0[dcq * 128:(dcq + 1) * 128, :],
                                      racc[:, dcq * S:(dcq + 1) * S], waits=_dn_now, inc=so)
                    else:
                        p.op("vector",
                             (lambda dc_: lambda e_: e_.tensor_tensor(
                                 out=racc[:, dc_ * S:(dc_ + 1) * S], in0=xsc[:],
                                 in1=racc[:, dc_ * S:(dc_ + 1) * S], op=ALU.add))(dc),
                             inc=dn)

            experts_done = after(dn, mm)
            if dbg and th == dump_half:
                p.dma("sync", dxq, xqT[:, 0:S], waits=experts_done, inc=so)
                p.dma("sync", dhb, h[:, 0:S], waits=experts_done, inc=so)
                p.dma("sync", dw1, w1c[:, 0:1024], waits=experts_done, inc=so)
                for dcq in range(FC):
                    p.dma("sync", dracc[dcq * 128:(dcq + 1) * 128, :],
                          racc[:, dcq * S:(dcq + 1) * S], waits=experts_done, inc=so)
                p.dma("sync", dh, h.bitcast(F32)[:, 0:2048], waits=experts_done, inc=so)
            dump_guard = [(so, so.n)] if dbg and dbg != "dumps_only" else []
            for dc in range(FC):
                p.dma("scalar", xsc[:], x1T[dc * 128:(dc + 1) * 128, tb:tb + S],
                      waits=experts_done + [(ve, ve.n)] + (dump_guard if dc == 0 else []), inc=wlA)
                p.op("vector",
                     (lambda dc_: lambda e_: e_.tensor_tensor(
                         out=racc[:, dc_ * S:(dc_ + 1) * S], in0=xsc[:],
                         in1=racc[:, dc_ * S:(dc_ + 1) * S], op=ALU.add))(dc),
                     waits=after(wlA), inc=ve)
            r1_done = after(ve)
            st20 = mm.n
            for oc in range(FC):
                par = (oc % 2) * 2048
                p.op("vector",
                     (lambda oc_, par_: lambda e_: e_.tensor_copy(
                         out=h[:, par_:par_ + 2048], in_=racc[:, oc_ * S:(oc_ + 1) * S]))(oc, par),
                     waits=r1_done + [(mm, st20 + 2 * max(0, oc - 1))], inc=ve)
                p.op("scalar",
                     (lambda par_: lambda e_: e_.activation(
                         out=h[:, 4096 + par_: 4096 + par_ + 2048], in_=h[:, par_:par_ + 2048],
                         func=AF.Square))(par),
                     waits=after(ve) + [(mm, st20 + 2 * max(0, oc - 1))], inc=sx)
                for n in range(4):
                    p.mm(psB[0:1, n * 512:(n + 1) * 512], ones_c[:],
                         h[:, 4096 + par + n * 512: 4096 + par + (n + 1) * 512],
                         start=(oc == 0), stop=(oc == FC - 1),
                         waits=(after(sx) if n == 0 else []),
                         inc=(mm if n == 3 else None))
                for n in range(4):
                    p.mm(psA[0:1, n * 512:(n + 1) * 512], ones_c[:],
                         h[:, par + n * 512: par + (n + 1) * 512],
                         start=(oc == 0), stop=(oc == FC - 1),
                         waits=(after(ve) if n == 0 else []),
                         inc=(mm if n == 3 else None))
            p.op("vector", lambda e_: e_.tensor_copy(out=r1, in_=psA[0:1, 0:2048]),
                 waits=after(mm))
            p.op("vector", lambda e_: e_.tensor_copy(out=r2, in_=psB[0:1, 0:2048]))
            p.op("vector", lambda e_: e_.tensor_scalar_mul(out=r3, in0=r1, scalar1=1.0 / D))
            p.op("vector", lambda e_: e_.tensor_tensor(out=r1, in0=r3, in1=r3, op=ALU.mult))
            p.op("vector", lambda e_: e_.tensor_scalar_mul(out=r2, in0=r2, scalar1=1.0 / D))
            p.op("vector", lambda e_: e_.tensor_tensor(out=r2, in0=r2, in1=r1, op=ALU.subtract))
            p.op("vector", lambda e_: e_.tensor_copy(out=hf[1:2, 6144:8192] if False else xsc[0:1, :], in_=r2), inc=ve)
            p.op("scalar", lambda e_: e_.activation(out=r1, in_=r2, func=AF.Sqrt,
                                                    bias=epst[0:1, 0:1]),
                 waits=after(ve), inc=sx)
            p.op("vector", lambda e_: e_.reciprocal(out=r2, in_=r1),
                 waits=after(sx))
            p.op("vector", lambda e_: e_.tensor_copy(out=xsc[0:1, :], in_=r2), inc=ve)
            rows_done = after(ve)
            for n in range(4):
                p.mm(psA[:, n * 512:(n + 1) * 512], ones_rf[:],
                     r3[0:1, n * 512:(n + 1) * 512], start=True, stop=True,
                     waits=(rows_done if n == 0 else []))
            for n in range(4):
                p.mm(psB[:, n * 512:(n + 1) * 512], ones_rf[:],
                     r2[0:1, n * 512:(n + 1) * 512], start=True, stop=True,
                     inc=(mm if n == 3 else None))
            for oc in range(FC):
                par = (oc % 2) * 2048
                sub = hf[:, 4096 + par: 4096 + par + 2048]
                outv = hf[:, par: par + 2048]
                p.op("vector",
                     (lambda oc_, sub_: lambda e_: e_.tensor_tensor(
                         out=sub_, in0=racc[:, oc_ * S:(oc_ + 1) * S], in1=psA[:, 0:2048],
                         op=ALU.subtract))(oc, sub),
                     waits=after(mm) + [(so, max(0, so.n - 16))])
                p.op("vector",
                     (lambda sub_, outv_: lambda e_: e_.tensor_tensor(
                         out=outv_, in0=sub_, in1=psB[:, 0:2048], op=ALU.mult))(sub, outv),
                     inc=ve)
                p.dma("sync", x2T[oc * 128:(oc + 1) * 128, tb:tb + S],
                      outv, waits=after(ve), inc=so)
            half_done = after(ve) + after(so)

        return after(so)


def b_io(nc):
    io = {}
    io["Wg"] = nc.dram_tensor("Wg", [D, NEXP], F32, kind="ExternalInput").ap()
    io["Ws1"] = nc.dram_tensor("Ws1", [D, FF], F32, kind="ExternalInput").ap()
    io["Ws3"] = nc.dram_tensor("Ws3", [D, FF], F32, kind="ExternalInput").ap()
    io["Ws2"] = nc.dram_tensor("Ws2", [FF, D], F32, kind="ExternalInput").ap()
    io["We1"] = nc.dram_tensor("We1", [NEXP, D, FF], F32, kind="ExternalInput").ap()
    io["We3"] = nc.dram_tensor("We3", [NEXP, D, FF], F32, kind="ExternalInput").ap()
    io["We2"] = nc.dram_tensor("We2", [NEXP, FF, D], F32, kind="ExternalInput").ap()
    io["sel"] = nc.dram_tensor("sel", [8, 8 * 128], BF16, kind="ExternalInput").ap()
    io["w1bf"] = nc.dram_tensor("w1bf", [9, D, FF], BF16, kind="Internal").ap()
    io["w3bf"] = nc.dram_tensor("w3bf", [9, D, FF], BF16, kind="Internal").ap()
    io["w2bf"] = nc.dram_tensor("w2bf", [9, FF, D], BF16, kind="Internal").ap()
    return io


def build_b(dbg=False, dump_half=0):
    nc = bass.Bass("TRN2", target_bir_lowering=False)
    io = b_io(nc)
    io["x1T"] = nc.dram_tensor("x1T", [D, T], F32, kind="ExternalInput").ap()
    io["ident_f32"] = nc.dram_tensor("ident_f32", [128, 128], F32, kind="ExternalInput").ap()
    io["x2T"] = nc.dram_tensor("x2T", [D, T], F32, kind="ExternalOutput").ap()
    if dbg:
        io["dgates"] = nc.dram_tensor("dgates", [8, S], F32, kind="ExternalOutput").ap()
        io["dwrows"] = nc.dram_tensor("dwrows", [8, S], BF16, kind="ExternalOutput").ap()
        io["dwb"] = nc.dram_tensor("dwb", [128, S], F32, kind="ExternalOutput").ap()
        io["dracc0"] = nc.dram_tensor("dracc0", [D, S], F32, kind="ExternalOutput").ap()
        io["dracc"] = nc.dram_tensor("dracc", [D, S], F32, kind="ExternalOutput").ap()
        io["dh"] = nc.dram_tensor("dh", [128, S], F32, kind="ExternalOutput").ap()
        io["dxq"] = nc.dram_tensor("dxq", [128, S], BF16, kind="ExternalOutput").ap()
        io["dhb"] = nc.dram_tensor("dhb", [128, S], BF16, kind="ExternalOutput").ap()
        io["dw1"] = nc.dram_tensor("dw1", [128, FC * 128], BF16, kind="ExternalOutput").ap()
        io["dgtm"] = nc.dram_tensor("dgtm", [128, 256], F32, kind="ExternalOutput").ap()
        io["dwtm"] = nc.dram_tensor("dwtm", [128, 256], F32, kind="ExternalOutput").ap()
    from contextlib import ExitStack
    ctx = ExitStack()
    with ctx:
        big = ctx.enter_context(nc.sbuf_tensor("big", [128, ARENA_WORDS], F32))
        psA = ctx.enter_context(nc.psum_tensor("psA", [128, 2048], F32))
        psB = ctx.enter_context(nc.psum_tensor("psB", [128, 2048], F32))
        p = Prog(nc, ctx)
        sm = SemPool(p)
        ex = phase_b(p, Arena(big), psA, psB, sm, io, dbg=dbg, dump_half=dump_half)
        p.op("sync", lambda e: None, waits=ex)
        p.finalize()
    return nc



def host_consts_c():
    def mk_mask(first_block):
        kp = np.arange(384)[:, None]
        qr = np.arange(256)[None, :]
        dist = qr + 128 - kp
        ok = (dist >= 1) & (dist <= 128) & (kp >= 1)
        if first_block:
            ok = ok & (kp >= 128)
        m = ok.astype(np.float32)
        out = np.zeros((128, 3 * 256), np.float32)
        for kc in range(3):
            out[:, kc * 256:(kc + 1) * 256] = m[kc * 128:(kc + 1) * 128]
        return out.astype(BF)
    t = np.arange(2048)
    l = np.arange(64)
    lat_allowed = l[None, :] < (t[:, None] // 32)
    lm = np.where(lat_allowed, 0.0, -1e9).astype(np.float32)
    lm_sb = np.zeros((128, 16 * 64), np.float32)
    for tc in range(16):
        lm_sb[:, tc * 64:(tc + 1) * 64] = lm[tc * 128:(tc + 1) * 128]
    rm = np.zeros((128, 16), np.float32)
    for tc in range(16):
        rm[:, tc] = (t[tc * 128:(tc + 1) * 128] >= 32).astype(np.float32)
    return {"maskg": mk_mask(False), "mask0": mk_mask(True),
            "latmask": lm_sb, "latrow": rm}


def phase_c(p, A, psA, psB, sm, io):
    if True:
        x2T = io["x2T"]; qkv_w = io["qkv_w"]; qkv_b = io["qkv_b"]
        rq_w = io["rq_w"]; rq_b = io["rq_b"]; rkv_w = io["rkv_w"]
        rkv_b = io["rkv_b"]; rout_w = io["rout_w"]; rout_b = io["rout_b"]
        out_w = io["out_w"]; out_b = io["out_b"]; gl_w = io["gl_w"]
        gr_w = io["gr_w"]; maskg_in = io["maskg"]; mask0_in = io["mask0"]
        latm_in = io["latmask"]; latr_in = io["latrow"]
        identb = io["ident_bf"]; identf = io["ident_f32"]; x3T = io["x3T"]
        A.reset()
        AR1 = A(128, 8192, BF16)
        AR2 = A(128, 35088, BF16)
        AR3 = A(128, 16384, BF16)
        qT = A(128, FC * S, BF16)
        rqT = A(128, 2 * S, BF16)
        rkvT = A(128, 4 * 64, BF16)
        rvTt = A(128, 2 * 64, BF16)
        remT = A(128, S, BF16)
        pooledbf = A(128, FC * 64, BF16)
        pooledf = A(128, 64, F32)
        vbias = A(128, D, F32)
        vbrow = A(1, D, BF16)
        maskg_sb = A(128, 768, BF16)
        mask0_sb = A(128, 768, BF16)
        latm_sb = A(128, 1024, F32)
        latr_sb = A(128, 16, F32)
        idb = A(128, 128, BF16)
        idf = A(128, 128, F32)
        PTl = A(128, 2 * 768, BF16)
        lsc = A(128, 80, F32)
        lattn = A(128, 64, BF16)
        sc1 = A(128, S, F32)
        xstg = A(128, S, F32)
        biasq = A(128, 32, F32)
        biaso = A(128, 16, F32)
        brow = A(1, 1024, F32)
        glw = A(128, 2, BF16)
        sumbf = A(1, 256, BF16)
        grow = A(1, S, BF16)
        recip = A(64, 256, F32)
        epst = A(1, 1, F32)
        ones_c = A(128, 1, BF16)
        ones_rb = A(1, 128, BF16)
        ones_rf = A(1, 128, F32)

        kTx = AR2[:, 0:8 * 2176]
        vX = AR2[:, 8 * 2176:8 * 2176 + 17 * 1040]
        accf = AR2.bitcast(F32)[:, 0:FC * S]
        x2qT = AR3[:, 0:FC * S]
        latT = AR3[:, 0:2 * S]
        loT = AR3[:, 2 * S:4 * S]
        lnbf = AR3
        lnf = AR3.bitcast(F32)
        r1 = lnf[0:1, 4096:6144]
        r2 = lnf[0:1, 6144:8192]
        r3 = lnf[0:1, 2048:4096]

        ldS = sm.H[0]; ldA = sm.H[1]; ldG = sm.G[0]; mm = sm.E[0]
        ve = sm.E[1]; ex = sm.E[2]; nrm = sm.E[3]; rq_ = sm.E[4]
        so = sm.H[3]; st2 = sm.E[5]; vp = sm.E[6]

        p.dma("sync", idb[:], identb, inc=ldS)
        p.dma("sync", idf[:], identf, inc=ldS)
        p.dma("sync", maskg_sb[:], maskg_in, inc=ldS)
        p.dma("sync", mask0_sb[:], mask0_in, inc=ldS)
        p.dma("sync", latm_sb[:], latm_in, inc=ldS)
        p.dma("sync", latr_sb[:], latr_in, inc=ldS)
        p.dma("gpsimd", glw[0:64, 0:1], gl_w, inc=ldG)
        p.dma("gpsimd", glw[0:64, 1:2], gr_w, inc=ldG)
        p.dma("gpsimd", glw[64:128, 0:1], gl_w, inc=ldG)
        p.dma("gpsimd", glw[64:128, 1:2], gr_w, inc=ldG)
        p.dma("gpsimd", vbrow[0:1, :], qkv_b[None, 2 * D:3 * D], inc=ldG)
        p.op("vector", lambda e: e.memset(ones_c[:], 1.0))
        p.op("vector", lambda e: e.memset(ones_rb[:], 1.0))
        p.op("vector", lambda e: e.memset(ones_rf[:], 1.0))
        p.op("vector", lambda e: e.memset(epst[:], 1e-5))
        p.op("vector", lambda e: None, inc=ve)

        def bias_cols(dram_vec, n_chunks, dest, dest_col0):
            for g0 in range(0, n_chunks, 8):
                gn = min(8, n_chunks - g0)
                p.dma("sync", brow[0:1, 0:gn * 128],
                      dram_vec[None, g0 * 128:(g0 + gn) * 128],
                      waits=[(ve, ve.n)], inc=ldS)
                ldv = (ldS, ldS.n)
                for ci in range(gn):
                    c = g0 + ci
                    slot = 1024 + (ci % 2) * 512
                    p.op("tensor",
                         (lambda ci_, slot_: lambda e: e.transpose(
                             psB[:, slot_:slot_ + 1],
                             brow[0:1, ci_ * 128:(ci_ + 1) * 128], idf[0:1, 0:1]))(ci, slot),
                         waits=[ldv, (ve, ve.n)], inc=mm)
                    p.op("vector",
                         (lambda c_, slot_: lambda e: e.tensor_copy(
                             out=dest[:, dest_col0 + c_:dest_col0 + c_ + 1],
                             in_=psB[:, slot_:slot_ + 1]))(c, slot),
                         waits=after(mm), inc=ve)
        bias_cols(qkv_b, 24, biasq, 0)
        bias_cols(rq_b, 2, biasq, 24)
        bias_cols(rkv_b, 4, biasq, 26)
        bias_cols(out_b, 8, biaso, 0)
        bias_cols(rout_b, 8, biaso, 8)
        for n in range(2):
            p.mm(psA[:, n * 512:(n + 1) * 512], ones_rb[:],
                 vbrow[0:1, n * 512:(n + 1) * 512], start=True, stop=True,
                 waits=(after(ldG) + [(ve, ve.n)] if n == 0 else []),
                 inc=(mm if n == 1 else None))
        p.op("vector", lambda e: e.tensor_copy(out=vbias[:], in_=psA[:, 0:1024]),
             waits=after(mm), inc=ve)
        consts_done = after(ve, mm)

        acc_read_done = []
        for b in range(BPC):
            tb = b * S
            p.op("vector", lambda e: e.memset(
                kTx.rearrange("p (o c) -> p o c", c=2176)[:, :, 0:128], 0.0),
                waits=acc_read_done)
            p.op("vector", lambda e: e.memset(vX[:, 0:1040], 0.0), inc=ve)
            for fc in range(FC):
                p.dma("scalar", xstg[:], x2T[fc * 128:(fc + 1) * 128, tb:tb + S],
                      waits=acc_read_done + ([(ve, ve.n)] if fc > 0 else (consts_done if b == 0 else [])),
                      inc=ldA)
                p.op("vector",
                     (lambda fc_: lambda e: e.tensor_copy(
                         out=x2qT[:, fc_ * S:(fc_ + 1) * S], in_=xstg[:]))(fc),
                     waits=after(ldA), inc=ve)
            xq_ready = after(ve)

            for part in ("q", "k"):
                for oc in range(FC):
                    buf = oc % 2
                    colbase = oc * 128 if part == "q" else D + oc * 128
                    p.dma("gpsimd", AR1[:, buf * 1024:(buf + 1) * 1024].rearrange(
                              "p (r c) -> p r c", r=FC),
                          qkv_w.rearrange("(r p) c -> p r c", p=128)[:, :, colbase:colbase + 128],
                          waits=[(mm, max(0, mm.n - 1))], inc=ldG)
                    wlv = (ldG, ldG.n)
                    for fc in range(FC):
                        for n in range(4):
                            p.mm(psA[:, n * 512:(n + 1) * 512],
                                 AR1[:, buf * 1024 + fc * 128: buf * 1024 + (fc + 1) * 128],
                                 x2qT[:, fc * S + n * 512: fc * S + (n + 1) * 512],
                                 start=(fc == 0), stop=(fc == FC - 1),
                                 waits=([wlv, (ve, ve.n)] + (xq_ready if oc == 0 and part == "q" else [])
                                        if fc == 0 and n == 0 else []),
                                 inc=(mm if fc == FC - 1 and n == 3 else None))
                    bcol = oc if part == "q" else 8 + oc
                    if part == "q":
                        outap = qT[:, oc * S:(oc + 1) * S]
                    else:
                        outap = kTx.rearrange("p (o c) -> p o c", c=2176)[:, oc, 128:2176]
                    p.op("vector",
                         (lambda outap_, bcol_: lambda e: e.tensor_scalar(
                             out=outap_, in0=psA[:, 0:2048],
                             scalar1=biasq[:, bcol_:bcol_ + 1], scalar2=None,
                             op0=ALU.add))(outap, bcol),
                         waits=after(mm), inc=ve)

            p.dma("gpsimd", AR1[:].rearrange("p (r c) -> p r c", r=FC),
                  qkv_w.rearrange("(r p) c -> p r c", p=128)[:, :, 2 * D:3 * D],
                  waits=[(mm, mm.n)], inc=ldG)
            vwv = (ldG, ldG.n)
            for tc in range(16):
                slot = (tc % 2) * 1024
                for fc in range(FC):
                    for n in range(2):
                        p.mm(psB[:, slot + n * 512: slot + (n + 1) * 512],
                             x2qT[:, fc * S + tc * 128: fc * S + tc * 128 + 128],
                             AR1[:, fc * 1024 + n * 512: fc * 1024 + (n + 1) * 512],
                             start=(fc == 0), stop=(fc == FC - 1),
                             waits=([vwv, (ve, ve.n - 1 if tc >= 2 else 0)]
                                    if fc == 0 and n == 0 else []),
                             inc=(mm if fc == FC - 1 and n == 1 else None))
                vv = vX[:, (tc + 1) * 1040:(tc + 2) * 1040]
                p.op("vector",
                     (lambda vv_: lambda e: e.memset(
                         vv_.rearrange("p (h c) -> p h c", c=65)[:, :, 64:65], 1.0))(vv),
                     waits=after(mm))
                p.op("vector",
                     (lambda vv_, slot_: lambda e: e.scalar_tensor_tensor(
                         out=vv_.rearrange("p (h c) -> p h c", c=65)[:, :, 0:64],
                         in0=psB[:, slot_:slot_ + 1024], scalar=1.0,
                         in1=vbias[:], op0=ALU.mult, op1=ALU.add))(vv, slot),
                     inc=ve)

            p.dma("gpsimd", AR1[:, 2048:4096].rearrange("p (r c) -> p r c", r=FC),
                  rq_w.rearrange("(r p) c -> p r c", p=128),
                  waits=[(mm, mm.n)], inc=ldG)
            p.dma("gpsimd", AR1[:, 4096:8192].rearrange("p (r c) -> p r c", r=FC),
                  rkv_w.rearrange("(r p) c -> p r c", p=128),
                  waits=[(mm, mm.n)], inc=ldG)
            rql = (ldG, ldG.n)
            for oc in range(2):
                for fc in range(FC):
                    for n in range(4):
                        p.mm(psA[:, n * 512:(n + 1) * 512],
                             AR1[:, 2048 + fc * 256 + oc * 128: 2048 + fc * 256 + (oc + 1) * 128],
                             x2qT[:, fc * S + n * 512: fc * S + (n + 1) * 512],
                             start=(fc == 0), stop=(fc == FC - 1),
                             waits=([(ve, ve.n), rql] if fc == 0 and n == 0 else []),
                             inc=(mm if fc == FC - 1 and n == 3 else None))
                p.op("vector",
                     (lambda oc_: lambda e: e.tensor_scalar(
                         out=rqT[:, oc_ * S:(oc_ + 1) * S], in0=psA[:, 0:2048],
                         scalar1=biasq[:, 24 + oc_:25 + oc_], scalar2=None,
                         op0=ALU.add))(oc),
                     waits=after(mm), inc=ve)

            for fc in range(FC):
                p.op("vector",
                     (lambda fc_: lambda e: e.tensor_reduce(
                         out=pooledf[:].rearrange("p (c o) -> p c o", o=1),
                         in_=x2qT[:, fc_ * S:(fc_ + 1) * S].rearrange("p (c k) -> p c k", k=32),
                         axis=AX.X, op=ALU.add))(fc))
                p.op("vector",
                     (lambda fc_: lambda e: e.tensor_scalar_mul(
                         out=pooledbf[:, fc_ * 64:(fc_ + 1) * 64], in0=pooledf[:],
                         scalar1=1.0 / 32.0))(fc),
                     inc=ve)

            for mc in range(4):
                for fc in range(FC):
                    p.mm(psB[:, 1280 + (mc % 2) * 512: 1280 + (mc % 2) * 512 + 64],
                         AR1[:, 4096 + fc * 512 + mc * 128: 4096 + fc * 512 + (mc + 1) * 128],
                         pooledbf[:, fc * 64:(fc + 1) * 64],
                         start=(fc == 0), stop=(fc == FC - 1),
                         waits=([(ve, ve.n), (st2, st2.n)] if fc == 0 else []),
                         inc=(mm if fc == FC - 1 else None))
                p.op("vector",
                     (lambda mc_: lambda e: e.tensor_scalar(
                         out=rkvT[:, mc_ * 64:(mc_ + 1) * 64],
                         in0=psB[:, 1280 + (mc_ % 2) * 512: 1280 + (mc_ % 2) * 512 + 64],
                         scalar1=biasq[:, 26 + mc_:27 + mc_], scalar2=None,
                         op0=ALU.add))(mc),
                     waits=after(mm), inc=st2)
            for lh in range(4):
                mc = 2 + lh // 2
                base = (lh % 2) * 64
                p.op("tensor",
                     (lambda mc_, base_, lh_: lambda e: e.transpose(
                         psA[0:64, 1024 + (lh_ % 2) * 512 + (lh_ // 2) * 64:
                             1024 + (lh_ % 2) * 512 + (lh_ // 2) * 64 + 64].bitcast(BF16)[:, 0:64],
                         rkvT[base_:base_ + 64, mc_ * 64:(mc_ + 1) * 64],
                         idb[base_:base_ + 64, base_:base_ + 64]))(mc, base, lh),
                     waits=(after(st2) if lh == 0 else []), inc=mm)
            for lh in range(4):
                p.op("vector",
                     (lambda lh_: lambda e: e.tensor_copy(
                         out=rvTt[(lh_ % 2) * 64:(lh_ % 2) * 64 + 64, (lh_ // 2) * 64:(lh_ // 2) * 64 + 64],
                         in_=psA[0:64, 1024 + (lh_ % 2) * 512 + (lh_ // 2) * 64:
                                 1024 + (lh_ % 2) * 512 + (lh_ // 2) * 64 + 64].bitcast(BF16)[:, 0:64]))(lh),
                     waits=after(mm), inc=ve)

            loc0 = nrm.n
            for h in range(NH):
                oc, base = h // 2, (h % 2) * 64
                for nb in range(8):
                    it = h * 8 + nb
                    par = it % 2
                    sS = par * 768
                    sV = 1024 + par * 384
                    for kc in range(3):
                        p.mm(psA[:, sS + kc * 256: sS + (kc + 1) * 256],
                             kTx[base:base + 64, oc * 2176 + nb * 256 + kc * 128: oc * 2176 + nb * 256 + kc * 128 + 128],
                             qT[base:base + 64, oc * S + nb * 256: oc * S + (nb + 1) * 256],
                             start=True, stop=True,
                             waits=([(ex, max(0, ex.n - 1)), (nrm, max(0, nrm.n - 1))]
                                    if kc == 0 else []),
                             inc=(mm if kc == 2 else None))
                    p.op("scalar",
                         (lambda sS_, par_: lambda e: e.activation(
                             out=PTl[:, par_ * 768:(par_ + 1) * 768], in_=psA[:, sS_:sS_ + 768],
                             func=AF.Exp, scale=0.125))(sS, par),
                         waits=after(mm), inc=ex)
                    msk = mask0_sb if nb == 0 else maskg_sb
                    p.op("vector",
                         (lambda par_, msk_: lambda e: e.tensor_tensor(
                             out=PTl[:, par_ * 768:(par_ + 1) * 768],
                             in0=PTl[:, par_ * 768:(par_ + 1) * 768],
                             in1=msk_[:], op=ALU.mult))(par, msk),
                         waits=after(ex), inc=ve)
                    for kc in range(3):
                        p.mm(psB[0:65, par * 512: par * 512 + 256],
                             vX[:, (2 * nb + kc) * 1040 + 65 * h: (2 * nb + kc) * 1040 + 65 * h + 65],
                             PTl[:, par * 768 + kc * 256: par * 768 + (kc + 1) * 256],
                             start=(kc == 0), stop=(kc == 2),
                             waits=(after(ve) if kc == 0 else []),
                             inc=(mm if kc == 2 else None))
                    p.op("vector",
                         (lambda par_: lambda e: e.tensor_scalar(
                             out=sumbf[0:1, 0:256], in0=psB[64:65, par_ * 512: par_ * 512 + 256],
                             scalar1=1e-30, scalar2=None, op0=ALU.max))(par),
                         waits=after(mm), inc=ve)
                    p.mm(psB[64:128, 1024 + par * 512: 1024 + par * 512 + 256],
                         ones_rb[0:1, 0:64], sumbf[0:1, 0:256], start=True, stop=True,
                         waits=after(ve), inc=mm)
                    p.op("vector",
                         (lambda par_: lambda e: e.reciprocal(
                             out=recip[:, 0:256],
                             in_=psB[64:128, 1024 + par_ * 512: 1024 + par_ * 512 + 256]))(par),
                         waits=after(mm))
                    p.op("vector",
                         (lambda par_, base_, oc_, nb_: lambda e: e.tensor_tensor(
                             out=qT[base_:base_ + 64, oc_ * S + nb_ * 256: oc_ * S + (nb_ + 1) * 256],
                             in0=psB[0:64, par_ * 512: par_ * 512 + 256],
                             in1=recip[:, 0:256], op=ALU.mult))(par, base, oc, nb),
                         inc=nrm)

            for lh in range(4):
                rqc, rqb = lh // 2, (lh % 2) * 64
                for tc in range(16):
                    slot = 1024 + (tc % 2) * 768
                    p.mm(psA[:, slot:slot + 64],
                         rqT[rqb:rqb + 64, rqc * S + tc * 128: rqc * S + tc * 128 + 128],
                         rkvT[(lh % 2) * 64:(lh % 2) * 64 + 64, (lh // 2) * 64:(lh // 2) * 64 + 64],
                         start=True, stop=True,
                         waits=([(ve, ve.n)] if tc == 0 and lh == 0 else [(vp, max(0, vp.n - 3))]),
                         inc=mm)
                    p.op("vector",
                         (lambda slot_, tc_: lambda e: e.scalar_tensor_tensor(
                             out=lsc[:, 0:64], in0=psA[:, slot_:slot_ + 64], scalar=0.125,
                             in1=latm_sb[:, tc_ * 64:(tc_ + 1) * 64],
                             op0=ALU.mult, op1=ALU.add))(slot, tc),
                         waits=after(mm), inc=vp)
                    p.op("vector", lambda e: e.tensor_reduce(
                        out=lsc[:, 64:65], in_=lsc[:, 0:64], axis=AX.X, op=ALU.max,
                        negate=True), waits=after(vp), inc=vp)
                    p.op("scalar", lambda e: e.activation(
                        out=lsc[:, 0:64], in_=lsc[:, 0:64], func=AF.Exp,
                        bias=lsc[:, 64:65], accum_out=lsc[:, 65:66]),
                        waits=after(vp), inc=ex)
                    p.op("vector", lambda e: e.reciprocal(out=lsc[:, 66:67], in_=lsc[:, 65:66]),
                         waits=after(ex), inc=vp)
                    p.op("vector",
                         (lambda tc_: lambda e: e.tensor_scalar(
                             out=lattn[:], in0=lsc[:, 0:64], scalar1=lsc[:, 66:67],
                             scalar2=latr_sb[:, tc_:tc_ + 1], op0=ALU.mult, op1=ALU.mult))(tc),
                         waits=after(vp) + [(mm, mm.n)], inc=vp)
                    p.op("tensor",
                         (lambda tc_, slot_: lambda e: e.transpose(
                             psB[0:64, 1024 + (tc_ % 2) * 512: 1024 + (tc_ % 2) * 512 + 128].bitcast(BF16)[:, 0:128],
                             lattn[:], idb[:]))(tc, slot),
                         waits=after(vp), inc=mm)
                    p.op("vector",
                         (lambda tc_, lh_: lambda e: e.tensor_copy(
                             out=latT[(lh_ % 2) * 64:(lh_ % 2) * 64 + 64,
                                      (lh_ // 2) * S + tc_ * 128:(lh_ // 2) * S + tc_ * 128 + 128],
                             in_=psB[0:64, 1024 + (tc_ % 2) * 512: 1024 + (tc_ % 2) * 512 + 128].bitcast(BF16)[:, 0:128]))(tc, lh),
                         waits=after(mm), inc=vp)
            lat_done = after(vp, mm)
            for lh in range(4):
                base = (lh % 2) * 64
                chk = lh // 2
                for n in range(4):
                    p.mm(psA[0:64, n * 512:(n + 1) * 512],
                         rvTt[base:base + 64, chk * 64:(chk + 1) * 64],
                         latT[base:base + 64, chk * S + n * 512: chk * S + (n + 1) * 512],
                         start=True, stop=True,
                         waits=((lat_done + [(nrm, nrm.n)] if lh == 0 else [(ve, ve.n)])
                                if n == 0 else []),
                         inc=(mm if n == 3 else None))
                p.op("vector",
                     (lambda lh_, base_, chk_: lambda e: e.tensor_copy(
                         out=loT[base_:base_ + 64, chk_ * S:(chk_ + 1) * S],
                         in_=psA[0:64, 0:2048]))(lh, base, chk),
                     waits=after(mm), inc=ve)

            p.dma("gpsimd", AR1[:, 0:2048].rearrange("p (r c) -> p r c", r=2),
                  rout_w.rearrange("(r p) c -> p r c", p=128),
                  waits=[(mm, mm.n)], inc=ldG)
            routl = (ldG, ldG.n)
            for oc in range(FC):
                for kc in range(2):
                    for n in range(4):
                        p.mm(psA[:, n * 512:(n + 1) * 512],
                             AR1[:, kc * D + oc * 128: kc * D + (oc + 1) * 128],
                             loT[:, kc * S + n * 512: kc * S + (n + 1) * 512],
                             start=(kc == 0), stop=(kc == 1),
                             waits=([routl] + after(ve) + [(nrm, nrm.n)] if kc == 0 and n == 0 and oc == 0
                                    else ([(ve, ve.n)] if kc == 0 and n == 0 else [])),
                             inc=(mm if kc == 1 and n == 3 else None))
                p.op("vector",
                     (lambda oc_: lambda e: e.tensor_scalar(
                         out=remT[:], in0=psA[:, 0:2048],
                         scalar1=biaso[:, 8 + oc_:9 + oc_], scalar2=None, op0=ALU.add))(oc),
                     waits=after(mm), inc=ve)
                for hh in range(2):
                    base = hh * 64
                    for n in range(4):
                        p.mm(psB[0:1, n * 512:(n + 1) * 512],
                             glw[base:base + 64, 0:1],
                             qT[base:base + 64, oc * S + n * 512: oc * S + (n + 1) * 512],
                             start=True, stop=False,
                             waits=(after(ve) + [(nrm, nrm.n)] if n == 0 else []))
                    for n in range(4):
                        p.mm(psB[0:1, n * 512:(n + 1) * 512],
                             glw[base:base + 64, 1:2],
                             remT[base:base + 64, n * 512:(n + 1) * 512],
                             start=False, stop=True,
                             inc=(mm if n == 3 else None))
                    p.op("scalar", lambda e: e.activation(
                        out=grow[0:1, :], in_=psB[0:1, 0:2048], func=AF.Sigmoid),
                        waits=after(mm), inc=ex)
                    for n in range(4):
                        p.mm(psB[64:128, n * 512:(n + 1) * 512],
                             ones_rb[0:1, 0:64], grow[0:1, n * 512:(n + 1) * 512],
                             start=True, stop=True,
                             waits=(after(ex) if n == 0 else []),
                             inc=(mm if n == 3 else None))
                    p.op("vector",
                         (lambda base_, oc_: lambda e: e.tensor_tensor(
                             out=sc1[base_:base_ + 64, :],
                             in0=qT[base_:base_ + 64, oc_ * S:(oc_ + 1) * S],
                             in1=remT[base_:base_ + 64, :], op=ALU.subtract))(base, oc),
                         waits=after(mm))
                    p.op("vector",
                         (lambda base_: lambda e: e.tensor_tensor(
                             out=sc1[base_:base_ + 64, :], in0=sc1[base_:base_ + 64, :],
                             in1=psB[64:128, 0:2048], op=ALU.mult))(base))
                    p.op("vector",
                         (lambda base_, oc_: lambda e: e.tensor_tensor(
                             out=qT[base_:base_ + 64, oc_ * S:(oc_ + 1) * S],
                             in0=sc1[base_:base_ + 64, :], in1=remT[base_:base_ + 64, :],
                             op=ALU.add))(base, oc),
                         inc=nrm)

            att_done = after(nrm)
            for oc in range(FC):
                buf = oc % 2
                p.dma("gpsimd", AR1[:, buf * 1024:(buf + 1) * 1024].rearrange(
                          "p (r c) -> p r c", r=FC),
                      out_w.rearrange("(r p) c -> p r c", p=128)[:, :, oc * 128:(oc + 1) * 128],
                      waits=[(mm, max(0, mm.n - 1))] + (att_done if oc <= 1 else []), inc=ldG)
                owv = (ldG, ldG.n)
                for dc in range(FC):
                    for n in range(4):
                        w = []
                        if dc == 0 and n == 0:
                            w = att_done + [owv] + ([(rq_, rq_.n)] if oc > 0 else [])
                        p.mm(psA[:, n * 512:(n + 1) * 512],
                             AR1[:, buf * 1024 + dc * 128: buf * 1024 + (dc + 1) * 128],
                             qT[:, dc * S + n * 512: dc * S + (n + 1) * 512],
                             start=(dc == 0), stop=(dc == FC - 1),
                             waits=w, inc=(mm if dc == FC - 1 and n == 3 else None))
                p.dma("scalar", xstg[:], x2T[oc * 128:(oc + 1) * 128, tb:tb + S],
                      waits=[(rq_, rq_.n)] + (att_done if oc == 0 else []), inc=ldA)
                p.op("vector",
                     (lambda oc_: lambda e: e.tensor_scalar(
                         out=accf[:, oc_ * S:(oc_ + 1) * S], in0=psA[:, 0:2048],
                         scalar1=biaso[:, oc_:oc_ + 1], scalar2=None, op0=ALU.add))(oc),
                     waits=after(mm, ldA))
                p.op("vector",
                     (lambda oc_: lambda e: e.tensor_tensor(
                         out=accf[:, oc_ * S:(oc_ + 1) * S], in0=xstg[:],
                         in1=accf[:, oc_ * S:(oc_ + 1) * S], op=ALU.add))(oc),
                     inc=rq_)
            r1_done = after(rq_)
            st2_0 = st2.n
            for oc in range(FC):
                par = (oc % 2) * 2048
                p.op("vector",
                     (lambda oc_, par_: lambda e: e.tensor_copy(
                         out=lnbf[:, par_:par_ + 2048], in_=accf[:, oc_ * S:(oc_ + 1) * S]))(oc, par),
                     waits=r1_done + [(st2, st2_0 + 2 * max(0, oc - 1))], inc=ve)
                p.op("scalar",
                     (lambda par_: lambda e: e.activation(
                         out=lnbf[:, 4096 + par_: 4096 + par_ + 2048], in_=lnbf[:, par_:par_ + 2048],
                         func=AF.Square))(par),
                     waits=after(ve) + [(st2, st2_0 + 2 * max(0, oc - 1))], inc=ex)
                for n in range(4):
                    p.mm(psB[0:1, n * 512:(n + 1) * 512], ones_c[:],
                         lnbf[:, 4096 + par + n * 512: 4096 + par + (n + 1) * 512],
                         start=(oc == 0), stop=(oc == FC - 1),
                         waits=(after(ex) if n == 0 else []),
                         inc=(st2 if n == 3 else None))
                for n in range(4):
                    p.mm(psA[0:1, n * 512:(n + 1) * 512], ones_c[:],
                         lnbf[:, par + n * 512: par + (n + 1) * 512],
                         start=(oc == 0), stop=(oc == FC - 1),
                         waits=(after(ve) if n == 0 else []),
                         inc=(st2 if n == 3 else None))
            p.op("vector", lambda e: e.tensor_copy(out=r1, in_=psA[0:1, 0:2048]),
                 waits=[(st2, st2_0 + 2 * FC)])
            p.op("vector", lambda e: e.tensor_copy(out=r2, in_=psB[0:1, 0:2048]))
            p.op("vector", lambda e: e.tensor_scalar_mul(out=r3, in0=r1, scalar1=1.0 / D))
            p.op("vector", lambda e: e.tensor_tensor(out=r1, in0=r3, in1=r3, op=ALU.mult))
            p.op("vector", lambda e: e.tensor_scalar_mul(out=r2, in0=r2, scalar1=1.0 / D))
            p.op("vector", lambda e: e.tensor_tensor(out=r2, in0=r2, in1=r1, op=ALU.subtract))
            p.op("vector", lambda e: e.tensor_copy(out=lnf[0:1, 0:2048], in_=r2), inc=ve)
            p.op("scalar", lambda e: e.activation(out=r1, in_=r2, func=AF.Sqrt,
                                                  bias=epst[0:1, 0:1]),
                 waits=after(ve), inc=ex)
            p.op("vector", lambda e: e.reciprocal(out=r2, in_=r1),
                 waits=after(ex))
            p.op("vector", lambda e: e.tensor_copy(out=lnf[0:1, 0:2048], in_=r2), inc=ve)
            rows_done = after(ve)
            for n in range(4):
                p.mm(psA[:, n * 512:(n + 1) * 512], ones_rf[:],
                     r3[0:1, n * 512:(n + 1) * 512], start=True, stop=True,
                     waits=(rows_done if n == 0 else []))
            for n in range(4):
                p.mm(psB[:, n * 512:(n + 1) * 512], ones_rf[:],
                     r2[0:1, n * 512:(n + 1) * 512], start=True, stop=True,
                     inc=(mm if n == 3 else None))
            for oc in range(FC):
                par = (oc % 2) * 2048
                sub = lnf[:, 4096 + par: 4096 + par + 2048]
                outv = lnf[:, par: par + 2048]
                p.op("vector",
                     (lambda oc_, sub_: lambda e: e.tensor_tensor(
                         out=sub_, in0=accf[:, oc_ * S:(oc_ + 1) * S], in1=psA[:, 0:2048],
                         op=ALU.subtract))(oc, sub),
                     waits=after(mm) + [(so, max(0, so.n - 16))])
                p.op("vector",
                     (lambda sub_, outv_: lambda e: e.tensor_tensor(
                         out=outv_, in0=sub_, in1=psB[:, 0:2048], op=ALU.mult))(sub, outv),
                     inc=rq_)
                p.dma("sync", x3T[oc * 128:(oc + 1) * 128, tb:tb + S],
                      outv, waits=after(rq_), inc=so)
            acc_read_done = after(rq_) + after(so)

        return after(so)


def c_io(nc):
    io = {}
    io["qkv_w"] = nc.dram_tensor("qkv_w", [D, 3 * D], F32, kind="ExternalInput").ap()
    io["qkv_b"] = nc.dram_tensor("qkv_b", [3 * D], F32, kind="ExternalInput").ap()
    io["rq_w"] = nc.dram_tensor("rq_w", [D, 256], F32, kind="ExternalInput").ap()
    io["rq_b"] = nc.dram_tensor("rq_b", [256], F32, kind="ExternalInput").ap()
    io["rkv_w"] = nc.dram_tensor("rkv_w", [D, 512], F32, kind="ExternalInput").ap()
    io["rkv_b"] = nc.dram_tensor("rkv_b", [512], F32, kind="ExternalInput").ap()
    io["rout_w"] = nc.dram_tensor("rout_w", [256, D], F32, kind="ExternalInput").ap()
    io["rout_b"] = nc.dram_tensor("rout_b", [D], F32, kind="ExternalInput").ap()
    io["out_w"] = nc.dram_tensor("out_w", [D, D], F32, kind="ExternalInput").ap()
    io["out_b"] = nc.dram_tensor("out_b", [D], F32, kind="ExternalInput").ap()
    io["gl_w"] = nc.dram_tensor("gl_w", [HD, 1], F32, kind="ExternalInput").ap()
    io["gr_w"] = nc.dram_tensor("gr_w", [HD, 1], F32, kind="ExternalInput").ap()
    io["maskg"] = nc.dram_tensor("maskg", [128, 768], BF16, kind="ExternalInput").ap()
    io["mask0"] = nc.dram_tensor("mask0", [128, 768], BF16, kind="ExternalInput").ap()
    io["latmask"] = nc.dram_tensor("latmask", [128, 1024], F32, kind="ExternalInput").ap()
    io["latrow"] = nc.dram_tensor("latrow", [128, 16], F32, kind="ExternalInput").ap()
    return io


def build_c():
    nc = bass.Bass("TRN2", target_bir_lowering=False)
    io = c_io(nc)
    io["x2T"] = nc.dram_tensor("x2T", [D, T], F32, kind="ExternalInput").ap()
    io["ident_bf"] = nc.dram_tensor("ident_bf", [128, 128], BF16, kind="ExternalInput").ap()
    io["ident_f32"] = nc.dram_tensor("ident_f32", [128, 128], F32, kind="ExternalInput").ap()
    io["x3T"] = nc.dram_tensor("x3T", [D, T], F32, kind="ExternalOutput").ap()
    from contextlib import ExitStack
    ctx = ExitStack()
    with ctx:
        big = ctx.enter_context(nc.sbuf_tensor("big", [128, ARENA_WORDS], F32))
        psA = ctx.enter_context(nc.psum_tensor("psA", [128, 2048], F32))
        psB = ctx.enter_context(nc.psum_tensor("psB", [128, 2048], F32))
        p = Prog(nc, ctx)
        sm = SemPool(p)
        ex = phase_c(p, Arena(big), psA, psB, sm, io)
        p.op("sync", lambda e: None, waits=ex)
        p.finalize()
    return nc



FFD = 2816
FFDC = FFD // 128


def phase_d(p, A, psA, psB, sm, io):
    if True:
        x3T = io["x3T"]; w_gate = io["w_gate"]; w_up = io["w_up"]
        w_down = io["w_down"]; identb = io["ident_bf"]; identf = io["ident_f32"]
        OUT = io["out4"]
        A.reset()
        wgT = A(128, FC * FFD, BF16)
        wuT = A(128, FC * FFD, BF16)
        wdT = A(128, FFDC * D, BF16)
        wstg = A(128, FFD, F32)
        ybuf = A(128, FFD, F32)
        cbuf = A(128, FFD, F32)
        scl = A(128, FFD // 16, F32)
        rcp = A(128, FFD // 16, F32)
        sgn = A(128, FFD, BF16)
        qbf = A(128, FFD, BF16)
        hbuf = A(128, FFD, F32)
        hqT = A(128, FFD, BF16)
        x3t = A(128, D, F32)
        xqT = A(128, FC * 128, BF16)
        accb = A(128, D, F32)
        col1 = A(128, 4, F32)
        idb = A(128, 128, BF16)
        idf = A(128, 128, F32)
        epst = A(1, 1, F32)
        epsc = A(128, 1, F32)

        ldS = sm.H[0]; ldG = sm.G[0]; mm = sm.E[0]; ve = sm.E[1]
        ex = sm.E[2]; so = sm.H[3]; qd = sm.E[3]

        p.dma("sync", idb[:], identb, inc=ldS)
        p.dma("sync", idf[:], identf, inc=ldS)
        p.op("vector", lambda e: e.memset(epst[:], 1e-5))
        p.op("vector", lambda e: e.memset(epsc[:], 1e-5))

        def quant(src_ap, W, out_ap, waits0):
            nb = W // 16
            bv = src_ap.rearrange("p (n k) -> p n k", k=16)
            p.op("vector", lambda e: e.tensor_reduce(
                out=scl[:, 0:nb].rearrange("p (n o) -> p n o", o=1), in_=bv,
                axis=AX.X, op=ALU.max, apply_absolute_value=True), waits=list(waits0))
            p.op("vector", lambda e: e.tensor_scalar(
                out=scl[:, 0:nb], in0=scl[:, 0:nb], scalar1=1.0 / 6.0, scalar2=1e-12,
                op0=ALU.mult, op1=ALU.max))
            p.op("vector", lambda e: e.reciprocal(out=rcp[:, 0:nb], in_=scl[:, 0:nb]))
            p.op("scalar", (lambda s_: lambda e: e.activation(
                out=ybuf[:, 0:W], in_=s_, func=AF.Abs))(src_ap),
                waits=list(waits0), inc=ex)
            p.op("scalar", (lambda s_: lambda e: e.sign(out=sgn[:, 0:W], in_=s_))(src_ap),
                 waits=list(waits0), inc=ex)
            rb = rcp[:, 0:nb].rearrange("p (n o) -> p n o", o=1).broadcast_to([128, nb, 16])
            p.op("vector", lambda e: e.tensor_tensor(
                out=ybuf[:, 0:W].rearrange("p (n k) -> p n k", k=16),
                in0=ybuf[:, 0:W].rearrange("p (n k) -> p n k", k=16), in1=rb,
                op=ALU.mult), waits=after(ex))
            p.op("vector", lambda e: e.tensor_scalar(
                out=cbuf[:, 0:W], in0=ybuf[:, 0:W], scalar1=0.25, scalar2=None,
                op0=ALU.is_gt))
            for u in (0.75, 1.25, 1.75, 2.5, 2.5, 3.5, 3.5, 5.0, 5.0, 5.0, 5.0):
                p.op("vector", (lambda u_: lambda e: e.scalar_tensor_tensor(
                    out=cbuf[:, 0:W], in0=ybuf[:, 0:W], scalar=u_, in1=cbuf[:, 0:W],
                    op0=ALU.is_gt, op1=ALU.add))(u))
            sb = scl[:, 0:nb].rearrange("p (n o) -> p n o", o=1).broadcast_to([128, nb, 16])
            p.op("vector", lambda e: e.scalar_tensor_tensor(
                out=cbuf[:, 0:W].rearrange("p (n k) -> p n k", k=16),
                in0=cbuf[:, 0:W].rearrange("p (n k) -> p n k", k=16), scalar=0.5, in1=sb,
                op0=ALU.mult, op1=ALU.mult))
            p.op("vector", (lambda o_: lambda e: e.tensor_tensor(
                out=o_, in0=cbuf[:, 0:W], in1=sgn[:, 0:W], op=ALU.mult))(out_ap),
                waits=after(ex), inc=ve)

        def prep_w(wdram, R, C, dest, destW):
            RC, CC = R // 128, C // 128
            for rc in range(RC):
                p.dma("sync", wstg[:, 0:C], wdram[rc * 128:(rc + 1) * 128, :],
                      waits=[(ve, ve.n), (ex, ex.n)], inc=ldS)
                quant(wstg[:, 0:C], C, qbf[:, 0:C], [(ldS, ldS.n), (ve, ve.n)])
                for cc in range(CC):
                    p.op("tensor",
                         (lambda rc_, cc_: lambda e: e.transpose(
                             psA[:, (cc_ % 8) * 256:(cc_ % 8) * 256 + 64].bitcast(BF16)[:, 0:128],
                             qbf[:, cc_ * 128:(cc_ + 1) * 128], idb[:]))(rc, cc),
                         waits=(after(ve) if cc == 0 else []) +
                               ([(ve, ve.n)] if cc % 8 == 0 and cc > 0 else []),
                         inc=(mm if cc % 8 == 7 or cc == CC - 1 else None))
                    if cc % 8 == 7 or cc == CC - 1:
                        lo = (cc // 8) * 8
                        ln = cc - lo + 1
                        dv = dest[:, 0:CC * destW].rearrange(
                            "p (cc r) -> p cc r", cc=CC)[:, lo:lo + ln, rc * 128:(rc + 1) * 128]
                        p.op("vector",
                             (lambda dv_, ln_: lambda e: e.tensor_copy(
                                 out=dv_, in_=psA[:, 0:2048].bitcast(BF16).rearrange(
                                     "p (cc x) -> p cc x", x=512)[:, 0:ln_, 0:128]))(dv, ln),
                             waits=after(mm), inc=ve)

        prep_w(w_gate, FFD, D, wgT, FFD)
        prep_w(w_up, FFD, D, wuT, FFD)
        prep_w(w_down, D, FFD, wdT, D)
        wq_done = after(ve, mm)

        FFB = [(0, 1024), (1024, 1024), (2048, 768)]
        for tc in range(32):
            p.dma("scalar", wstg[:, 0:1024].rearrange("p (r c) -> p r c", c=128),
                  x3T[:, tc * 128:(tc + 1) * 128].rearrange("(r p) c -> p r c", p=128),
                  waits=wq_done + [(qd, qd.n)], inc=ldG)
            for fc in range(FC):
                p.op("tensor",
                     (lambda fc_: lambda e: e.transpose(
                         psB[:, 1024 + fc_ * 128: 1024 + (fc_ + 1) * 128],
                         wstg[:, fc_ * 128:(fc_ + 1) * 128], idf[:]))(fc),
                     waits=([(ldG, ldG.n), (ve, ve.n), (qd, qd.n)] if fc == 0 else []),
                     inc=(mm if fc == FC - 1 else None))
            p.op("vector", lambda e: e.tensor_copy(out=x3t[:], in_=psB[:, 1024:2048]),
                 waits=after(mm), inc=qd)
            quant(x3t[:], D, qbf[:, 0:D], [(qd, qd.n)])
            for fc in range(FC):
                p.op("tensor",
                     (lambda fc_: lambda e: e.transpose(
                         psB[:, (fc_ % 8) * 256:(fc_ % 8) * 256 + 64].bitcast(BF16)[:, 0:128],
                         qbf[:, fc_ * 128:(fc_ + 1) * 128], idb[:]))(fc),
                     waits=(after(ve) if fc == 0 else []),
                     inc=(mm if fc == FC - 1 else None))
            p.op("vector", lambda e: e.tensor_copy(
                out=xqT[:].rearrange("p (cc r) -> p cc r", cc=FC),
                in_=psB[:, 0:2048].bitcast(BF16).rearrange(
                    "p (cc x) -> p cc x", x=512)[:, 0:FC, 0:128]),
                waits=after(mm), inc=qd)
            for (f0, fl) in FFB:
                nn = (fl + 511) // 512
                for fc in range(FC):
                    for n in range(nn):
                        nl = min(512, fl - n * 512)
                        p.mm(psA[:, n * 512:n * 512 + nl],
                             xqT[:, fc * 128:(fc + 1) * 128],
                             wgT[:, fc * FFD + f0 + n * 512: fc * FFD + f0 + n * 512 + nl],
                             start=(fc == 0), stop=(fc == FC - 1),
                             waits=([(qd, qd.n), (ex, ex.n)] if fc == 0 and n == 0 else []),
                             inc=(mm if fc == FC - 1 and n == nn - 1 else None))
                        p.mm(psA[:, 1024 + n * 512: 1024 + n * 512 + nl],
                             xqT[:, fc * 128:(fc + 1) * 128],
                             wuT[:, fc * FFD + f0 + n * 512: fc * FFD + f0 + n * 512 + nl],
                             start=(fc == 0), stop=(fc == FC - 1),
                             inc=(mm if fc == FC - 1 and n == nn - 1 else None))
                p.op("scalar",
                     (lambda f0_, fl_: lambda e: e.activation(
                         out=hbuf[:, f0_:f0_ + fl_], in_=psA[:, 0:fl_],
                         func=AF.Silu))(f0, fl),
                     waits=after(mm) + [(so, so.n)], inc=ex)
                p.op("vector",
                     (lambda f0_, fl_: lambda e: e.tensor_tensor(
                         out=hbuf[:, f0_:f0_ + fl_], in0=hbuf[:, f0_:f0_ + fl_],
                         in1=psA[:, 1024:1024 + fl_], op=ALU.mult))(f0, fl),
                     waits=after(ex), inc=qd)
            quant(hbuf[:], FFD, qbf[:, 0:FFD], [(qd, qd.n)])
            for cc in range(FFDC):
                p.op("tensor",
                     (lambda cc_: lambda e: e.transpose(
                         psB[:, (cc_ % 8) * 256:(cc_ % 8) * 256 + 64].bitcast(BF16)[:, 0:128],
                         qbf[:, cc_ * 128:(cc_ + 1) * 128], idb[:]))(cc),
                     waits=(after(ve) if cc == 0 else []) +
                           ([(ve, ve.n)] if cc == 8 or cc == 16 else []),
                     inc=(mm if cc in (7, 15, FFDC - 1) else None))
                if cc == 7 or cc == 15 or cc == FFDC - 1:
                    lo = (cc // 8) * 8
                    ln = cc - lo + 1
                    p.op("vector",
                         (lambda lo_, ln_: lambda e: e.tensor_copy(
                             out=hqT[:].rearrange("p (cc r) -> p cc r", cc=FFDC)[:, lo_:lo_ + ln_, :],
                             in_=psB[:, 0:2048].bitcast(BF16).rearrange(
                                 "p (cc x) -> p cc x", x=512)[:, 0:ln_, 0:128]))(lo, ln),
                         waits=after(mm), inc=ve)
            hq_done = after(ve)
            for cc in range(FFDC):
                for n in range(2):
                    p.mm(psA[:, n * 512:(n + 1) * 512],
                         hqT[:, cc * 128:(cc + 1) * 128],
                         wdT[:, cc * D + n * 512: cc * D + (n + 1) * 512],
                         start=(cc == 0), stop=(cc == FFDC - 1),
                         waits=(hq_done + [(qd, qd.n)] if cc == 0 and n == 0 else []),
                         inc=(mm if cc == FFDC - 1 and n == 1 else None))
            p.op("vector", lambda e: e.tensor_tensor(
                out=accb[:], in0=psA[:, 0:1024], in1=x3t[:], op=ALU.add),
                waits=after(mm))
            p.op("vector", lambda e: e.tensor_reduce(
                out=col1[:, 0:1], in_=accb[:], axis=AX.X, op=ALU.add))
            p.op("vector", lambda e: e.tensor_scalar_mul(
                out=col1[:, 1:2], in0=col1[:, 0:1], scalar1=1.0 / D))
            p.op("vector", lambda e: e.tensor_scalar(
                out=accb[:], in0=accb[:], scalar1=col1[:, 1:2], scalar2=None,
                op0=ALU.subtract), inc=ve)
            p.op("scalar", lambda e: e.activation(
                out=hbuf[:, 0:D], in_=accb[:], func=AF.Square, accum_out=col1[:, 2:3]),
                waits=after(ve), inc=ex)
            p.op("scalar", lambda e: e.activation(
                out=col1[:, 3:4], in_=col1[:, 2:3], func=AF.Sqrt, scale=1.0 / D,
                bias=epsc[:, 0:1]), inc=ex)
            p.op("vector", lambda e: e.reciprocal(out=col1[:, 2:3], in_=col1[:, 3:4]),
                 waits=after(ex))
            p.op("vector", lambda e: e.tensor_scalar(
                     out=hbuf[:, 1024:2048], in0=accb[:], scalar1=col1[:, 2:3],
                     scalar2=None, op0=ALU.mult),
                 waits=[(so, so.n)], inc=qd)
            p.dma("sync", OUT[tc * 128:(tc + 1) * 128, :], hbuf[:, 1024:2048],
                  waits=after(qd), inc=so)

        return after(so)


def build_d():
    nc = bass.Bass("TRN2", target_bir_lowering=False)
    io = {}
    io["x3T"] = nc.dram_tensor("x3T", [D, T], F32, kind="ExternalInput").ap()
    io["w_gate"] = nc.dram_tensor("w_gate", [FFD, D], F32, kind="ExternalInput").ap()
    io["w_up"] = nc.dram_tensor("w_up", [FFD, D], F32, kind="ExternalInput").ap()
    io["w_down"] = nc.dram_tensor("w_down", [D, FFD], F32, kind="ExternalInput").ap()
    io["ident_bf"] = nc.dram_tensor("ident_bf", [128, 128], BF16, kind="ExternalInput").ap()
    io["ident_f32"] = nc.dram_tensor("ident_f32", [128, 128], F32, kind="ExternalInput").ap()
    io["out4"] = nc.dram_tensor("out4", [T, D], F32, kind="ExternalOutput").ap()
    from contextlib import ExitStack
    ctx = ExitStack()
    with ctx:
        big = ctx.enter_context(nc.sbuf_tensor("big", [128, ARENA_WORDS], F32))
        psA = ctx.enter_context(nc.psum_tensor("psA", [128, 2048], F32))
        psB = ctx.enter_context(nc.psum_tensor("psB", [128, 2048], F32))
        p = Prog(nc, ctx)
        sm = SemPool(p)
        ex = phase_d(p, Arena(big), psA, psB, sm, io)
        p.op("sync", lambda e: None, waits=ex)
        p.finalize()
    return nc


# version 08
# speedup vs baseline: 1.1536x; 1.1536x over previous
import numpy as np
import ml_dtypes
import concourse.bass as bass
import concourse.mybir as mybir
from concourse.bass_utils import run_bass_kernel_spmd

F32 = mybir.dt.float32
BF16 = mybir.dt.bfloat16
AF = mybir.ActivationFunctionType
ALU = mybir.AluOpType
AX = mybir.AxisListType

NCORES = 8
S = 2048
BPC = 2
T = S * BPC
D = 1024
NH = 16
HD = 64
DROPE = 16
FC = D // 128
LAT = 256

BF = ml_dtypes.bfloat16



class Sem:
    def __init__(self, handle):
        self.h = handle
        self.n = 0


class Prog:
    ENGINES = ("sync", "gpsimd", "scalar", "vector", "tensor")

    CHAINED = ("vector", "scalar", "gpsimd")

    def __init__(self, nc, ctx):
        self.nc = nc
        self.ctx = ctx
        self.ops = {e: [] for e in self.ENGINES}
        self._semcount = 0
        self.chain = {e: self.sem(f"chain_{e}") for e in self.CHAINED}
        self.last_evt = {e: None for e in self.CHAINED}

    def sem(self, name=None):
        self._semcount += 1
        h = self.ctx.enter_context(self.nc.semaphore(name or f"s{self._semcount}"))
        return Sem(h)

    def op(self, engine, make, waits=(), inc=None, inc_amt=1, _chain=True):
        waits = list(waits)
        if _chain and engine in self.CHAINED:
            if self.last_evt[engine] is not None:
                waits.append(self.last_evt[engine])
            if inc is None:
                inc = self.chain[engine]
                inc_amt = 1
        wl = [(s.h, v) for (s, v) in waits if v > 0]
        il = None
        if inc is not None:
            inc.n += inc_amt
            il = (inc.h, inc_amt)
            if _chain and engine in self.CHAINED:
                self.last_evt[engine] = (inc, inc.n)
        def fn(eng):
            for h, v in wl:
                eng.wait_ge(h, v)
            inst = make(eng)
            if il is not None:
                if inst is None:
                    inst = eng.nop()
                inst.then_inc(*il)
        self.ops[engine].append(fn)

    def dma(self, queue, out, in_, waits=(), inc=None, **kw):
        waits = list(waits)
        if inc is not None and inc.n > 0:
            waits.append((inc, inc.n))
        self.op(queue, lambda e: e.dma_start(out=out, in_=in_, **kw),
                waits=waits, inc=inc, inc_amt=16, _chain=False)

    def dmaT(self, queue, out, in_, waits=(), inc=None):
        waits = list(waits)
        if inc is not None and inc.n > 0:
            waits.append((inc, inc.n))
        self.op(queue, lambda e: e.dma_start_transpose(out=out, in_=in_),
                waits=waits, inc=inc, inc_amt=16, _chain=False)

    def mm(self, out, lhsT, rhs, start, stop, waits=(), inc=None):
        self.op("tensor",
                lambda e: e.matmul(out, lhsT, rhs, start=start, stop=stop),
                waits=waits, inc=inc)

    def finalize(self):
        block = self.ctx.enter_context(self.nc.Block())
        for ename in self.ENGINES:
            ops = self.ops[ename]
            if not ops:
                continue
            section = getattr(block, ename)
            def make_body(ops_):
                def body(eng):
                    for fn in ops_:
                        fn(eng)
                return body
            section(make_body(list(ops)))


def after(*sems):
    return [(s, s.n) for s in sems]


ARENA_WORDS = 53200


class Arena:

    def __init__(self, big):
        self.big = big
        self.off = 0

    def reset(self):
        self.off = 0

    def __call__(self, rows, cols, dtype):
        if dtype == F32:
            w = cols
        elif dtype == BF16:
            w = (cols + 1) // 2
        else:
            raise ValueError(dtype)
        off = self.off
        self.off = off + w
        assert self.off <= ARENA_WORDS, f"arena overflow: {self.off} words"
        ap = self.big[0:rows, off:off + w]
        if dtype == BF16:
            ap = ap.bitcast(BF16)[:, 0:cols]
        return ap


class SemPool:

    def __init__(self, p):
        self.E = [p.sem(f"E{i}") for i in range(9)]
        self.H = [p.sem(f"H{i}") for i in range(6)]
        self.G = [p.sem(f"G{i}") for i in range(18)]


def barrier(p, pool, skip=()):
    waits = [(s, s.n) for s in (pool.E + pool.H + pool.G +
                                list(p.chain.values()))
             if s.n > 0 and s not in skip]
    for e in Prog.ENGINES:
        p.op(e, lambda _: None, waits=waits, _chain=False)



def host_consts_a():
    inv = (1.0 / 10000.0 ** (np.arange(0, HD, 2, dtype=np.float64) / HD))
    CT = np.ones((128, FC), np.float32)
    ST = np.zeros((128, FC), np.float32)
    for oc in range(FC):
        for blk in range(2):
            h = 2 * oc + blk
            for r in range(16):
                ang = h * inv[r // 2]
                CT[blk * 64 + r, oc] = np.cos(ang)
                ST[blk * 64 + r, oc] = np.sin(ang) * (-1.0 if r % 2 == 0 else 1.0)
    return {
        "ct": CT, "st": ST,
        "ident_bf": np.eye(128, dtype=BF),
    }



def phase_a1(p, A, psA, psB, sm, io):
    if True:
        x0 = io["x0"]; wq = io["wq"]; wk_a = io["wk_a"]; wk_b = io["wk_b"]
        wv_a = io["wv_a"]; wv_b = io["wv_b"]
        ct_in = io["ct"]; st_in = io["st"]; ident = io["ident_bf"]
        qTr = io["qTr"]; kTr = io["kTr"]; vext = io["vext"]; x0bf = io["x0bf"]
        A.reset()
        wqT = A(128, FC * D, BF16)
        wkaT = A(128, FC * LAT, BF16)
        wkbT = A(128, 2 * D, BF16)
        wvaT = A(128, FC * LAT, BF16)
        wvbT = A(128, 2 * D, BF16)
        wnat = A(128, FC * D, BF16)
        id_sb = A(128, 128, BF16)
        ct_sb = A(128, FC, F32)
        st_sb = A(128, FC, F32)
        x0T = A(128, FC * S, BF16)
        qT = A(128, FC * S, BF16)
        kT = A(128, FC * S, BF16)
        klatT = A(128, 2 * S, BF16)
        vlatT = A(128, 2 * S, BF16)
        rs1 = A(128, S, F32)
        rs2 = A(128, S, F32)
        vstg = A(128, 2 * NH * 65, BF16)
        PS = [psA, psB]
        ld = sm.G[0]; ldC = sm.H[0]; wtr = sm.E[7]; wev = sm.E[8]
        xb = sm.G[1]; xt = sm.H[1]; mm = sm.E[0]; ve = sm.E[1]
        vd = sm.H[2]; qs = sm.H[3]

        p.dma("sync", id_sb[:], ident, inc=ldC)
        p.dma("sync", ct_sb[:], ct_in, inc=ldC)
        p.dma("sync", st_sb[:], st_in, inc=ldC)
        for i in range(4):
            sl = slice(i * (T // 4), (i + 1) * (T // 4))
            p.dma("gpsimd", x0bf[sl, :], x0[sl, :], inc=xb)

        def prep_weight(wdram, R, C, dest):
            RC, CC = R // 128, C // 128
            nat_view = wnat[:, 0:RC * C].rearrange("p (r c) -> p r c", r=RC)
            src = wdram.rearrange("(r p) c -> p r c", p=128)
            p.dma("gpsimd", nat_view, src, waits=after(wev), inc=ld)
            ldv = (ld, ld.n)
            ldcv = (ldC, ldC.n)
            for rc in range(RC):
                pst = PS[rc % 2][:, 0:512].bitcast(BF16)
                for cc in range(CC):
                    p.op("tensor",
                         (lambda rc_, cc_, pst_: lambda e: e.transpose(
                             pst_[:, cc_ * 128:(cc_ + 1) * 128],
                             wnat[:, rc_ * C + cc_ * 128: rc_ * C + (cc_ + 1) * 128],
                             id_sb[:]))(rc, cc, pst),
                         waits=[ldv, ldcv, (wev, wev.n)],
                         inc=(wtr if cc == CC - 1 else None))
                dv = dest[:, 0:CC * R].rearrange(
                    "p (cc r) -> p cc r", cc=CC)[:, :, rc * 128:(rc + 1) * 128]
                p.op("vector",
                     (lambda dv_, pst_, CC_: lambda e: e.tensor_copy(
                         out=dv_, in_=pst_[:, 0:CC_ * 128]))(dv, pst, CC),
                     waits=after(wtr), inc=wev)

        prep_weight(wq, D, D, wqT)
        prep_weight(wk_a, LAT, D, wkaT)
        prep_weight(wk_b, D, LAT, wkbT)
        prep_weight(wv_a, LAT, D, wvaT)
        prep_weight(wv_b, D, LAT, wvbT)
        wev_all = after(wev)
        ld_all = after(ld)
        ldc_all = after(ldC)

        shuf_mask = []
        for j in range(16):
            shuf_mask += [2 * j + 1, 2 * j]

        x0t_free = []
        store_guard = []

        for b in range(BPC):
            tb = b * S
            for fc in range(FC):
                p.dmaT("scalar", x0T[:, fc * S:(fc + 1) * S],
                       x0bf[tb:tb + S, fc * 128:(fc + 1) * 128],
                       waits=after(xb) + x0t_free, inc=xt)
            xt_all = after(xt)

            for lc in range(2):
                for fc in range(FC):
                    for n in range(4):
                        p.mm(psA[:, n * 512:(n + 1) * 512],
                             wkaT[:, fc * LAT + lc * 128: fc * LAT + (lc + 1) * 128],
                             x0T[:, fc * S + n * 512: fc * S + (n + 1) * 512],
                             start=(fc == 0), stop=(fc == FC - 1),
                             waits=(wev_all + ld_all + ldc_all + xt_all + after(ve)
                                    if fc == 0 and n == 0 else []),
                             inc=(mm if fc == FC - 1 and n == 3 else None))
                p.op("vector",
                     (lambda lc_: lambda e: e.tensor_copy(
                         out=klatT[:, lc_ * S:(lc_ + 1) * S], in_=psA[:, 0:2048]))(lc),
                     waits=after(mm), inc=ve)
                for fc in range(FC):
                    for n in range(4):
                        p.mm(psB[:, n * 512:(n + 1) * 512],
                             wvaT[:, fc * LAT + lc * 128: fc * LAT + (lc + 1) * 128],
                             x0T[:, fc * S + n * 512: fc * S + (n + 1) * 512],
                             start=(fc == 0), stop=(fc == FC - 1),
                             waits=(wev_all + xt_all + after(ve) if fc == 0 and n == 0 else []),
                             inc=(mm if fc == FC - 1 and n == 3 else None))
                p.op("vector",
                     (lambda lc_: lambda e: e.tensor_copy(
                         out=vlatT[:, lc_ * S:(lc_ + 1) * S], in_=psB[:, 0:2048]))(lc),
                     waits=after(mm), inc=ve)

            def rope_chunk(dest, oc, ps):
                p.op("vector",
                     (lambda ps_, oc_: lambda e: e.tensor_scalar(
                         out=rs1[:], in0=ps_[:, 0:2048],
                         scalar1=ct_sb[:, oc_:oc_ + 1], scalar2=None,
                         op0=ALU.mult))(ps, oc),
                     waits=after(mm))
                p.op("vector",
                     (lambda ps_: lambda e: e.stream_shuffle(
                         out=rs2[:], in_=ps_[:, 0:2048], mask=shuf_mask))(ps))
                p.op("vector",
                     (lambda oc_: lambda e: e.tensor_scalar(
                         out=rs2[:], in0=rs2[:],
                         scalar1=st_sb[:, oc_:oc_ + 1], scalar2=None,
                         op0=ALU.mult))(oc))
                p.op("vector",
                     (lambda dest_, oc_: lambda e: e.tensor_tensor(
                         out=dest_[:, oc_ * S:(oc_ + 1) * S], in0=rs1[:], in1=rs2[:],
                         op=ALU.add))(dest, oc),
                     waits=store_guard, inc=ve)

            ve0 = ve.n
            for oc in range(FC):
                ps = PS[oc % 2]
                for fc in range(FC):
                    for n in range(4):
                        p.mm(ps[:, n * 512:(n + 1) * 512],
                             wqT[:, fc * D + oc * 128: fc * D + oc * 128 + 128],
                             x0T[:, fc * S + n * 512: fc * S + (n + 1) * 512],
                             start=(fc == 0), stop=(fc == FC - 1),
                             waits=([(ve, ve0 + max(0, oc - 1))] if fc == 0 and n == 0 else []),
                             inc=(mm if fc == FC - 1 and n == 3 else None))
                rope_chunk(qT, oc, ps)

            ve0 = ve.n
            for oc in range(FC):
                ps = PS[oc % 2]
                for lc in range(2):
                    for n in range(4):
                        p.mm(ps[:, n * 512:(n + 1) * 512],
                             wkbT[:, lc * D + oc * 128: lc * D + oc * 128 + 128],
                             klatT[:, lc * S + n * 512: lc * S + (n + 1) * 512],
                             start=(lc == 0), stop=(lc == 1),
                             waits=([(ve, ve0 + max(0, oc - 1))] if lc == 0 and n == 0 else []),
                             inc=(mm if lc == 1 and n == 3 else None))
                rope_chunk(kT, oc, ps)

            x0t_free = after(mm)

            for fc in range(FC):
                p.dma("sync", qTr[fc * 128:(fc + 1) * 128, tb:tb + S],
                      qT[:, fc * S:(fc + 1) * S], waits=after(ve), inc=qs)
                p.dma("sync", kTr[fc * 128:(fc + 1) * 128, tb:tb + S],
                      kT[:, fc * S:(fc + 1) * S], waits=after(ve), inc=qs)
            store_guard = after(qs)

            ve0 = ve.n
            for tc in range(16):
                ps = PS[tc % 2]
                for lc in range(2):
                    for n in range(2):
                        p.mm(ps[:, n * 512:(n + 1) * 512],
                             vlatT[:, lc * S + tc * 128: lc * S + tc * 128 + 128],
                             wvbT[:, lc * D + n * 512: lc * D + (n + 1) * 512],
                             start=(lc == 0), stop=(lc == 1),
                             waits=([(ve, ve0 + max(0, tc - 1))] if lc == 0 and n == 0 else []),
                             inc=(mm if lc == 1 and n == 1 else None))
                buf = tc % 2
                vv = vstg[:, buf * NH * 65:(buf + 1) * NH * 65]
                p.op("vector",
                     (lambda vv_: lambda e: e.memset(
                         vv_.rearrange("p (h c) -> p h c", c=65)[:, :, 64:65], 1.0))(vv),
                     waits=after(mm) + [(vd, max(0, vd.n - 16))])
                p.op("vector",
                     (lambda vv_, ps_: lambda e: e.tensor_copy(
                         out=vv_.rearrange("p (h c) -> p h c", c=65)[:, :, 0:64],
                         in_=ps_[:, 0:1024]))(vv, ps),
                     inc=ve)
                p.dma("scalar", vext[tb + tc * 128: tb + (tc + 1) * 128, :], vv,
                      waits=after(ve), inc=vd)

        return after(qs, vd, xb)


def a1_io(nc):
    io = {}
    io["x0"] = nc.dram_tensor("x0", [T, D], F32, kind="ExternalInput").ap()
    io["wq"] = nc.dram_tensor("wq", [D, D], F32, kind="ExternalInput").ap()
    io["wk_a"] = nc.dram_tensor("wk_a", [LAT, D], F32, kind="ExternalInput").ap()
    io["wk_b"] = nc.dram_tensor("wk_b", [D, LAT], F32, kind="ExternalInput").ap()
    io["wv_a"] = nc.dram_tensor("wv_a", [LAT, D], F32, kind="ExternalInput").ap()
    io["wv_b"] = nc.dram_tensor("wv_b", [D, LAT], F32, kind="ExternalInput").ap()
    io["ct"] = nc.dram_tensor("ct", [128, FC], F32, kind="ExternalInput").ap()
    io["st"] = nc.dram_tensor("st", [128, FC], F32, kind="ExternalInput").ap()
    io["x0bf"] = nc.dram_tensor("x0bf", [T, D], BF16, kind="Internal").ap()
    return io


def build_a1():
    nc = bass.Bass("TRN2", target_bir_lowering=False)
    io = a1_io(nc)
    io["ident_bf"] = nc.dram_tensor("ident_bf", [128, 128], BF16,
                                    kind="ExternalInput").ap()
    io["qTr"] = nc.dram_tensor("qTr", [D, T], BF16, kind="ExternalOutput").ap()
    io["kTr"] = nc.dram_tensor("kTr", [D, T], BF16, kind="ExternalOutput").ap()
    io["vext"] = nc.dram_tensor("vext", [T, NH * 65], BF16, kind="ExternalOutput").ap()
    from contextlib import ExitStack
    ctx = ExitStack()
    with ctx:
        big = ctx.enter_context(nc.sbuf_tensor("big", [128, ARENA_WORDS], F32))
        psA = ctx.enter_context(nc.psum_tensor("psA", [128, 2048], F32))
        psB = ctx.enter_context(nc.psum_tensor("psB", [128, 2048], F32))
        p = Prog(nc, ctx)
        sm = SemPool(p)
        ex = phase_a1(p, Arena(big), psA, psB, sm, io)
        p.op("sync", lambda e: None, waits=ex)
        p.finalize()
    return nc


_NC_CACHE = {}


def _get_nc(name):
    if name not in _NC_CACHE:
        _NC_CACHE[name] = {"a1": build_a1, "a2": build_a2, "b": build_b,
                           "c": build_c, "d": build_d,
                           "full": build_full}[name]()
    return _NC_CACHE[name]


def build_full():
    nc = bass.Bass("TRN2", target_bir_lowering=False)
    io = a1_io(nc)
    io.update(b_io(nc))
    io.update(c_io(nc))
    io["ident_bf"] = nc.dram_tensor("ident_bf", [128, 128], BF16, kind="ExternalInput").ap()
    io["ident_f32"] = nc.dram_tensor("ident_f32", [128, 128], F32, kind="ExternalInput").ap()
    io["wo"] = nc.dram_tensor("wo", [D, D], F32, kind="ExternalInput").ap()
    io["w_gate"] = nc.dram_tensor("w_gate", [FFD, D], F32, kind="ExternalInput").ap()
    io["w_up"] = nc.dram_tensor("w_up", [FFD, D], F32, kind="ExternalInput").ap()
    io["w_down"] = nc.dram_tensor("w_down", [D, FFD], F32, kind="ExternalInput").ap()
    io["qTr"] = nc.dram_tensor("qTr", [D, T], BF16, kind="Internal").ap()
    io["kTr"] = nc.dram_tensor("kTr", [D, T], BF16, kind="Internal").ap()
    io["vext"] = nc.dram_tensor("vext", [T, NH * 65], BF16, kind="Internal").ap()
    io["x1T"] = nc.dram_tensor("x1T", [D, T], F32, kind="Internal").ap()
    io["x2T"] = nc.dram_tensor("x2T", [D, T], F32, kind="Internal").ap()
    io["x3T"] = nc.dram_tensor("x3T", [D, T], F32, kind="Internal").ap()
    io["out4"] = nc.dram_tensor("out4", [T, D], F32, kind="ExternalOutput").ap()
    from contextlib import ExitStack
    ctx = ExitStack()
    with ctx:
        big = ctx.enter_context(nc.sbuf_tensor("big", [128, ARENA_WORDS], F32))
        psA = ctx.enter_context(nc.psum_tensor("psA", [128, 2048], F32))
        psB = ctx.enter_context(nc.psum_tensor("psB", [128, 2048], F32))
        p = Prog(nc, ctx)
        sm = SemPool(p)
        A = Arena(big)
        phase_a1(p, A, psA, psB, sm, io)
        casts = phase_b_casts(p, sm, io)
        cast_sems = set(casts[0])
        barrier(p, sm, skip=cast_sems)
        phase_a2(p, A, psA, psB, sm, io)
        barrier(p, sm, skip=cast_sems)
        phase_b(p, A, psA, psB, sm, io, casts=casts)
        barrier(p, sm)
        phase_c(p, A, psA, psB, sm, io)
        barrier(p, sm)
        phase_d(p, A, psA, psB, sm, io)
        endw = [(s, s.n) for s in (sm.E + sm.H + sm.G) if s.n > 0]
        p.op("sync", lambda e: None, waits=endw)
        p.finalize()
    return nc


def kernel(**inputs):
    g = lambda k: np.ascontiguousarray(np.asarray(inputs[k], dtype=np.float32))
    x0 = g("L0_attn_x")
    ca = host_consts_a()
    cb = host_consts_b()
    cc = host_consts_c()
    idb = np.eye(128, dtype=BF)
    idf = np.eye(128, dtype=np.float32)
    cores = list(range(NCORES))
    shards = [np.ascontiguousarray(x0[2 * c:2 * c + 2].reshape(T, D)) for c in cores]

    common = {
        "wq": g("L0_attn_wq"), "wk_a": g("L0_attn_wk_a"), "wk_b": g("L0_attn_wk_b"),
        "wv_a": g("L0_attn_wv_a"), "wv_b": g("L0_attn_wv_b"), "wo": g("L0_attn_wo"),
        "Wg": g("L0_ffn_Wg"), "Ws1": g("L0_ffn_Ws1"), "Ws3": g("L0_ffn_Ws3"),
        "Ws2": g("L0_ffn_Ws2"), "We1": g("L0_ffn_We1"), "We3": g("L0_ffn_We3"),
        "We2": g("L0_ffn_We2"),
        "qkv_w": g("L1_attn_qkv_w"), "qkv_b": g("L1_attn_qkv_b"),
        "rq_w": g("L1_attn_rq_w"), "rq_b": g("L1_attn_rq_b"),
        "rkv_w": g("L1_attn_rkv_w"), "rkv_b": g("L1_attn_rkv_b"),
        "rout_w": g("L1_attn_rout_w"), "rout_b": g("L1_attn_rout_b"),
        "out_w": g("L1_attn_out_w"), "out_b": g("L1_attn_out_b"),
        "gl_w": g("L1_attn_gl_w"), "gr_w": g("L1_attn_gr_w"),
        "w_gate": g("L1_ffn_w_gate"), "w_up": g("L1_ffn_w_up"),
        "w_down": g("L1_ffn_w_down"),
        **ca, **cb, **cc, "ident_bf": idb, "ident_f32": idf,
    }
    maps = [{"x0": shards[c], **common} for c in cores]
    res = run_bass_kernel_spmd(_get_nc("full"), maps, core_ids=cores).results

    out = np.empty((16, S, D), np.float32)
    for c in cores:
        out[2 * c:2 * c + 2] = res[c]["out4"].reshape(2, S, D)
    return out



def phase_a2(p, A, psA, psB, sm, io, dbg=False):
    if True:
        x0 = io["x0"]; wo = io["wo"]
        qTr = io["qTr"]; kTr = io["kTr"]; vext = io["vext"]
        identb = io["ident_bf"]; identf = io["ident_f32"]; x1T = io["x1T"]
        if dbg:
            dattn = io["dattn"]; dacc = io["dacc"]; drows = io["drows"]
        A.reset()
        woT = A(128, FC * D, BF16)
        wnat = A(128, FC * D, BF16)
        idb = A(128, 128, BF16)
        idf = A(128, 128, F32)
        qT = A(128, FC * S, BF16)
        AR = A(128, 33024, BF16)
        PT = A(128, 16384, BF16)
        recip = A(64, 1024, F32)
        r1 = A(1, S, F32)
        r2 = A(1, S, F32)
        r3 = A(1, S, F32)
        epst = A(1, 1, F32)
        sumbf = A(1, 1024, BF16)
        attcp = A(64, 1024, F32)
        ones_c = A(128, 1, BF16)
        ones_rb = A(1, 128, BF16)
        ones_rf = A(1, 128, F32)

        kS = AR[:, 0:16384]
        vS = AR[:, 16384:33024]
        accf = AR.bitcast(F32)[:, 0:16384]
        PTf = PT.bitcast(F32)
        x0stg = wnat.bitcast(F32)[:, 0:2048]

        ld = sm.H[0]; wtr = sm.E[7]; wev = sm.E[8]; kvS = sm.H[1]
        kvA = sm.H[2]; mm = sm.E[0]; ex = sm.E[2]; ve = sm.E[1]
        nrm = sm.E[3]; rq = sm.E[4]; so = sm.H[3]; st2 = sm.E[5]
        ldg = sm.G[0]; rc = sm.E[6]

        p.dma("sync", idb[:], identb, inc=ld)
        p.dma("sync", idf[:], identf, inc=ld)
        p.op("vector", lambda e: e.memset(ones_c[:], 1.0))
        p.op("vector", lambda e: e.memset(ones_rb[:], 1.0))
        p.op("vector", lambda e: e.memset(ones_rf[:], 1.0))
        p.op("vector", lambda e: e.memset(epst[:], 1e-5), inc=ve)

        def prep_weight(wdram, R, C, dest):
            RC, CC = R // 128, C // 128
            nat_view = wnat[:, 0:RC * C].rearrange("p (r c) -> p r c", r=RC)
            src = wdram.rearrange("(r p) c -> p r c", p=128)
            p.dma("gpsimd", nat_view, src, waits=after(wev), inc=ldg)
            ldv = (ldg, ldg.n)
            idv = (ld, ld.n)
            for rc in range(RC):
                pst = [psA, psB][rc % 2][:, 0:512].bitcast(BF16)
                for cc in range(CC):
                    p.op("tensor",
                         (lambda rc_, cc_, pst_: lambda e: e.transpose(
                             pst_[:, cc_ * 128:(cc_ + 1) * 128],
                             wnat[:, rc_ * C + cc_ * 128: rc_ * C + (cc_ + 1) * 128],
                             idb[:]))(rc, cc, pst),
                         waits=[ldv, idv, (wev, wev.n)],
                         inc=(wtr if cc == CC - 1 else None))
                dv = dest[:, 0:CC * R].rearrange(
                    "p (cc r) -> p cc r", cc=CC)[:, :, rc * 128:(rc + 1) * 128]
                p.op("vector",
                     (lambda dv_, pst_, CC_: lambda e: e.tensor_copy(
                         out=dv_, in_=pst_[:, 0:CC_ * 128]))(dv, pst, CC),
                     waits=after(wtr), inc=wev)

        prep_weight(wo, D, D, woT)
        wev_all = after(wev)

        acc_read_done = []

        for b in range(BPC):
            tb = b * S
            for oc in range(FC):
                p.dma("sync", qT[:, oc * S:(oc + 1) * S],
                      qTr[oc * 128:(oc + 1) * 128, tb:tb + S],
                      waits=acc_read_done, inc=kvS)
                p.dma("sync", kS[:, oc * S:(oc + 1) * S],
                      kTr[oc * 128:(oc + 1) * 128, tb:tb + S],
                      waits=acc_read_done, inc=kvS)
            p.dma("scalar",
                  vS.rearrange("p (tc c) -> p tc c", c=1040),
                  vext[tb:tb + S, :].rearrange("(tc p) c -> p tc c", p=128),
                  waits=acc_read_done, inc=kvA)
            kv_all = after(kvS, kvA)

            avguard = 0
            for h in range(NH):
                oc, base = h // 2, (h % 2) * 64
                for hq in range(2):
                    qoff = oc * S + hq * 1024
                    ex0 = ex.n
                    avg_prev = avguard
                    mmv = {}

                    def st_mm(sc):
                        pse = psA[:, (sc % 2) * 1024:(sc % 2) * 1024 + 1024]
                        for n in range(2):
                            w = []
                            if n == 0:
                                if ex0 + sc - 1 > 0:
                                    w.append((ex, ex0 + sc - 1))
                                if sc == 0:
                                    w = w + kv_all + wev_all
                            p.mm(pse[:, n * 512:(n + 1) * 512],
                                 kS[base:base + 64, oc * S + sc * 128: oc * S + sc * 128 + 128],
                                 qT[base:base + 64, qoff + n * 512: qoff + n * 512 + 512],
                                 start=True, stop=True,
                                 waits=w, inc=(mm if n == 1 else None))
                        mmv[sc] = mm.n

                    st_mm(0)
                    for sc in range(16):
                        if sc + 1 < 16:
                            st_mm(sc + 1)
                        pse = psA[:, (sc % 2) * 1024:(sc % 2) * 1024 + 1024]
                        p.op("scalar",
                             (lambda sc_, pse_: lambda e: e.activation(
                                 out=PT[:, sc_ * 1024:(sc_ + 1) * 1024], in_=pse_,
                                 func=AF.Exp, scale=0.125))(sc, pse),
                             waits=[(mm, mmv[sc])], inc=ex)
                        for n in range(2):
                            w = []
                            if n == 0:
                                w.append((ex, ex0 + sc + 1))
                                if sc == 0 and avg_prev > 0:
                                    w.append((ve, avg_prev))
                            p.mm(psB[0:65, n * 512:(n + 1) * 512],
                                 vS[:, sc * 1040 + 65 * h: sc * 1040 + 65 * h + 65],
                                 PT[:, sc * 1024 + n * 512: sc * 1024 + n * 512 + 512],
                                 start=(sc == 0), stop=(sc == 15),
                                 waits=w,
                                 inc=(mm if sc == 15 and n == 1 else None))
                    rc_prev = rc.n
                    p.op("vector", lambda e: e.tensor_copy(out=sumbf[:], in_=psB[64:65, 0:1024]),
                         waits=after(mm), inc=ve)
                    p.op("vector", lambda e: e.tensor_copy(out=attcp[:], in_=psB[0:64, 0:1024]),
                         inc=ve)
                    avguard = ve.n
                    for n in range(2):
                        w = []
                        if n == 0:
                            w = after(ve)
                            if rc_prev > 0:
                                w.append((rc, rc_prev))
                        p.mm(psB[0:64, 1024 + n * 512:1024 + (n + 1) * 512],
                             ones_rb[0:1, 0:64], sumbf[0:1, n * 512:(n + 1) * 512],
                             start=True, stop=True,
                             waits=w, inc=(mm if n == 1 else None))
                    p.op("vector", lambda e: e.reciprocal(out=recip[:], in_=psB[0:64, 1024:2048]),
                         waits=after(mm), inc=rc)
                    p.op("vector",
                         (lambda base_, qoff_: lambda e: e.tensor_tensor(
                             out=qT[base_:base_ + 64, qoff_: qoff_ + 1024],
                             in0=attcp[:], in1=recip[:], op=ALU.mult))(base, qoff),
                         inc=nrm)

            att_done = after(nrm)
            dump_g = []
            if dbg and b == 0:
                for ocq in range(FC):
                    p.dma("gpsimd", dattn[ocq * 128:(ocq + 1) * 128, :],
                          qT[:, ocq * S:(ocq + 1) * S], waits=att_done, inc=so)
                dump_g = [(so, so.n)]
            for oc in range(FC):
                for dc in range(FC):
                    for n in range(4):
                        w = []
                        if dc == 0 and n == 0:
                            w = att_done + dump_g + ([(rq, rq.n)] if oc > 0 else [])
                        p.mm(psA[:, n * 512:(n + 1) * 512],
                             woT[:, dc * D + oc * 128: dc * D + oc * 128 + 128],
                             qT[:, dc * S + n * 512: dc * S + (n + 1) * 512],
                             start=(dc == 0), stop=(dc == FC - 1),
                             waits=w, inc=(mm if dc == FC - 1 and n == 3 else None))
                p.dma("scalar", x0stg[:].rearrange("p (tc f) -> p tc f", f=128),
                      x0[tb:tb + S, oc * 128:(oc + 1) * 128].rearrange("(tc p) f -> p tc f", p=128),
                      waits=[(wtr, wtr.n), (rq, rq.n)], inc=kvA)
                kvv = (kvA, kvA.n)
                for tc in range(16):
                    p.op("tensor",
                         (lambda tc_: lambda e: e.transpose(
                             psB[:, tc_ * 128:(tc_ + 1) * 128],
                             x0stg[:, tc_ * 128:(tc_ + 1) * 128], idf[:]))(tc),
                         waits=([kvv, (rq, rq.n)] + (att_done if oc == 0 else [])
                                if tc == 0 else []),
                         inc=(mm if tc == 15 else None))
                p.op("vector",
                     (lambda oc_: lambda e: e.tensor_copy(
                         out=accf[:, oc_ * S:(oc_ + 1) * S], in_=psB[:, 0:2048]))(oc),
                     waits=after(mm))
                p.op("vector",
                     (lambda oc_: lambda e: e.tensor_tensor(
                         out=accf[:, oc_ * S:(oc_ + 1) * S], in0=psA[:, 0:2048],
                         in1=accf[:, oc_ * S:(oc_ + 1) * S], op=ALU.add))(oc),
                     inc=rq)

            if dbg and b == 0:
                for ocq in range(FC):
                    p.dma("gpsimd", dacc[ocq * 128:(ocq + 1) * 128, :],
                          accf[:, ocq * S:(ocq + 1) * S], waits=after(rq), inc=so)
                p.op("vector", lambda e: None, waits=[(so, so.n)], inc=ve)
            r1_done = after(rq) + ([(so, so.n)] if dbg and b == 0 else [])
            st2_0 = st2.n
            for oc in range(FC):
                par = (oc % 2) * 2048
                p.op("vector",
                     (lambda oc_, par_: lambda e: e.tensor_copy(
                         out=PT[:, par_:par_ + 2048], in_=accf[:, oc_ * S:(oc_ + 1) * S]))(oc, par),
                     waits=r1_done + [(st2, st2_0 + 2 * max(0, oc - 1))], inc=ve)
                p.op("scalar",
                     (lambda par_: lambda e: e.activation(
                         out=PT[:, 4096 + par_: 4096 + par_ + 2048], in_=PT[:, par_:par_ + 2048],
                         func=AF.Square))(par),
                     waits=after(ve) + [(st2, st2_0 + 2 * max(0, oc - 1))], inc=ex)
                for n in range(4):
                    p.mm(psB[0:1, n * 512:(n + 1) * 512], ones_c[:],
                         PT[:, 4096 + par + n * 512: 4096 + par + (n + 1) * 512],
                         start=(oc == 0), stop=(oc == FC - 1),
                         waits=(after(ex) if n == 0 else []),
                         inc=(st2 if n == 3 else None))
                for n in range(4):
                    p.mm(psA[0:1, n * 512:(n + 1) * 512], ones_c[:],
                         PT[:, par + n * 512: par + (n + 1) * 512],
                         start=(oc == 0), stop=(oc == FC - 1),
                         waits=(after(ve) if n == 0 else []),
                         inc=(st2 if n == 3 else None))
            p.op("vector", lambda e: e.tensor_copy(out=r1[:], in_=psA[0:1, 0:2048]),
                 waits=[(st2, st2_0 + 2 * FC)])
            p.op("vector", lambda e: e.tensor_copy(out=r2[:], in_=psB[0:1, 0:2048]))
            p.op("vector", lambda e: e.tensor_scalar_mul(out=r3[:], in0=r1[:], scalar1=1.0 / D))
            p.op("vector", lambda e: e.tensor_tensor(out=r1[:], in0=r3[:], in1=r3[:], op=ALU.mult))
            p.op("vector", lambda e: e.tensor_scalar_mul(out=r2[:], in0=r2[:], scalar1=1.0 / D))
            p.op("vector", lambda e: e.tensor_tensor(out=r2[:], in0=r2[:], in1=r1[:], op=ALU.subtract))
            p.op("vector", lambda e: e.tensor_copy(out=PTf[0:1, 0:2048], in_=r2[:]), inc=ve)
            p.op("scalar", lambda e: e.activation(out=r1[:], in_=r2[:], func=AF.Sqrt,
                                                  bias=epst[0:1, 0:1]),
                 waits=after(ve), inc=ex)
            p.op("vector", lambda e: e.reciprocal(out=r2[:], in_=r1[:]),
                 waits=after(ex))
            p.op("vector", lambda e: e.tensor_copy(out=PTf[0:1, 0:2048], in_=r2[:]), inc=ve)
            if dbg and b == 0:
                p.dma("gpsimd", drows[0:1, :], r3[:], waits=after(ve), inc=so)
                p.dma("gpsimd", drows[1:2, :], r2[:], waits=after(ve), inc=so)
            rows_done = after(ve) + ([(so, so.n)] if dbg and b == 0 else [])
            for n in range(4):
                p.mm(psA[:, n * 512:(n + 1) * 512], ones_rf[:],
                     r3[0:1, n * 512:(n + 1) * 512], start=True, stop=True,
                     waits=(rows_done if n == 0 else []))
            for n in range(4):
                p.mm(psB[:, n * 512:(n + 1) * 512], ones_rf[:],
                     r2[0:1, n * 512:(n + 1) * 512], start=True, stop=True,
                     inc=(mm if n == 3 else None))
            for oc in range(FC):
                par = (oc % 2) * 2048
                sub = PTf[:, 4096 + par: 4096 + par + 2048]
                outv = PTf[:, par: par + 2048]
                p.op("vector",
                     (lambda oc_, sub_: lambda e: e.tensor_tensor(
                         out=sub_, in0=accf[:, oc_ * S:(oc_ + 1) * S], in1=psA[:, 0:2048],
                         op=ALU.subtract))(oc, sub),
                     waits=after(mm) + [(so, max(0, so.n - 16))])
                p.op("vector",
                     (lambda sub_, outv_: lambda e: e.tensor_tensor(
                         out=outv_, in0=sub_, in1=psB[:, 0:2048], op=ALU.mult))(sub, outv),
                     inc=rq)
                p.dma("sync", x1T[oc * 128:(oc + 1) * 128, tb:tb + S],
                      outv, waits=after(rq), inc=so)
            acc_read_done = after(rq) + after(so)

        return after(so)


def build_a2(dbg=False):
    nc = bass.Bass("TRN2", target_bir_lowering=False)
    io = {}
    io["x0"] = nc.dram_tensor("x0", [T, D], F32, kind="ExternalInput").ap()
    io["wo"] = nc.dram_tensor("wo", [D, D], F32, kind="ExternalInput").ap()
    io["qTr"] = nc.dram_tensor("qTr", [D, T], BF16, kind="ExternalInput").ap()
    io["kTr"] = nc.dram_tensor("kTr", [D, T], BF16, kind="ExternalInput").ap()
    io["vext"] = nc.dram_tensor("vext", [T, NH * 65], BF16, kind="ExternalInput").ap()
    io["ident_bf"] = nc.dram_tensor("ident_bf", [128, 128], BF16, kind="ExternalInput").ap()
    io["ident_f32"] = nc.dram_tensor("ident_f32", [128, 128], F32, kind="ExternalInput").ap()
    io["x1T"] = nc.dram_tensor("x1T", [D, T], F32, kind="ExternalOutput").ap()
    if dbg:
        io["dattn"] = nc.dram_tensor("dattn", [D, S], BF16, kind="ExternalOutput").ap()
        io["dacc"] = nc.dram_tensor("dacc", [D, S], F32, kind="ExternalOutput").ap()
        io["drows"] = nc.dram_tensor("drows", [2, S], F32, kind="ExternalOutput").ap()
    from contextlib import ExitStack
    ctx = ExitStack()
    with ctx:
        big = ctx.enter_context(nc.sbuf_tensor("big", [128, ARENA_WORDS], F32))
        psA = ctx.enter_context(nc.psum_tensor("psA", [128, 2048], F32))
        psB = ctx.enter_context(nc.psum_tensor("psB", [128, 2048], F32))
        p = Prog(nc, ctx)
        sm = SemPool(p)
        ex = phase_a2(p, Arena(big), psA, psB, sm, io, dbg=dbg)
        p.op("sync", lambda e: None, waits=ex)
        p.finalize()
    return nc



NEXP = 8
FF = 2048
FFC = FF // 128


def host_consts_b():
    sel = np.zeros((8, 8 * 128), np.float32)
    for e in range(8):
        sel[e, e * 128:(e + 1) * 128] = 1.0
    return {"sel": sel.astype(BF), "ident_f32": np.eye(128, dtype=np.float32)}


def phase_b_casts(p, sm, io):
    w1bf = io["w1bf"]; w3bf = io["w3bf"]; w2bf = io["w2bf"]
    Ws1 = io["Ws1"]; Ws3 = io["Ws3"]; Ws2 = io["Ws2"]
    We1 = io["We1"]; We3 = io["We3"]; We2 = io["We2"]
    cbs = [sm.G[9 + e] for e in range(9)]
    for e in range(9):
        if e == 0:
            s1, s3, s2 = Ws1, Ws3, Ws2
        else:
            s1, s3, s2 = We1[e - 1], We3[e - 1], We2[e - 1]
        p.dma("gpsimd", w1bf[e], s1, inc=cbs[e])
        p.dma("gpsimd", w3bf[e], s3, inc=cbs[e])
        p.dma("gpsimd", w2bf[e], s2, inc=cbs[e])
    return cbs, [s.n for s in cbs]


def phase_b(p, A, psA, psB, sm, io, dbg=False, dump_half=0, casts=None):
    if True:
        x1T = io["x1T"]; Wg = io["Wg"]; Ws1 = io["Ws1"]; Ws3 = io["Ws3"]
        Ws2 = io["Ws2"]; We1 = io["We1"]; We3 = io["We3"]; We2 = io["We2"]
        sel_in = io["sel"]; identf = io["ident_f32"]; x2T = io["x2T"]
        w1bf = io["w1bf"]; w3bf = io["w3bf"]; w2bf = io["w2bf"]
        if dbg:
            dgates = io["dgates"]; dwrows = io["dwrows"]; dwb = io["dwb"]
            dracc0 = io["dracc0"]; dracc = io["dracc"]; dh = io["dh"]
            dxq = io["dxq"]; dhb = io["dhb"]; dw1 = io["dw1"]
            dgtm = io["dgtm"]; dwtm = io["dwtm"]
        A.reset()
        xqT = A(128, FC * S, BF16)
        h = A(128, FFC * S, BF16)
        racc = A(128, FC * S, F32)
        w1c = A(128, 4 * FC * 128, BF16)
        w3c = A(128, 4 * FC * 128, BF16)
        w2c = A(128, 2 * FFC * 128, BF16)
        wb_sb = A(128, S, F32)
        wrows = A(8, S, BF16)
        sel_sb = A(8, 8 * 128, BF16)
        wg_sb = A(128, FC * NEXP, F32)
        xsc = A(128, S, F32)
        idf = A(128, 128, F32)
        ones_c = A(128, 1, BF16)
        ones_rf = A(1, 128, F32)
        epst = A(1, 1, F32)

        hf = h.bitcast(F32)
        gatesT = hf[0:8, 0:2048]
        gates_tm = hf[:, 2048:2304].rearrange("p (tc e) -> p tc e", e=8)
        w_tm = hf[:, 2304:2560].rearrange("p (tc e) -> p tc e", e=8)
        r1 = hf[0:1, 4096:6144]
        r2 = hf[0:1, 6144:8192]
        r3 = hf[0:1, 2048:4096]

        ld = sm.H[0]; wlS = sm.H[1]; wlA = sm.H[2]; mm = sm.E[0]
        sx = sm.E[2]; ve = sm.E[1]; hv = sm.E[3]; dn = sm.E[4]
        so = sm.H[3]; g2 = sm.E[5]; vp = sm.E[6]; wlV = sm.H[4]

        p.dma("sync", idf[:], identf, inc=ld)
        p.dma("sync", sel_sb[:], sel_in, inc=ld)
        p.dma("sync", wg_sb[:].rearrange("p (r c) -> p r c", r=FC),
              Wg.rearrange("(r p) c -> p r c", p=128), inc=ld)
        p.op("vector", lambda e: e.memset(ones_c[:], 1.0))
        p.op("vector", lambda e: e.memset(ones_rf[:], 1.0))
        p.op("vector", lambda e: e.memset(epst[:], 1e-5), inc=ve)

        if casts is None:
            casts = phase_b_casts(p, sm, io)
        cbs, cb_done = casts

        half_done = []
        dm_prev = None
        for th in range(2):
            tb = th * S
            for fc in range(FC):
                p.dma("scalar", xsc[:],
                      x1T[fc * 128:(fc + 1) * 128, tb:tb + S],
                      waits=half_done + ([(ve, ve.n), (mm, mm.n)] if fc > 0
                                         else [(mm, mm.n)]), inc=wlA)
                p.op("vector",
                     (lambda fc_: lambda e: e.tensor_copy(
                         out=xqT[:, fc_ * S:(fc_ + 1) * S], in_=xsc[:]))(fc),
                     waits=after(wlA), inc=ve)
                for n in range(4):
                    p.mm(psA[0:8, n * 512:(n + 1) * 512],
                         wg_sb[:, fc * NEXP:(fc + 1) * NEXP],
                         xsc[:, n * 512:(n + 1) * 512],
                         start=(fc == 0), stop=(fc == FC - 1),
                         waits=(after(wlA) + after(ld) + half_done if n == 0 and fc == 0
                                else (after(wlA) if n == 0 else [])),
                         inc=(mm if n == 3 else None))
            xq_ready = after(ve)
            p.op("scalar", lambda e: e.activation(out=gatesT, in_=psA[0:8, 0:2048],
                                                  func=AF.Sigmoid),
                 waits=after(mm) + half_done, inc=sx)
            g20 = g2.n
            for tc in range(16):
                p.op("tensor",
                     (lambda tc_: lambda e: e.transpose(
                         psB[:, (tc_ % 4) * 8:(tc_ % 4) * 8 + 8],
                         gatesT[0:8, tc_ * 128:(tc_ + 1) * 128], idf[0:8, 0:8]))(tc),
                     waits=(after(sx) if tc == 0 else []) + [(g2, g20 + max(0, 2 * tc - 6))],
                     inc=mm)
                p.op("vector",
                     (lambda tc_: lambda e: e.tensor_copy(
                         out=gates_tm[:, tc_, :], in_=psB[:, (tc_ % 4) * 8:(tc_ % 4) * 8 + 8]))(tc),
                     waits=after(mm), inc=g2)
                gv = gates_tm[:, tc, :]
                wv = w_tm[:, tc, :]
                def vlink(make, extra=()):
                    p.op("vector", make, waits=after(vp) + list(extra), inc=vp)
                vlink((lambda gv_: lambda e: e.tensor_reduce(
                    out=xsc[:, 0:1], in_=gv_, axis=AX.X, op=ALU.max))(gv),
                      extra=after(g2))
                vlink((lambda gv_: lambda e: e.tensor_scalar(
                    out=xsc[:, 8:16], in0=gv_, scalar1=xsc[:, 0:1], scalar2=-1e30,
                    op0=ALU.is_equal, op1=ALU.mult))(gv))
                vlink((lambda gv_: lambda e: e.tensor_tensor(
                    out=xsc[:, 8:16], in0=xsc[:, 8:16], in1=gv_, op=ALU.add))(gv))
                vlink(lambda e: e.tensor_reduce(
                    out=xsc[:, 1:2], in_=xsc[:, 8:16], axis=AX.X, op=ALU.max))
                vlink(lambda e: e.tensor_tensor(
                    out=xsc[:, 2:3], in0=xsc[:, 0:1], in1=xsc[:, 1:2], op=ALU.add))
                vlink(lambda e: e.tensor_scalar_mul(
                    out=xsc[:, 2:3], in0=xsc[:, 2:3], scalar1=3.0))
                vlink(lambda e: e.reciprocal(out=xsc[:, 3:4], in_=xsc[:, 2:3]))
                vlink((lambda gv_: lambda e: e.tensor_scalar(
                    out=xsc[:, 8:16], in0=gv_, scalar1=xsc[:, 1:2], scalar2=None,
                    op0=ALU.is_ge))(gv))
                p.op("vector", (lambda gv_, wv_: lambda e: e.scalar_tensor_tensor(
                    out=wv_, in0=gv_, scalar=xsc[:, 3:4], in1=xsc[:, 8:16],
                    op0=ALU.mult, op1=ALU.mult))(gv, wv),
                     waits=after(vp), inc=ve)
                p.op("tensor",
                     (lambda tc_, wv_: lambda e: e.transpose(
                         psB[:, 512 + (tc_ % 4) * 128: 512 + (tc_ % 4) * 128 + 128][0:8, :],
                         wv_, idf[:]))(tc, wv),
                     waits=after(ve), inc=mm)
                p.op("vector",
                     (lambda tc_: lambda e: e.tensor_copy(
                         out=wrows[0:8, tc_ * 128:(tc_ + 1) * 128],
                         in_=psB[:, 512 + (tc_ % 4) * 128: 512 + (tc_ % 4) * 128 + 128][0:8, :]))(tc),
                     waits=after(mm), inc=g2)
            p.op("vector", lambda e_: e_.tensor_copy(out=wb_sb[0:8, 0:2048], in_=wrows[0:8, :]),
                 inc=g2)
            gating_done = after(g2, ve)
            if dbg in (True, "dumps_only", "g4") and th == dump_half:
                p.dma("sync", dgates, gatesT, waits=gating_done, inc=so)
                p.dma("sync", dwrows, wrows[0:8, :], waits=gating_done, inc=so)
                p.dma("sync", dgtm, hf[:, 2048:2304], waits=gating_done, inc=so)
                p.dma("sync", dwtm, hf[:, 2304:2560], waits=gating_done, inc=so)

            for e in range(1 if dbg == "shared" else 9):
                routed = e > 0
                if routed:
                    for n in range(4):
                        p.mm(psB[:, n * 512:(n + 1) * 512],
                             sel_sb[:, (e - 1) * 128:e * 128],
                             wrows[0:8, n * 512:(n + 1) * 512],
                             start=True, stop=True,
                             waits=(gating_done + after(ve, dn, hv) if n == 0 else []),
                             inc=(mm if n == 3 else None))
                    p.op("vector", lambda e_: e_.tensor_copy(out=wb_sb[:], in_=psB[:, 0:2048]),
                         waits=after(mm), inc=ve)
                    if dbg in (True, "dumps_only", "wb1") and th == dump_half and e == 1:
                        p.dma("sync", dwb, wb_sb[:], waits=after(ve), inc=so)
                hv0 = hv.n
                for ffc in range(FFC):
                    if ffc % 2 == 0:
                        sb = (ffc // 2) % 2
                        p.dma("sync", w1c[:, sb * 2048:(sb + 1) * 2048].rearrange(
                                  "p (r c) -> p r c", r=FC),
                              w1bf[e].rearrange("(r p) c -> p r c", p=128)[:, :, ffc * 128:(ffc + 2) * 128],
                              waits=[(cbs[e], cb_done[e]), (hv, hv0 + max(0, ffc - 2))], inc=wlS)
                        p.dma("scalar", w3c[:, sb * 2048:(sb + 1) * 2048].rearrange(
                                  "p (r c) -> p r c", r=FC),
                              w3bf[e].rearrange("(r p) c -> p r c", p=128)[:, :, ffc * 128:(ffc + 2) * 128],
                              waits=[(cbs[e], cb_done[e]), (hv, hv0 + max(0, ffc - 2))], inc=wlA)
                    sb_ = (ffc // 2) % 2
                    wcol = lambda fc_: sb_ * 2048 + fc_ * 256 + (ffc % 2) * 128
                    wlv = [(wlS, wlS.n), (wlA, wlA.n)]
                    for fc in range(FC):
                        for n in range(4):
                            p.mm(psA[:, n * 512:(n + 1) * 512],
                                 w1c[:, wcol(fc): wcol(fc) + 128],
                                 xqT[:, fc * S + n * 512: fc * S + (n + 1) * 512],
                                 start=(fc == 0), stop=(fc == FC - 1),
                                 waits=((wlv + [(sx, sx.n)]
                                         + ([(dn, dn.n), (ve, ve.n)] + (gating_done if e == 0 else [])
                                            if ffc == 0 else []))
                                        if fc == 0 and n == 0 else []),
                                 inc=(mm if fc == FC - 1 and n == 3 else None))
                    for fc in range(FC):
                        for n in range(4):
                            p.mm(psB[:, n * 512:(n + 1) * 512],
                                 w3c[:, wcol(fc): wcol(fc) + 128],
                                 xqT[:, fc * S + n * 512: fc * S + (n + 1) * 512],
                                 start=(fc == 0), stop=(fc == FC - 1),
                                 waits=((wlv + [(hv, hv.n)]
                                         + ([(dn, dn.n), (ve, ve.n)] + (gating_done if e == 0 else [])
                                            if ffc == 0 else []))
                                        if fc == 0 and n == 0 else []),
                                 inc=(mm if fc == FC - 1 and n == 3 else None))
                    mmv = (mm, mm.n)
                    p.op("scalar",
                         (lambda ffc_: lambda e_: e_.activation(
                             out=h[:, ffc_ * S:(ffc_ + 1) * S], in_=psA[:, 0:2048],
                             func=AF.Silu))(ffc),
                         waits=[mmv, (dn, dn.n)], inc=sx)
                    p.op("vector",
                         (lambda ffc_: lambda e_: e_.tensor_tensor(
                             out=h[:, ffc_ * S:(ffc_ + 1) * S],
                             in0=h[:, ffc_ * S:(ffc_ + 1) * S],
                             in1=psB[:, 0:2048], op=ALU.mult))(ffc),
                         waits=after(sx), inc=hv)
                hv_all = after(hv)
                dm_cur = []
                for dc in range(FC):
                    buf = dc % 2
                    if dc >= 2:
                        w2guard = (mm, dm_cur[dc - 2])
                    elif dm_prev is not None:
                        w2guard = (mm, dm_prev[dc + 6])
                    else:
                        w2guard = (mm, 0)
                    p.dma("scalar", w2c[:, buf * 2048:(buf + 1) * 2048].rearrange(
                              "p (r c) -> p r c", r=FFC),
                          w2bf[e].rearrange("(r p) c -> p r c", p=128)[:, :, dc * 128:(dc + 1) * 128],
                          waits=[(cbs[e], cb_done[e]), w2guard],
                          inc=wlA)
                    wlv = (wlA, wlA.n)
                    for ffc in range(FFC):
                        for n in range(4):
                            p.mm(psA[:, n * 512:(n + 1) * 512],
                                 w2c[:, buf * 2048 + ffc * 128: buf * 2048 + (ffc + 1) * 128],
                                 h[:, ffc * S + n * 512: ffc * S + (n + 1) * 512],
                                 start=(ffc == 0), stop=(ffc == FFC - 1),
                                 waits=(hv_all + [wlv, (dn, max(0, dn.n - 1))]
                                        if ffc == 0 and n == 0 else []),
                                 inc=(mm if ffc == FFC - 1 and n == 3 else None))
                    dm_cur.append(mm.n)
                    if routed:
                        p.op("vector",
                             (lambda dc_: lambda e_: e_.tensor_tensor(
                                 out=xsc[:], in0=psA[:, 0:2048], in1=wb_sb[:],
                                 op=ALU.mult))(dc),
                             waits=after(mm), inc=dn)
                    else:
                        p.op("vector",
                             (lambda dc_: lambda e_: e_.tensor_scalar_mul(
                                 out=xsc[:], in0=psA[:, 0:2048], scalar1=1.0 / 3.0))(dc),
                             waits=after(mm), inc=dn)
                    if e == 0:
                        p.op("vector",
                             (lambda dc_: lambda e_: e_.tensor_copy(
                                 out=racc[:, dc_ * S:(dc_ + 1) * S], in_=xsc[:]))(dc),
                             waits=half_done, inc=dn)
                        if dbg in (True, "dumps_only", "r0") and th == dump_half and dc == FC - 1:
                            _dn_now = after(dn)
                            for dcq in range(FC):
                                p.dma("gpsimd", dracc0[dcq * 128:(dcq + 1) * 128, :],
                                      racc[:, dcq * S:(dcq + 1) * S], waits=_dn_now, inc=so)
                    else:
                        p.op("vector",
                             (lambda dc_: lambda e_: e_.tensor_tensor(
                                 out=racc[:, dc_ * S:(dc_ + 1) * S], in0=xsc[:],
                                 in1=racc[:, dc_ * S:(dc_ + 1) * S], op=ALU.add))(dc),
                             inc=dn)

            experts_done = after(dn, mm)
            if dbg and th == dump_half:
                p.dma("sync", dxq, xqT[:, 0:S], waits=experts_done, inc=so)
                p.dma("sync", dhb, h[:, 0:S], waits=experts_done, inc=so)
                p.dma("sync", dw1, w1c[:, 0:1024], waits=experts_done, inc=so)
                for dcq in range(FC):
                    p.dma("sync", dracc[dcq * 128:(dcq + 1) * 128, :],
                          racc[:, dcq * S:(dcq + 1) * S], waits=experts_done, inc=so)
                p.dma("sync", dh, h.bitcast(F32)[:, 0:2048], waits=experts_done, inc=so)
            dump_guard = [(so, so.n)] if dbg and dbg != "dumps_only" else []
            for dc in range(FC):
                p.dma("scalar", xsc[:], x1T[dc * 128:(dc + 1) * 128, tb:tb + S],
                      waits=experts_done + [(ve, ve.n)] + (dump_guard if dc == 0 else []), inc=wlA)
                p.op("vector",
                     (lambda dc_: lambda e_: e_.tensor_tensor(
                         out=racc[:, dc_ * S:(dc_ + 1) * S], in0=xsc[:],
                         in1=racc[:, dc_ * S:(dc_ + 1) * S], op=ALU.add))(dc),
                     waits=after(wlA), inc=ve)
            r1_done = after(ve)
            st20 = mm.n
            for oc in range(FC):
                par = (oc % 2) * 2048
                p.op("vector",
                     (lambda oc_, par_: lambda e_: e_.tensor_copy(
                         out=h[:, par_:par_ + 2048], in_=racc[:, oc_ * S:(oc_ + 1) * S]))(oc, par),
                     waits=r1_done + [(mm, st20 + 2 * max(0, oc - 1))], inc=ve)
                p.op("scalar",
                     (lambda par_: lambda e_: e_.activation(
                         out=h[:, 4096 + par_: 4096 + par_ + 2048], in_=h[:, par_:par_ + 2048],
                         func=AF.Square))(par),
                     waits=after(ve) + [(mm, st20 + 2 * max(0, oc - 1))], inc=sx)
                for n in range(4):
                    p.mm(psB[0:1, n * 512:(n + 1) * 512], ones_c[:],
                         h[:, 4096 + par + n * 512: 4096 + par + (n + 1) * 512],
                         start=(oc == 0), stop=(oc == FC - 1),
                         waits=(after(sx) if n == 0 else []),
                         inc=(mm if n == 3 else None))
                for n in range(4):
                    p.mm(psA[0:1, n * 512:(n + 1) * 512], ones_c[:],
                         h[:, par + n * 512: par + (n + 1) * 512],
                         start=(oc == 0), stop=(oc == FC - 1),
                         waits=(after(ve) if n == 0 else []),
                         inc=(mm if n == 3 else None))
            p.op("vector", lambda e_: e_.tensor_copy(out=r1, in_=psA[0:1, 0:2048]),
                 waits=after(mm))
            p.op("vector", lambda e_: e_.tensor_copy(out=r2, in_=psB[0:1, 0:2048]))
            p.op("vector", lambda e_: e_.tensor_scalar_mul(out=r3, in0=r1, scalar1=1.0 / D))
            p.op("vector", lambda e_: e_.tensor_tensor(out=r1, in0=r3, in1=r3, op=ALU.mult))
            p.op("vector", lambda e_: e_.tensor_scalar_mul(out=r2, in0=r2, scalar1=1.0 / D))
            p.op("vector", lambda e_: e_.tensor_tensor(out=r2, in0=r2, in1=r1, op=ALU.subtract))
            p.op("vector", lambda e_: e_.tensor_copy(out=hf[1:2, 6144:8192] if False else xsc[0:1, :], in_=r2), inc=ve)
            p.op("scalar", lambda e_: e_.activation(out=r1, in_=r2, func=AF.Sqrt,
                                                    bias=epst[0:1, 0:1]),
                 waits=after(ve), inc=sx)
            p.op("vector", lambda e_: e_.reciprocal(out=r2, in_=r1),
                 waits=after(sx))
            p.op("vector", lambda e_: e_.tensor_copy(out=xsc[0:1, :], in_=r2), inc=ve)
            rows_done = after(ve)
            for n in range(4):
                p.mm(psA[:, n * 512:(n + 1) * 512], ones_rf[:],
                     r3[0:1, n * 512:(n + 1) * 512], start=True, stop=True,
                     waits=(rows_done if n == 0 else []))
            for n in range(4):
                p.mm(psB[:, n * 512:(n + 1) * 512], ones_rf[:],
                     r2[0:1, n * 512:(n + 1) * 512], start=True, stop=True,
                     inc=(mm if n == 3 else None))
            for oc in range(FC):
                par = (oc % 2) * 2048
                sub = hf[:, 4096 + par: 4096 + par + 2048]
                outv = hf[:, par: par + 2048]
                p.op("vector",
                     (lambda oc_, sub_: lambda e_: e_.tensor_tensor(
                         out=sub_, in0=racc[:, oc_ * S:(oc_ + 1) * S], in1=psA[:, 0:2048],
                         op=ALU.subtract))(oc, sub),
                     waits=after(mm) + [(so, max(0, so.n - 16))])
                p.op("vector",
                     (lambda sub_, outv_: lambda e_: e_.tensor_tensor(
                         out=outv_, in0=sub_, in1=psB[:, 0:2048], op=ALU.mult))(sub, outv),
                     inc=ve)
                p.dma("sync", x2T[oc * 128:(oc + 1) * 128, tb:tb + S],
                      outv, waits=after(ve), inc=so)
            half_done = after(ve) + after(so)

        return after(so)


def b_io(nc):
    io = {}
    io["Wg"] = nc.dram_tensor("Wg", [D, NEXP], F32, kind="ExternalInput").ap()
    io["Ws1"] = nc.dram_tensor("Ws1", [D, FF], F32, kind="ExternalInput").ap()
    io["Ws3"] = nc.dram_tensor("Ws3", [D, FF], F32, kind="ExternalInput").ap()
    io["Ws2"] = nc.dram_tensor("Ws2", [FF, D], F32, kind="ExternalInput").ap()
    io["We1"] = nc.dram_tensor("We1", [NEXP, D, FF], F32, kind="ExternalInput").ap()
    io["We3"] = nc.dram_tensor("We3", [NEXP, D, FF], F32, kind="ExternalInput").ap()
    io["We2"] = nc.dram_tensor("We2", [NEXP, FF, D], F32, kind="ExternalInput").ap()
    io["sel"] = nc.dram_tensor("sel", [8, 8 * 128], BF16, kind="ExternalInput").ap()
    io["w1bf"] = nc.dram_tensor("w1bf", [9, D, FF], BF16, kind="Internal").ap()
    io["w3bf"] = nc.dram_tensor("w3bf", [9, D, FF], BF16, kind="Internal").ap()
    io["w2bf"] = nc.dram_tensor("w2bf", [9, FF, D], BF16, kind="Internal").ap()
    return io


def build_b(dbg=False, dump_half=0):
    nc = bass.Bass("TRN2", target_bir_lowering=False)
    io = b_io(nc)
    io["x1T"] = nc.dram_tensor("x1T", [D, T], F32, kind="ExternalInput").ap()
    io["ident_f32"] = nc.dram_tensor("ident_f32", [128, 128], F32, kind="ExternalInput").ap()
    io["x2T"] = nc.dram_tensor("x2T", [D, T], F32, kind="ExternalOutput").ap()
    if dbg:
        io["dgates"] = nc.dram_tensor("dgates", [8, S], F32, kind="ExternalOutput").ap()
        io["dwrows"] = nc.dram_tensor("dwrows", [8, S], BF16, kind="ExternalOutput").ap()
        io["dwb"] = nc.dram_tensor("dwb", [128, S], F32, kind="ExternalOutput").ap()
        io["dracc0"] = nc.dram_tensor("dracc0", [D, S], F32, kind="ExternalOutput").ap()
        io["dracc"] = nc.dram_tensor("dracc", [D, S], F32, kind="ExternalOutput").ap()
        io["dh"] = nc.dram_tensor("dh", [128, S], F32, kind="ExternalOutput").ap()
        io["dxq"] = nc.dram_tensor("dxq", [128, S], BF16, kind="ExternalOutput").ap()
        io["dhb"] = nc.dram_tensor("dhb", [128, S], BF16, kind="ExternalOutput").ap()
        io["dw1"] = nc.dram_tensor("dw1", [128, FC * 128], BF16, kind="ExternalOutput").ap()
        io["dgtm"] = nc.dram_tensor("dgtm", [128, 256], F32, kind="ExternalOutput").ap()
        io["dwtm"] = nc.dram_tensor("dwtm", [128, 256], F32, kind="ExternalOutput").ap()
    from contextlib import ExitStack
    ctx = ExitStack()
    with ctx:
        big = ctx.enter_context(nc.sbuf_tensor("big", [128, ARENA_WORDS], F32))
        psA = ctx.enter_context(nc.psum_tensor("psA", [128, 2048], F32))
        psB = ctx.enter_context(nc.psum_tensor("psB", [128, 2048], F32))
        p = Prog(nc, ctx)
        sm = SemPool(p)
        ex = phase_b(p, Arena(big), psA, psB, sm, io, dbg=dbg, dump_half=dump_half)
        p.op("sync", lambda e: None, waits=ex)
        p.finalize()
    return nc



def host_consts_c():
    def mk_mask(first_block):
        kp = np.arange(384)[:, None]
        qr = np.arange(256)[None, :]
        dist = qr + 128 - kp
        ok = (dist >= 1) & (dist <= 128) & (kp >= 1)
        if first_block:
            ok = ok & (kp >= 128)
        m = ok.astype(np.float32)
        out = np.zeros((128, 3 * 256), np.float32)
        for kc in range(3):
            out[:, kc * 256:(kc + 1) * 256] = m[kc * 128:(kc + 1) * 128]
        return out.astype(BF)
    t = np.arange(2048)
    l = np.arange(64)
    lat_allowed = l[None, :] < (t[:, None] // 32)
    lm = np.where(lat_allowed, 0.0, -1e9).astype(np.float32)
    lm_sb = np.zeros((128, 16 * 64), np.float32)
    for tc in range(16):
        lm_sb[:, tc * 64:(tc + 1) * 64] = lm[tc * 128:(tc + 1) * 128]
    rm = np.zeros((128, 16), np.float32)
    for tc in range(16):
        rm[:, tc] = (t[tc * 128:(tc + 1) * 128] >= 32).astype(np.float32)
    return {"maskg": mk_mask(False), "mask0": mk_mask(True),
            "latmask": lm_sb, "latrow": rm}


def phase_c(p, A, psA, psB, sm, io):
    if True:
        x2T = io["x2T"]; qkv_w = io["qkv_w"]; qkv_b = io["qkv_b"]
        rq_w = io["rq_w"]; rq_b = io["rq_b"]; rkv_w = io["rkv_w"]
        rkv_b = io["rkv_b"]; rout_w = io["rout_w"]; rout_b = io["rout_b"]
        out_w = io["out_w"]; out_b = io["out_b"]; gl_w = io["gl_w"]
        gr_w = io["gr_w"]; maskg_in = io["maskg"]; mask0_in = io["mask0"]
        latm_in = io["latmask"]; latr_in = io["latrow"]
        identb = io["ident_bf"]; identf = io["ident_f32"]; x3T = io["x3T"]
        A.reset()
        AR1 = A(128, 8192, BF16)
        AR2 = A(128, 35088, BF16)
        AR3 = A(128, 16384, BF16)
        qT = A(128, FC * S, BF16)
        rqT = A(128, 2 * S, BF16)
        rkvT = A(128, 4 * 64, BF16)
        rvTt = A(128, 2 * 64, BF16)
        remT = A(128, S, BF16)
        pooledbf = A(128, FC * 64, BF16)
        pooledf = A(128, 64, F32)
        vbias = A(128, D, F32)
        vbrow = A(1, D, BF16)
        maskg_sb = A(128, 768, BF16)
        mask0_sb = A(128, 768, BF16)
        latm_sb = A(128, 1024, F32)
        latr_sb = A(128, 16, F32)
        idb = A(128, 128, BF16)
        idf = A(128, 128, F32)
        PTl = A(128, 2 * 768, BF16)
        lsc = A(128, 80, F32)
        lattn = A(128, 64, BF16)
        sc1 = A(128, S, F32)
        xstg = A(128, S, F32)
        biasq = A(128, 32, F32)
        biaso = A(128, 16, F32)
        brow = A(1, 1024, F32)
        glw = A(128, 2, BF16)
        sumbf = A(1, 256, BF16)
        grow = A(1, S, BF16)
        recip = A(64, 256, F32)
        epst = A(1, 1, F32)
        ones_c = A(128, 1, BF16)
        ones_rb = A(1, 128, BF16)
        ones_rf = A(1, 128, F32)

        kTx = AR2[:, 0:8 * 2176]
        vX = AR2[:, 8 * 2176:8 * 2176 + 17 * 1040]
        accf = AR2.bitcast(F32)[:, 0:FC * S]
        x2qT = AR3[:, 0:FC * S]
        latT = AR3[:, 0:2 * S]
        loT = AR3[:, 2 * S:4 * S]
        lnbf = AR3
        lnf = AR3.bitcast(F32)
        r1 = lnf[0:1, 4096:6144]
        r2 = lnf[0:1, 6144:8192]
        r3 = lnf[0:1, 2048:4096]

        ldS = sm.H[0]; ldA = sm.H[1]; ldG = sm.G[0]; mm = sm.E[0]
        ve = sm.E[1]; ex = sm.E[2]; nrm = sm.E[3]; rq_ = sm.E[4]
        so = sm.H[3]; st2 = sm.E[5]; vp = sm.E[6]

        p.dma("sync", idb[:], identb, inc=ldS)
        p.dma("sync", idf[:], identf, inc=ldS)
        p.dma("sync", maskg_sb[:], maskg_in, inc=ldS)
        p.dma("sync", mask0_sb[:], mask0_in, inc=ldS)
        p.dma("sync", latm_sb[:], latm_in, inc=ldS)
        p.dma("sync", latr_sb[:], latr_in, inc=ldS)
        p.dma("gpsimd", glw[0:64, 0:1], gl_w, inc=ldG)
        p.dma("gpsimd", glw[0:64, 1:2], gr_w, inc=ldG)
        p.dma("gpsimd", glw[64:128, 0:1], gl_w, inc=ldG)
        p.dma("gpsimd", glw[64:128, 1:2], gr_w, inc=ldG)
        p.dma("gpsimd", vbrow[0:1, :], qkv_b[None, 2 * D:3 * D], inc=ldG)
        p.op("vector", lambda e: e.memset(ones_c[:], 1.0))
        p.op("vector", lambda e: e.memset(ones_rb[:], 1.0))
        p.op("vector", lambda e: e.memset(ones_rf[:], 1.0))
        p.op("vector", lambda e: e.memset(epst[:], 1e-5))
        p.op("vector", lambda e: None, inc=ve)

        def bias_cols(dram_vec, n_chunks, dest, dest_col0):
            for g0 in range(0, n_chunks, 8):
                gn = min(8, n_chunks - g0)
                p.dma("sync", brow[0:1, 0:gn * 128],
                      dram_vec[None, g0 * 128:(g0 + gn) * 128],
                      waits=[(ve, ve.n)], inc=ldS)
                ldv = (ldS, ldS.n)
                for ci in range(gn):
                    c = g0 + ci
                    slot = 1024 + (ci % 2) * 512
                    p.op("tensor",
                         (lambda ci_, slot_: lambda e: e.transpose(
                             psB[:, slot_:slot_ + 1],
                             brow[0:1, ci_ * 128:(ci_ + 1) * 128], idf[0:1, 0:1]))(ci, slot),
                         waits=[ldv, (ve, ve.n)], inc=mm)
                    p.op("vector",
                         (lambda c_, slot_: lambda e: e.tensor_copy(
                             out=dest[:, dest_col0 + c_:dest_col0 + c_ + 1],
                             in_=psB[:, slot_:slot_ + 1]))(c, slot),
                         waits=after(mm), inc=ve)
        bias_cols(qkv_b, 24, biasq, 0)
        bias_cols(rq_b, 2, biasq, 24)
        bias_cols(rkv_b, 4, biasq, 26)
        bias_cols(out_b, 8, biaso, 0)
        bias_cols(rout_b, 8, biaso, 8)
        for n in range(2):
            p.mm(psA[:, n * 512:(n + 1) * 512], ones_rb[:],
                 vbrow[0:1, n * 512:(n + 1) * 512], start=True, stop=True,
                 waits=(after(ldG) + [(ve, ve.n)] if n == 0 else []),
                 inc=(mm if n == 1 else None))
        p.op("vector", lambda e: e.tensor_copy(out=vbias[:], in_=psA[:, 0:1024]),
             waits=after(mm), inc=ve)
        consts_done = after(ve, mm)

        acc_read_done = []
        for b in range(BPC):
            tb = b * S
            p.op("vector", lambda e: e.memset(
                kTx.rearrange("p (o c) -> p o c", c=2176)[:, :, 0:128], 0.0),
                waits=acc_read_done)
            p.op("vector", lambda e: e.memset(vX[:, 0:1040], 0.0), inc=ve)
            for fc in range(FC):
                p.dma("scalar", xstg[:], x2T[fc * 128:(fc + 1) * 128, tb:tb + S],
                      waits=acc_read_done + ([(ve, ve.n)] if fc > 0 else (consts_done if b == 0 else [])),
                      inc=ldA)
                p.op("vector",
                     (lambda fc_: lambda e: e.tensor_copy(
                         out=x2qT[:, fc_ * S:(fc_ + 1) * S], in_=xstg[:]))(fc),
                     waits=after(ldA), inc=ve)
            xq_ready = after(ve)

            for part in ("q", "k"):
                for oc in range(FC):
                    buf = oc % 2
                    colbase = oc * 128 if part == "q" else D + oc * 128
                    p.dma("gpsimd", AR1[:, buf * 1024:(buf + 1) * 1024].rearrange(
                              "p (r c) -> p r c", r=FC),
                          qkv_w.rearrange("(r p) c -> p r c", p=128)[:, :, colbase:colbase + 128],
                          waits=[(mm, max(0, mm.n - 1))], inc=ldG)
                    wlv = (ldG, ldG.n)
                    for fc in range(FC):
                        for n in range(4):
                            p.mm(psA[:, n * 512:(n + 1) * 512],
                                 AR1[:, buf * 1024 + fc * 128: buf * 1024 + (fc + 1) * 128],
                                 x2qT[:, fc * S + n * 512: fc * S + (n + 1) * 512],
                                 start=(fc == 0), stop=(fc == FC - 1),
                                 waits=([wlv, (ve, ve.n)] + (xq_ready if oc == 0 and part == "q" else [])
                                        if fc == 0 and n == 0 else []),
                                 inc=(mm if fc == FC - 1 and n == 3 else None))
                    bcol = oc if part == "q" else 8 + oc
                    if part == "q":
                        outap = qT[:, oc * S:(oc + 1) * S]
                    else:
                        outap = kTx.rearrange("p (o c) -> p o c", c=2176)[:, oc, 128:2176]
                    p.op("vector",
                         (lambda outap_, bcol_: lambda e: e.tensor_scalar(
                             out=outap_, in0=psA[:, 0:2048],
                             scalar1=biasq[:, bcol_:bcol_ + 1], scalar2=None,
                             op0=ALU.add))(outap, bcol),
                         waits=after(mm), inc=ve)

            p.dma("gpsimd", AR1[:].rearrange("p (r c) -> p r c", r=FC),
                  qkv_w.rearrange("(r p) c -> p r c", p=128)[:, :, 2 * D:3 * D],
                  waits=[(mm, mm.n)], inc=ldG)
            vwv = (ldG, ldG.n)
            for tc in range(16):
                slot = (tc % 2) * 1024
                for fc in range(FC):
                    for n in range(2):
                        p.mm(psB[:, slot + n * 512: slot + (n + 1) * 512],
                             x2qT[:, fc * S + tc * 128: fc * S + tc * 128 + 128],
                             AR1[:, fc * 1024 + n * 512: fc * 1024 + (n + 1) * 512],
                             start=(fc == 0), stop=(fc == FC - 1),
                             waits=([vwv, (ve, ve.n - 1 if tc >= 2 else 0)]
                                    if fc == 0 and n == 0 else []),
                             inc=(mm if fc == FC - 1 and n == 1 else None))
                vv = vX[:, (tc + 1) * 1040:(tc + 2) * 1040]
                p.op("vector",
                     (lambda vv_: lambda e: e.memset(
                         vv_.rearrange("p (h c) -> p h c", c=65)[:, :, 64:65], 1.0))(vv),
                     waits=after(mm))
                p.op("vector",
                     (lambda vv_, slot_: lambda e: e.scalar_tensor_tensor(
                         out=vv_.rearrange("p (h c) -> p h c", c=65)[:, :, 0:64],
                         in0=psB[:, slot_:slot_ + 1024], scalar=1.0,
                         in1=vbias[:], op0=ALU.mult, op1=ALU.add))(vv, slot),
                     inc=ve)

            p.dma("gpsimd", AR1[:, 2048:4096].rearrange("p (r c) -> p r c", r=FC),
                  rq_w.rearrange("(r p) c -> p r c", p=128),
                  waits=[(mm, mm.n)], inc=ldG)
            p.dma("gpsimd", AR1[:, 4096:8192].rearrange("p (r c) -> p r c", r=FC),
                  rkv_w.rearrange("(r p) c -> p r c", p=128),
                  waits=[(mm, mm.n)], inc=ldG)
            rql = (ldG, ldG.n)
            for oc in range(2):
                for fc in range(FC):
                    for n in range(4):
                        p.mm(psA[:, n * 512:(n + 1) * 512],
                             AR1[:, 2048 + fc * 256 + oc * 128: 2048 + fc * 256 + (oc + 1) * 128],
                             x2qT[:, fc * S + n * 512: fc * S + (n + 1) * 512],
                             start=(fc == 0), stop=(fc == FC - 1),
                             waits=([(ve, ve.n), rql] if fc == 0 and n == 0 else []),
                             inc=(mm if fc == FC - 1 and n == 3 else None))
                p.op("vector",
                     (lambda oc_: lambda e: e.tensor_scalar(
                         out=rqT[:, oc_ * S:(oc_ + 1) * S], in0=psA[:, 0:2048],
                         scalar1=biasq[:, 24 + oc_:25 + oc_], scalar2=None,
                         op0=ALU.add))(oc),
                     waits=after(mm), inc=ve)

            for fc in range(FC):
                p.op("vector",
                     (lambda fc_: lambda e: e.tensor_reduce(
                         out=pooledf[:].rearrange("p (c o) -> p c o", o=1),
                         in_=x2qT[:, fc_ * S:(fc_ + 1) * S].rearrange("p (c k) -> p c k", k=32),
                         axis=AX.X, op=ALU.add))(fc))
                p.op("vector",
                     (lambda fc_: lambda e: e.tensor_scalar_mul(
                         out=pooledbf[:, fc_ * 64:(fc_ + 1) * 64], in0=pooledf[:],
                         scalar1=1.0 / 32.0))(fc),
                     inc=ve)

            for mc in range(4):
                for fc in range(FC):
                    p.mm(psB[:, 1280 + (mc % 2) * 512: 1280 + (mc % 2) * 512 + 64],
                         AR1[:, 4096 + fc * 512 + mc * 128: 4096 + fc * 512 + (mc + 1) * 128],
                         pooledbf[:, fc * 64:(fc + 1) * 64],
                         start=(fc == 0), stop=(fc == FC - 1),
                         waits=([(ve, ve.n), (st2, st2.n)] if fc == 0 else []),
                         inc=(mm if fc == FC - 1 else None))
                p.op("vector",
                     (lambda mc_: lambda e: e.tensor_scalar(
                         out=rkvT[:, mc_ * 64:(mc_ + 1) * 64],
                         in0=psB[:, 1280 + (mc_ % 2) * 512: 1280 + (mc_ % 2) * 512 + 64],
                         scalar1=biasq[:, 26 + mc_:27 + mc_], scalar2=None,
                         op0=ALU.add))(mc),
                     waits=after(mm), inc=st2)
            for lh in range(4):
                mc = 2 + lh // 2
                base = (lh % 2) * 64
                p.op("tensor",
                     (lambda mc_, base_, lh_: lambda e: e.transpose(
                         psA[0:64, 1024 + (lh_ % 2) * 512 + (lh_ // 2) * 64:
                             1024 + (lh_ % 2) * 512 + (lh_ // 2) * 64 + 64].bitcast(BF16)[:, 0:64],
                         rkvT[base_:base_ + 64, mc_ * 64:(mc_ + 1) * 64],
                         idb[base_:base_ + 64, base_:base_ + 64]))(mc, base, lh),
                     waits=(after(st2) if lh == 0 else []), inc=mm)
            for lh in range(4):
                p.op("vector",
                     (lambda lh_: lambda e: e.tensor_copy(
                         out=rvTt[(lh_ % 2) * 64:(lh_ % 2) * 64 + 64, (lh_ // 2) * 64:(lh_ // 2) * 64 + 64],
                         in_=psA[0:64, 1024 + (lh_ % 2) * 512 + (lh_ // 2) * 64:
                                 1024 + (lh_ % 2) * 512 + (lh_ // 2) * 64 + 64].bitcast(BF16)[:, 0:64]))(lh),
                     waits=after(mm), inc=ve)

            loc0 = nrm.n
            for h in range(NH):
                oc, base = h // 2, (h % 2) * 64
                for nb in range(8):
                    it = h * 8 + nb
                    par = it % 2
                    sS = par * 768
                    sV = 1024 + par * 384
                    for kc in range(3):
                        p.mm(psA[:, sS + kc * 256: sS + (kc + 1) * 256],
                             kTx[base:base + 64, oc * 2176 + nb * 256 + kc * 128: oc * 2176 + nb * 256 + kc * 128 + 128],
                             qT[base:base + 64, oc * S + nb * 256: oc * S + (nb + 1) * 256],
                             start=True, stop=True,
                             waits=([(ex, max(0, ex.n - 1)), (nrm, max(0, nrm.n - 1))]
                                    if kc == 0 else []),
                             inc=(mm if kc == 2 else None))
                    p.op("scalar",
                         (lambda sS_, par_: lambda e: e.activation(
                             out=PTl[:, par_ * 768:(par_ + 1) * 768], in_=psA[:, sS_:sS_ + 768],
                             func=AF.Exp, scale=0.125))(sS, par),
                         waits=after(mm), inc=ex)
                    msk = mask0_sb if nb == 0 else maskg_sb
                    p.op("vector",
                         (lambda par_, msk_: lambda e: e.tensor_tensor(
                             out=PTl[:, par_ * 768:(par_ + 1) * 768],
                             in0=PTl[:, par_ * 768:(par_ + 1) * 768],
                             in1=msk_[:], op=ALU.mult))(par, msk),
                         waits=after(ex), inc=ve)
                    for kc in range(3):
                        p.mm(psB[0:65, par * 512: par * 512 + 256],
                             vX[:, (2 * nb + kc) * 1040 + 65 * h: (2 * nb + kc) * 1040 + 65 * h + 65],
                             PTl[:, par * 768 + kc * 256: par * 768 + (kc + 1) * 256],
                             start=(kc == 0), stop=(kc == 2),
                             waits=(after(ve) if kc == 0 else []),
                             inc=(mm if kc == 2 else None))
                    p.op("vector",
                         (lambda par_: lambda e: e.tensor_scalar(
                             out=sumbf[0:1, 0:256], in0=psB[64:65, par_ * 512: par_ * 512 + 256],
                             scalar1=1e-30, scalar2=None, op0=ALU.max))(par),
                         waits=after(mm), inc=ve)
                    p.mm(psB[64:128, 1024 + par * 512: 1024 + par * 512 + 256],
                         ones_rb[0:1, 0:64], sumbf[0:1, 0:256], start=True, stop=True,
                         waits=after(ve), inc=mm)
                    p.op("vector",
                         (lambda par_: lambda e: e.reciprocal(
                             out=recip[:, 0:256],
                             in_=psB[64:128, 1024 + par_ * 512: 1024 + par_ * 512 + 256]))(par),
                         waits=after(mm))
                    p.op("vector",
                         (lambda par_, base_, oc_, nb_: lambda e: e.tensor_tensor(
                             out=qT[base_:base_ + 64, oc_ * S + nb_ * 256: oc_ * S + (nb_ + 1) * 256],
                             in0=psB[0:64, par_ * 512: par_ * 512 + 256],
                             in1=recip[:, 0:256], op=ALU.mult))(par, base, oc, nb),
                         inc=nrm)

            for lh in range(4):
                rqc, rqb = lh // 2, (lh % 2) * 64
                for tc in range(16):
                    slot = 1024 + (tc % 2) * 768
                    p.mm(psA[:, slot:slot + 64],
                         rqT[rqb:rqb + 64, rqc * S + tc * 128: rqc * S + tc * 128 + 128],
                         rkvT[(lh % 2) * 64:(lh % 2) * 64 + 64, (lh // 2) * 64:(lh // 2) * 64 + 64],
                         start=True, stop=True,
                         waits=([(ve, ve.n)] if tc == 0 and lh == 0 else [(vp, max(0, vp.n - 3))]),
                         inc=mm)
                    p.op("vector",
                         (lambda slot_, tc_: lambda e: e.scalar_tensor_tensor(
                             out=lsc[:, 0:64], in0=psA[:, slot_:slot_ + 64], scalar=0.125,
                             in1=latm_sb[:, tc_ * 64:(tc_ + 1) * 64],
                             op0=ALU.mult, op1=ALU.add))(slot, tc),
                         waits=after(mm), inc=vp)
                    p.op("vector", lambda e: e.tensor_reduce(
                        out=lsc[:, 64:65], in_=lsc[:, 0:64], axis=AX.X, op=ALU.max,
                        negate=True), waits=after(vp), inc=vp)
                    p.op("scalar", lambda e: e.activation(
                        out=lsc[:, 0:64], in_=lsc[:, 0:64], func=AF.Exp,
                        bias=lsc[:, 64:65], accum_out=lsc[:, 65:66]),
                        waits=after(vp), inc=ex)
                    p.op("vector", lambda e: e.reciprocal(out=lsc[:, 66:67], in_=lsc[:, 65:66]),
                         waits=after(ex), inc=vp)
                    p.op("vector",
                         (lambda tc_: lambda e: e.tensor_scalar(
                             out=lattn[:], in0=lsc[:, 0:64], scalar1=lsc[:, 66:67],
                             scalar2=latr_sb[:, tc_:tc_ + 1], op0=ALU.mult, op1=ALU.mult))(tc),
                         waits=after(vp) + [(mm, mm.n)], inc=vp)
                    p.op("tensor",
                         (lambda tc_, slot_: lambda e: e.transpose(
                             psB[0:64, 1024 + (tc_ % 2) * 512: 1024 + (tc_ % 2) * 512 + 128].bitcast(BF16)[:, 0:128],
                             lattn[:], idb[:]))(tc, slot),
                         waits=after(vp), inc=mm)
                    p.op("vector",
                         (lambda tc_, lh_: lambda e: e.tensor_copy(
                             out=latT[(lh_ % 2) * 64:(lh_ % 2) * 64 + 64,
                                      (lh_ // 2) * S + tc_ * 128:(lh_ // 2) * S + tc_ * 128 + 128],
                             in_=psB[0:64, 1024 + (tc_ % 2) * 512: 1024 + (tc_ % 2) * 512 + 128].bitcast(BF16)[:, 0:128]))(tc, lh),
                         waits=after(mm), inc=vp)
            lat_done = after(vp, mm)
            for lh in range(4):
                base = (lh % 2) * 64
                chk = lh // 2
                for n in range(4):
                    p.mm(psA[0:64, n * 512:(n + 1) * 512],
                         rvTt[base:base + 64, chk * 64:(chk + 1) * 64],
                         latT[base:base + 64, chk * S + n * 512: chk * S + (n + 1) * 512],
                         start=True, stop=True,
                         waits=((lat_done + [(nrm, nrm.n)] if lh == 0 else [(ve, ve.n)])
                                if n == 0 else []),
                         inc=(mm if n == 3 else None))
                p.op("vector",
                     (lambda lh_, base_, chk_: lambda e: e.tensor_copy(
                         out=loT[base_:base_ + 64, chk_ * S:(chk_ + 1) * S],
                         in_=psA[0:64, 0:2048]))(lh, base, chk),
                     waits=after(mm), inc=ve)

            p.dma("gpsimd", AR1[:, 0:2048].rearrange("p (r c) -> p r c", r=2),
                  rout_w.rearrange("(r p) c -> p r c", p=128),
                  waits=[(mm, mm.n)], inc=ldG)
            routl = (ldG, ldG.n)
            for oc in range(FC):
                for kc in range(2):
                    for n in range(4):
                        p.mm(psA[:, n * 512:(n + 1) * 512],
                             AR1[:, kc * D + oc * 128: kc * D + (oc + 1) * 128],
                             loT[:, kc * S + n * 512: kc * S + (n + 1) * 512],
                             start=(kc == 0), stop=(kc == 1),
                             waits=([routl] + after(ve) + [(nrm, nrm.n)] if kc == 0 and n == 0 and oc == 0
                                    else ([(ve, ve.n)] if kc == 0 and n == 0 else [])),
                             inc=(mm if kc == 1 and n == 3 else None))
                p.op("vector",
                     (lambda oc_: lambda e: e.tensor_scalar(
                         out=remT[:], in0=psA[:, 0:2048],
                         scalar1=biaso[:, 8 + oc_:9 + oc_], scalar2=None, op0=ALU.add))(oc),
                     waits=after(mm), inc=ve)
                for hh in range(2):
                    base = hh * 64
                    for n in range(4):
                        p.mm(psB[0:1, n * 512:(n + 1) * 512],
                             glw[base:base + 64, 0:1],
                             qT[base:base + 64, oc * S + n * 512: oc * S + (n + 1) * 512],
                             start=True, stop=False,
                             waits=(after(ve) + [(nrm, nrm.n)] if n == 0 else []))
                    for n in range(4):
                        p.mm(psB[0:1, n * 512:(n + 1) * 512],
                             glw[base:base + 64, 1:2],
                             remT[base:base + 64, n * 512:(n + 1) * 512],
                             start=False, stop=True,
                             inc=(mm if n == 3 else None))
                    p.op("scalar", lambda e: e.activation(
                        out=grow[0:1, :], in_=psB[0:1, 0:2048], func=AF.Sigmoid),
                        waits=after(mm), inc=ex)
                    for n in range(4):
                        p.mm(psB[64:128, n * 512:(n + 1) * 512],
                             ones_rb[0:1, 0:64], grow[0:1, n * 512:(n + 1) * 512],
                             start=True, stop=True,
                             waits=(after(ex) if n == 0 else []),
                             inc=(mm if n == 3 else None))
                    p.op("vector",
                         (lambda base_, oc_: lambda e: e.tensor_tensor(
                             out=sc1[base_:base_ + 64, :],
                             in0=qT[base_:base_ + 64, oc_ * S:(oc_ + 1) * S],
                             in1=remT[base_:base_ + 64, :], op=ALU.subtract))(base, oc),
                         waits=after(mm))
                    p.op("vector",
                         (lambda base_: lambda e: e.tensor_tensor(
                             out=sc1[base_:base_ + 64, :], in0=sc1[base_:base_ + 64, :],
                             in1=psB[64:128, 0:2048], op=ALU.mult))(base))
                    p.op("vector",
                         (lambda base_, oc_: lambda e: e.tensor_tensor(
                             out=qT[base_:base_ + 64, oc_ * S:(oc_ + 1) * S],
                             in0=sc1[base_:base_ + 64, :], in1=remT[base_:base_ + 64, :],
                             op=ALU.add))(base, oc),
                         inc=nrm)

            att_done = after(nrm)
            for oc in range(FC):
                buf = oc % 2
                p.dma("gpsimd", AR1[:, buf * 1024:(buf + 1) * 1024].rearrange(
                          "p (r c) -> p r c", r=FC),
                      out_w.rearrange("(r p) c -> p r c", p=128)[:, :, oc * 128:(oc + 1) * 128],
                      waits=[(mm, max(0, mm.n - 1))] + (att_done if oc <= 1 else []), inc=ldG)
                owv = (ldG, ldG.n)
                for dc in range(FC):
                    for n in range(4):
                        w = []
                        if dc == 0 and n == 0:
                            w = att_done + [owv] + ([(rq_, rq_.n)] if oc > 0 else [])
                        p.mm(psA[:, n * 512:(n + 1) * 512],
                             AR1[:, buf * 1024 + dc * 128: buf * 1024 + (dc + 1) * 128],
                             qT[:, dc * S + n * 512: dc * S + (n + 1) * 512],
                             start=(dc == 0), stop=(dc == FC - 1),
                             waits=w, inc=(mm if dc == FC - 1 and n == 3 else None))
                p.dma("scalar", xstg[:], x2T[oc * 128:(oc + 1) * 128, tb:tb + S],
                      waits=[(rq_, rq_.n)] + (att_done if oc == 0 else []), inc=ldA)
                p.op("vector",
                     (lambda oc_: lambda e: e.tensor_scalar(
                         out=accf[:, oc_ * S:(oc_ + 1) * S], in0=psA[:, 0:2048],
                         scalar1=biaso[:, oc_:oc_ + 1], scalar2=None, op0=ALU.add))(oc),
                     waits=after(mm, ldA))
                p.op("vector",
                     (lambda oc_: lambda e: e.tensor_tensor(
                         out=accf[:, oc_ * S:(oc_ + 1) * S], in0=xstg[:],
                         in1=accf[:, oc_ * S:(oc_ + 1) * S], op=ALU.add))(oc),
                     inc=rq_)
            r1_done = after(rq_)
            st2_0 = st2.n
            for oc in range(FC):
                par = (oc % 2) * 2048
                p.op("vector",
                     (lambda oc_, par_: lambda e: e.tensor_copy(
                         out=lnbf[:, par_:par_ + 2048], in_=accf[:, oc_ * S:(oc_ + 1) * S]))(oc, par),
                     waits=r1_done + [(st2, st2_0 + 2 * max(0, oc - 1))], inc=ve)
                p.op("scalar",
                     (lambda par_: lambda e: e.activation(
                         out=lnbf[:, 4096 + par_: 4096 + par_ + 2048], in_=lnbf[:, par_:par_ + 2048],
                         func=AF.Square))(par),
                     waits=after(ve) + [(st2, st2_0 + 2 * max(0, oc - 1))], inc=ex)
                for n in range(4):
                    p.mm(psB[0:1, n * 512:(n + 1) * 512], ones_c[:],
                         lnbf[:, 4096 + par + n * 512: 4096 + par + (n + 1) * 512],
                         start=(oc == 0), stop=(oc == FC - 1),
                         waits=(after(ex) if n == 0 else []),
                         inc=(st2 if n == 3 else None))
                for n in range(4):
                    p.mm(psA[0:1, n * 512:(n + 1) * 512], ones_c[:],
                         lnbf[:, par + n * 512: par + (n + 1) * 512],
                         start=(oc == 0), stop=(oc == FC - 1),
                         waits=(after(ve) if n == 0 else []),
                         inc=(st2 if n == 3 else None))
            p.op("vector", lambda e: e.tensor_copy(out=r1, in_=psA[0:1, 0:2048]),
                 waits=[(st2, st2_0 + 2 * FC)])
            p.op("vector", lambda e: e.tensor_copy(out=r2, in_=psB[0:1, 0:2048]))
            p.op("vector", lambda e: e.tensor_scalar_mul(out=r3, in0=r1, scalar1=1.0 / D))
            p.op("vector", lambda e: e.tensor_tensor(out=r1, in0=r3, in1=r3, op=ALU.mult))
            p.op("vector", lambda e: e.tensor_scalar_mul(out=r2, in0=r2, scalar1=1.0 / D))
            p.op("vector", lambda e: e.tensor_tensor(out=r2, in0=r2, in1=r1, op=ALU.subtract))
            p.op("vector", lambda e: e.tensor_copy(out=lnf[0:1, 0:2048], in_=r2), inc=ve)
            p.op("scalar", lambda e: e.activation(out=r1, in_=r2, func=AF.Sqrt,
                                                  bias=epst[0:1, 0:1]),
                 waits=after(ve), inc=ex)
            p.op("vector", lambda e: e.reciprocal(out=r2, in_=r1),
                 waits=after(ex))
            p.op("vector", lambda e: e.tensor_copy(out=lnf[0:1, 0:2048], in_=r2), inc=ve)
            rows_done = after(ve)
            for n in range(4):
                p.mm(psA[:, n * 512:(n + 1) * 512], ones_rf[:],
                     r3[0:1, n * 512:(n + 1) * 512], start=True, stop=True,
                     waits=(rows_done if n == 0 else []))
            for n in range(4):
                p.mm(psB[:, n * 512:(n + 1) * 512], ones_rf[:],
                     r2[0:1, n * 512:(n + 1) * 512], start=True, stop=True,
                     inc=(mm if n == 3 else None))
            for oc in range(FC):
                par = (oc % 2) * 2048
                sub = lnf[:, 4096 + par: 4096 + par + 2048]
                outv = lnf[:, par: par + 2048]
                p.op("vector",
                     (lambda oc_, sub_: lambda e: e.tensor_tensor(
                         out=sub_, in0=accf[:, oc_ * S:(oc_ + 1) * S], in1=psA[:, 0:2048],
                         op=ALU.subtract))(oc, sub),
                     waits=after(mm) + [(so, max(0, so.n - 16))])
                p.op("vector",
                     (lambda sub_, outv_: lambda e: e.tensor_tensor(
                         out=outv_, in0=sub_, in1=psB[:, 0:2048], op=ALU.mult))(sub, outv),
                     inc=rq_)
                p.dma("sync", x3T[oc * 128:(oc + 1) * 128, tb:tb + S],
                      outv, waits=after(rq_), inc=so)
            acc_read_done = after(rq_) + after(so)

        return after(so)


def c_io(nc):
    io = {}
    io["qkv_w"] = nc.dram_tensor("qkv_w", [D, 3 * D], F32, kind="ExternalInput").ap()
    io["qkv_b"] = nc.dram_tensor("qkv_b", [3 * D], F32, kind="ExternalInput").ap()
    io["rq_w"] = nc.dram_tensor("rq_w", [D, 256], F32, kind="ExternalInput").ap()
    io["rq_b"] = nc.dram_tensor("rq_b", [256], F32, kind="ExternalInput").ap()
    io["rkv_w"] = nc.dram_tensor("rkv_w", [D, 512], F32, kind="ExternalInput").ap()
    io["rkv_b"] = nc.dram_tensor("rkv_b", [512], F32, kind="ExternalInput").ap()
    io["rout_w"] = nc.dram_tensor("rout_w", [256, D], F32, kind="ExternalInput").ap()
    io["rout_b"] = nc.dram_tensor("rout_b", [D], F32, kind="ExternalInput").ap()
    io["out_w"] = nc.dram_tensor("out_w", [D, D], F32, kind="ExternalInput").ap()
    io["out_b"] = nc.dram_tensor("out_b", [D], F32, kind="ExternalInput").ap()
    io["gl_w"] = nc.dram_tensor("gl_w", [HD, 1], F32, kind="ExternalInput").ap()
    io["gr_w"] = nc.dram_tensor("gr_w", [HD, 1], F32, kind="ExternalInput").ap()
    io["maskg"] = nc.dram_tensor("maskg", [128, 768], BF16, kind="ExternalInput").ap()
    io["mask0"] = nc.dram_tensor("mask0", [128, 768], BF16, kind="ExternalInput").ap()
    io["latmask"] = nc.dram_tensor("latmask", [128, 1024], F32, kind="ExternalInput").ap()
    io["latrow"] = nc.dram_tensor("latrow", [128, 16], F32, kind="ExternalInput").ap()
    return io


def build_c():
    nc = bass.Bass("TRN2", target_bir_lowering=False)
    io = c_io(nc)
    io["x2T"] = nc.dram_tensor("x2T", [D, T], F32, kind="ExternalInput").ap()
    io["ident_bf"] = nc.dram_tensor("ident_bf", [128, 128], BF16, kind="ExternalInput").ap()
    io["ident_f32"] = nc.dram_tensor("ident_f32", [128, 128], F32, kind="ExternalInput").ap()
    io["x3T"] = nc.dram_tensor("x3T", [D, T], F32, kind="ExternalOutput").ap()
    from contextlib import ExitStack
    ctx = ExitStack()
    with ctx:
        big = ctx.enter_context(nc.sbuf_tensor("big", [128, ARENA_WORDS], F32))
        psA = ctx.enter_context(nc.psum_tensor("psA", [128, 2048], F32))
        psB = ctx.enter_context(nc.psum_tensor("psB", [128, 2048], F32))
        p = Prog(nc, ctx)
        sm = SemPool(p)
        ex = phase_c(p, Arena(big), psA, psB, sm, io)
        p.op("sync", lambda e: None, waits=ex)
        p.finalize()
    return nc



FFD = 2816
FFDC = FFD // 128


def phase_d(p, A, psA, psB, sm, io):
    if True:
        x3T = io["x3T"]; w_gate = io["w_gate"]; w_up = io["w_up"]
        w_down = io["w_down"]; identb = io["ident_bf"]; identf = io["ident_f32"]
        OUT = io["out4"]
        A.reset()
        wgT = A(128, FC * FFD, BF16)
        wuT = A(128, FC * FFD, BF16)
        wdT = A(128, FFDC * D, BF16)
        wstg = A(128, FFD, F32)
        ybuf = A(128, FFD, F32)
        cbuf = A(128, FFD, F32)
        scl = A(128, FFD // 16, F32)
        rcp = A(128, FFD // 16, F32)
        sgn = A(128, FFD, BF16)
        qbf = A(128, FFD, BF16)
        hbuf = A(128, FFD, F32)
        hqT = A(128, FFD, BF16)
        x3t = A(128, D, F32)
        xqT = A(128, FC * 128, BF16)
        accb = A(128, D, F32)
        col1 = A(128, 4, F32)
        idb = A(128, 128, BF16)
        idf = A(128, 128, F32)
        epst = A(1, 1, F32)
        epsc = A(128, 1, F32)

        ldS = sm.H[0]; xl = sm.H[2]; mm = sm.E[0]; ve = sm.E[1]
        ex = sm.E[2]; so = sm.H[3]; qd = sm.E[3]

        p.dma("sync", idb[:], identb, inc=ldS)
        p.dma("sync", idf[:], identf, inc=ldS)
        p.op("vector", lambda e: e.memset(epst[:], 1e-5))
        p.op("vector", lambda e: e.memset(epsc[:], 1e-5))

        def quant(src_ap, W, out_ap, waits0, absdone=None):
            nb = W // 16
            bv = src_ap.rearrange("p (n k) -> p n k", k=16)
            p.op("vector", lambda e: e.tensor_reduce(
                out=scl[:, 0:nb].rearrange("p (n o) -> p n o", o=1), in_=bv,
                axis=AX.X, op=ALU.max, apply_absolute_value=True), waits=list(waits0))
            p.op("vector", lambda e: e.tensor_scalar(
                out=scl[:, 0:nb], in0=scl[:, 0:nb], scalar1=1.0 / 6.0, scalar2=1e-12,
                op0=ALU.mult, op1=ALU.max))
            p.op("vector", lambda e: e.reciprocal(out=rcp[:, 0:nb], in_=scl[:, 0:nb]))
            if absdone is None:
                p.op("scalar", (lambda s_: lambda e: e.activation(
                    out=ybuf[:, 0:W], in_=s_, func=AF.Abs))(src_ap),
                    waits=list(waits0), inc=ex)
                p.op("scalar", (lambda s_: lambda e: e.sign(out=sgn[:, 0:W], in_=s_))(src_ap),
                     waits=list(waits0), inc=ex)
                absdone = after(ex)
            rb = rcp[:, 0:nb].rearrange("p (n o) -> p n o", o=1).broadcast_to([128, nb, 16])
            p.op("vector", lambda e: e.tensor_tensor(
                out=ybuf[:, 0:W].rearrange("p (n k) -> p n k", k=16),
                in0=ybuf[:, 0:W].rearrange("p (n k) -> p n k", k=16), in1=rb,
                op=ALU.mult), waits=list(absdone))
            p.op("vector", lambda e: e.tensor_scalar(
                out=cbuf[:, 0:W], in0=ybuf[:, 0:W], scalar1=2.5, scalar2=None,
                op0=ALU.is_gt))
            for u in (3.5, 5.0, 5.0):
                p.op("vector", (lambda u_: lambda e: e.scalar_tensor_tensor(
                    out=cbuf[:, 0:W], in0=ybuf[:, 0:W], scalar=u_, in1=cbuf[:, 0:W],
                    op0=ALU.is_gt, op1=ALU.add))(u))
            p.op("vector", lambda e: e.tensor_scalar(
                out=cbuf[:, 0:W], in0=cbuf[:, 0:W], scalar1=2.0, scalar2=None,
                op0=ALU.mult))
            for u in (0.25, 0.75, 1.25, 1.75):
                p.op("vector", (lambda u_: lambda e: e.scalar_tensor_tensor(
                    out=cbuf[:, 0:W], in0=ybuf[:, 0:W], scalar=u_, in1=cbuf[:, 0:W],
                    op0=ALU.is_gt, op1=ALU.add))(u))
            sb = scl[:, 0:nb].rearrange("p (n o) -> p n o", o=1).broadcast_to([128, nb, 16])
            p.op("vector", lambda e: e.scalar_tensor_tensor(
                out=cbuf[:, 0:W].rearrange("p (n k) -> p n k", k=16),
                in0=cbuf[:, 0:W].rearrange("p (n k) -> p n k", k=16), scalar=0.5, in1=sb,
                op0=ALU.mult, op1=ALU.mult))
            p.op("vector", (lambda o_: lambda e: e.tensor_tensor(
                out=o_, in0=cbuf[:, 0:W], in1=sgn[:, 0:W], op=ALU.mult))(out_ap),
                waits=list(absdone), inc=ve)

        def prep_w(wdram, R, C, dest, destW):
            RC, CC = R // 128, C // 128
            for rc in range(RC):
                p.dma("sync", wstg[:, 0:C], wdram[rc * 128:(rc + 1) * 128, :],
                      waits=[(ve, ve.n), (ex, ex.n)], inc=ldS)
                quant(wstg[:, 0:C], C, qbf[:, 0:C], [(ldS, ldS.n), (ve, ve.n)])
                for cc in range(CC):
                    p.op("tensor",
                         (lambda rc_, cc_: lambda e: e.transpose(
                             psA[:, (cc_ % 8) * 256:(cc_ % 8) * 256 + 64].bitcast(BF16)[:, 0:128],
                             qbf[:, cc_ * 128:(cc_ + 1) * 128], idb[:]))(rc, cc),
                         waits=(after(ve) if cc == 0 else []) +
                               ([(ve, ve.n)] if cc % 8 == 0 and cc > 0 else []),
                         inc=(mm if cc % 8 == 7 or cc == CC - 1 else None))
                    if cc % 8 == 7 or cc == CC - 1:
                        lo = (cc // 8) * 8
                        ln = cc - lo + 1
                        dv = dest[:, 0:CC * destW].rearrange(
                            "p (cc r) -> p cc r", cc=CC)[:, lo:lo + ln, rc * 128:(rc + 1) * 128]
                        p.op("vector",
                             (lambda dv_, ln_: lambda e: e.tensor_copy(
                                 out=dv_, in_=psA[:, 0:2048].bitcast(BF16).rearrange(
                                     "p (cc x) -> p cc x", x=512)[:, 0:ln_, 0:128]))(dv, ln),
                             waits=after(mm), inc=ve)

        prep_w(w_gate, FFD, D, wgT, FFD)
        prep_w(w_up, FFD, D, wuT, FFD)
        prep_w(w_down, D, FFD, wdT, D)
        wq_done = after(ve, mm)

        FFB = [(0, 1024), (1024, 1024), (2048, 768)]
        x3tb = wstg[:, 1024:2048]
        qd_x3evac_prev = 0
        ve_hq_prev = 0
        mm_xtrans_prev = 0
        pending_ln = None
        for tc in range(32):
            x3cur = x3t[:] if tc % 2 == 0 else x3tb
            p.dma("scalar", wstg[:, 0:1024].rearrange("p (r c) -> p r c", c=128),
                  x3T[:, tc * 128:(tc + 1) * 128].rearrange("(r p) c -> p r c", p=128),
                  waits=wq_done + [(mm, mm_xtrans_prev)], inc=xl)
            for fc in range(FC):
                p.op("tensor",
                     (lambda fc_: lambda e: e.transpose(
                         psB[:, 1024 + fc_ * 128: 1024 + (fc_ + 1) * 128],
                         wstg[:, fc_ * 128:(fc_ + 1) * 128], idf[:]))(fc),
                     waits=([(xl, xl.n), (ve, ve_hq_prev), (qd, qd_x3evac_prev)]
                            if fc == 0 else []),
                     inc=(mm if fc == FC - 1 else None))
            mm_xtrans_prev = mm.n
            p.op("vector", (lambda x3_: lambda e: e.tensor_copy(
                     out=x3_, in_=psB[:, 1024:2048]))(x3cur),
                 waits=after(mm), inc=qd)
            qd_x3evac_prev = qd.n
            quant(x3cur, D, qbf[:, 0:D], [(qd, qd.n)])
            for fc in range(FC):
                p.op("tensor",
                     (lambda fc_: lambda e: e.transpose(
                         psB[:, (fc_ % 8) * 256:(fc_ % 8) * 256 + 64].bitcast(BF16)[:, 0:128],
                         qbf[:, fc_ * 128:(fc_ + 1) * 128], idb[:]))(fc),
                     waits=(after(ve) if fc == 0 else []),
                     inc=(mm if fc == FC - 1 else None))
            p.op("vector", lambda e: e.tensor_copy(
                out=xqT[:].rearrange("p (cc r) -> p cc r", cc=FC),
                in_=psB[:, 0:2048].bitcast(BF16).rearrange(
                    "p (cc x) -> p cc x", x=512)[:, 0:FC, 0:128]),
                waits=after(mm), inc=qd)
            if pending_ln is not None:
                pending_ln()
                pending_ln = None
            qd_gate = qd.n
            ex_gate = ex.n
            for (f0, fl) in FFB:
                nn = (fl + 511) // 512
                for fc in range(FC):
                    for n in range(nn):
                        nl = min(512, fl - n * 512)
                        p.mm(psA[:, n * 512:n * 512 + nl],
                             xqT[:, fc * 128:(fc + 1) * 128],
                             wgT[:, fc * FFD + f0 + n * 512: fc * FFD + f0 + n * 512 + nl],
                             start=(fc == 0), stop=(fc == FC - 1),
                             waits=([(qd, qd_gate), (ex, ex_gate)] if fc == 0 and n == 0 else []),
                             inc=(mm if fc == FC - 1 and n == nn - 1 else None))
                        p.mm(psA[:, 1024 + n * 512: 1024 + n * 512 + nl],
                             xqT[:, fc * 128:(fc + 1) * 128],
                             wuT[:, fc * FFD + f0 + n * 512: fc * FFD + f0 + n * 512 + nl],
                             start=(fc == 0), stop=(fc == FC - 1),
                             inc=(mm if fc == FC - 1 and n == nn - 1 else None))
                p.op("scalar",
                     (lambda f0_, fl_: lambda e: e.activation(
                         out=hbuf[:, f0_:f0_ + fl_], in_=psA[:, 0:fl_],
                         func=AF.Silu))(f0, fl),
                     waits=after(mm), inc=ex)
                ex_gate = ex.n
                p.op("vector",
                     (lambda f0_, fl_: lambda e: e.tensor_tensor(
                         out=hbuf[:, f0_:f0_ + fl_], in0=hbuf[:, f0_:f0_ + fl_],
                         in1=psA[:, 1024:1024 + fl_], op=ALU.mult))(f0, fl),
                     waits=[(ex, ex_gate)], inc=qd)
                qd_gate = qd.n
                p.op("scalar",
                     (lambda f0_, fl_: lambda e: e.activation(
                         out=ybuf[:, f0_:f0_ + fl_], in_=hbuf[:, f0_:f0_ + fl_],
                         func=AF.Abs))(f0, fl),
                     waits=[(qd, qd_gate)], inc=ex)
                p.op("scalar",
                     (lambda f0_, fl_: lambda e: e.sign(
                         out=sgn[:, f0_:f0_ + fl_], in_=hbuf[:, f0_:f0_ + fl_]))(f0, fl),
                     inc=ex)
            quant(hbuf[:], FFD, qbf[:, 0:FFD], [(qd, qd.n)], absdone=after(ex))
            for cc in range(FFDC):
                p.op("tensor",
                     (lambda cc_: lambda e: e.transpose(
                         psB[:, (cc_ % 8) * 256:(cc_ % 8) * 256 + 64].bitcast(BF16)[:, 0:128],
                         qbf[:, cc_ * 128:(cc_ + 1) * 128], idb[:]))(cc),
                     waits=(after(ve) if cc == 0 else []) +
                           ([(ve, ve.n)] if cc == 8 or cc == 16 else []),
                     inc=(mm if cc in (7, 15, FFDC - 1) else None))
                if cc == 7 or cc == 15 or cc == FFDC - 1:
                    lo = (cc // 8) * 8
                    ln = cc - lo + 1
                    p.op("vector",
                         (lambda lo_, ln_: lambda e: e.tensor_copy(
                             out=hqT[:].rearrange("p (cc r) -> p cc r", cc=FFDC)[:, lo_:lo_ + ln_, :],
                             in_=psB[:, 0:2048].bitcast(BF16).rearrange(
                                 "p (cc x) -> p cc x", x=512)[:, 0:ln_, 0:128]))(lo, ln),
                         waits=after(mm), inc=ve)
            hq_done = after(ve)
            ve_hq_prev = ve.n
            for cc in range(FFDC):
                for n in range(2):
                    p.mm(psA[:, n * 512:(n + 1) * 512],
                         hqT[:, cc * 128:(cc + 1) * 128],
                         wdT[:, cc * D + n * 512: cc * D + (n + 1) * 512],
                         start=(cc == 0), stop=(cc == FFDC - 1),
                         waits=(hq_done + [(qd, qd.n)] if cc == 0 and n == 0 else []),
                         inc=(mm if cc == FFDC - 1 and n == 1 else None))
            mm_down = mm.n

            def make_ln(tc_, x3_, mm_down_):
                def emit():
                    p.op("vector", lambda e: e.tensor_tensor(
                        out=accb[:], in0=psA[:, 0:1024], in1=x3_, op=ALU.add),
                        waits=[(mm, mm_down_), (so, so.n)], inc=qd)
                    p.op("vector", lambda e: e.tensor_reduce(
                        out=col1[:, 0:1], in_=accb[:], axis=AX.X, op=ALU.add))
                    p.op("vector", lambda e: e.tensor_scalar_mul(
                        out=col1[:, 1:2], in0=col1[:, 0:1], scalar1=1.0 / D))
                    p.op("vector", lambda e: e.tensor_scalar(
                        out=accb[:], in0=accb[:], scalar1=col1[:, 1:2], scalar2=None,
                        op0=ALU.subtract), inc=ve)
                    p.op("scalar", lambda e: e.activation(
                        out=hbuf[:, 0:D], in_=accb[:], func=AF.Square, accum_out=col1[:, 2:3]),
                        waits=after(ve), inc=ex)
                    p.op("scalar", lambda e: e.activation(
                        out=col1[:, 3:4], in_=col1[:, 2:3], func=AF.Sqrt, scale=1.0 / D,
                        bias=epsc[:, 0:1]), inc=ex)
                    p.op("vector", lambda e: e.reciprocal(out=col1[:, 2:3], in_=col1[:, 3:4]),
                         waits=after(ex))
                    p.op("vector", lambda e: e.tensor_scalar(
                             out=accb[:], in0=accb[:], scalar1=col1[:, 2:3],
                             scalar2=None, op0=ALU.mult), inc=ve)
                    p.dma("sync", OUT[tc_ * 128:(tc_ + 1) * 128, :], accb[:],
                          waits=after(ve), inc=so)
                return emit

            pending_ln = make_ln(tc, x3cur, mm_down)

        pending_ln()
        return after(so)


def build_d():
    nc = bass.Bass("TRN2", target_bir_lowering=False)
    io = {}
    io["x3T"] = nc.dram_tensor("x3T", [D, T], F32, kind="ExternalInput").ap()
    io["w_gate"] = nc.dram_tensor("w_gate", [FFD, D], F32, kind="ExternalInput").ap()
    io["w_up"] = nc.dram_tensor("w_up", [FFD, D], F32, kind="ExternalInput").ap()
    io["w_down"] = nc.dram_tensor("w_down", [D, FFD], F32, kind="ExternalInput").ap()
    io["ident_bf"] = nc.dram_tensor("ident_bf", [128, 128], BF16, kind="ExternalInput").ap()
    io["ident_f32"] = nc.dram_tensor("ident_f32", [128, 128], F32, kind="ExternalInput").ap()
    io["out4"] = nc.dram_tensor("out4", [T, D], F32, kind="ExternalOutput").ap()
    from contextlib import ExitStack
    ctx = ExitStack()
    with ctx:
        big = ctx.enter_context(nc.sbuf_tensor("big", [128, ARENA_WORDS], F32))
        psA = ctx.enter_context(nc.psum_tensor("psA", [128, 2048], F32))
        psB = ctx.enter_context(nc.psum_tensor("psB", [128, 2048], F32))
        p = Prog(nc, ctx)
        sm = SemPool(p)
        ex = phase_d(p, Arena(big), psA, psB, sm, io)
        p.op("sync", lambda e: None, waits=ex)
        p.finalize()
    return nc
